# version 03
# baseline (speedup 1.0000x reference)
import numpy as np
import ml_dtypes
from contextlib import ExitStack

import concourse.bass as bass
import concourse.mybir as mybir
from concourse.bass_utils import run_bass_kernel_spmd

F32 = mybir.dt.float32
BF16 = mybir.dt.bfloat16
AF = mybir.ActivationFunctionType
ALU = mybir.AluOpType
AX = mybir.AxisListType
f32 = np.float32
bf16 = ml_dtypes.bfloat16

B, T, C = 2, 8192, 1024
HS = 64
E, FF = 8, 4096
NCORES = 8
TOK = 2048
NCH = 16
KCH = 64
NT = 1024
EPS = 1e-5

C_INV1024, C_NEG132, C_ONE, C_NEG1E30, C_ZERO = 0, 1, 2, 3, 4

ENGINES = ("sync", "scalar", "vector", "tensor", "gpsimd")


class Sem:
    def __init__(self, nc, es, name):
        self.h = es.enter_context(nc.semaphore(name))
        self.n = 0

    def bump(self, k):
        self.n += k
        return self.n


class Prog:

    def __init__(self):
        self.streams = {e: [] for e in ENGINES}

    def op(self, eng, fn, inc=None, inc_n=1):
        if inc is not None:
            inc.bump(inc_n)
            self.streams[eng].append(
                lambda Eo, fn=fn, s=inc.h, n=inc_n: fn(Eo).then_inc(s, n))
        else:
            self.streams[eng].append(lambda Eo, fn=fn: fn(Eo))

    def eop(self, eng, fn, sem):
        self.op(eng, fn, inc=sem)
        self.wait(eng, sem)

    def wait(self, eng, sem, val=None):
        v = sem.n if val is None else val
        if v > 0:
            self.streams[eng].append(lambda Eo, s=sem.h, v=v: Eo.wait_ge(s, v))

    def run(self, block):
        for eng in ENGINES:
            fns = tuple(self.streams[eng])

            def body(Eo, fns=fns):
                for f in fns:
                    f(Eo)

            getattr(block, eng)(body)


class Rot:

    def __init__(self, n):
        self.n = n
        self.rel = [[] for _ in range(n)]

    def acquire(self, P, eng, i):
        for sem, cnt in self.rel[i % self.n]:
            P.wait(eng, sem, cnt)
        self.rel[i % self.n] = []

    def release(self, i, sem):
        self.rel[i % self.n].append((sem, sem.n))


class DmaSlots:

    def __init__(self, nc, es, name, n=2):
        self.sems = [Sem(nc, es, "%s%d" % (name, i)) for i in range(n)]
        self.n = n

    def load(self, P, eng, i, fn):
        P.op(eng, fn, inc=self.sems[i % self.n], inc_n=16)

    def wait(self, P, eng, i):
        P.wait(eng, self.sems[i % self.n])

    def sem_count(self, i):
        s = self.sems[i % self.n]
        return (s, s.n)


def build_nc(debug_taps=()):
    nc = bass.Bass("TRN2", target_bir_lowering=False)

    d_xown = nc.dram_tensor("x_own", [TOK, C], F32, kind="ExternalInput").ap()
    d_xbat = nc.dram_tensor("x_batch", [T, C], F32, kind="ExternalInput").ap()
    d_wqkv = nc.dram_tensor("wqkvT", [128, 8, 192], F32, kind="ExternalInput").ap()
    d_qkb = nc.dram_tensor("qkb", [128, 2], F32, kind="ExternalInput").ap()
    d_bvr = nc.dram_tensor("bvrow", [1, HS], F32, kind="ExternalInput").ap()
    d_b1t = nc.dram_tensor("b1t", [128, E, 32], F32, kind="ExternalInput").ap()
    d_b2w = nc.dram_tensor("b2w", [E, C], F32, kind="ExternalInput").ap()
    d_rw = nc.dram_tensor("rw", [128, 8, E], F32, kind="ExternalInput").ap()
    d_w1 = nc.dram_tensor("w1", [E, C, FF], F32, kind="ExternalInput").ap()
    d_w2 = nc.dram_tensor("w2", [E, FF, C], F32, kind="ExternalInput").ap()
    d_idb = nc.dram_tensor("identb", [128, 128], BF16, kind="ExternalInput").ap()
    d_idf = nc.dram_tensor("identf", [128, 128], F32, kind="ExternalInput").ap()
    d_qpi = nc.dram_tensor("qpiota", [128, 512], F32, kind="ExternalInput").ap()
    d_ones = nc.dram_tensor("onesb", [1, 128], BF16, kind="ExternalInput").ap()
    d_qoff = nc.dram_tensor("qoffb", [128, 1], F32, kind="ExternalInput").ap()
    d_cst = nc.dram_tensor("consts", [128, 8], F32, kind="ExternalInput").ap()
    d_sel = nc.dram_tensor("selb", [E, E * 128], BF16, kind="ExternalInput").ap()
    d_y1sp = nc.dram_tensor("y1sp", [TOK, C], F32, kind="Internal").ap()
    d_y2 = nc.dram_tensor("y2", [TOK, C], F32, kind="ExternalOutput").ap()
    taps = {}
    for name, shape in debug_taps:
        taps[name] = nc.dram_tensor("tap_" + name, shape, F32, kind="ExternalOutput").ap()

    esP = ExitStack()
    sb = lambda name, shape, dt: esP.enter_context(nc.sbuf_tensor(name, shape, dt))

    idb = sb("idb", [128, 128], BF16)
    idf = sb("idf", [128, 128], F32)
    onesb = sb("onesb_s", [1, 128], BF16)
    qoffb = sb("qoffb_s", [128, 1], F32)
    cst = sb("cst", [128, 8], F32)
    b1t = sb("b1t_s", [128, E * 32], F32)
    b2b = sb("b2b", [E, C], BF16)
    rw = sb("rw_s", [128, 8 * E], F32)
    y1T = sb("y1T", [128, 8 * TOK], BF16)
    gatesT = sb("gatesT", [E, TOK], BF16)
    selb = sb("selb_s", [E, E * 128], BF16)

    esA = ExitStack()
    sbA = lambda name, shape, dt: esA.enter_context(nc.sbuf_tensor(name, shape, dt))
    qpi = sbA("qpi", [128, 512], F32)
    maskadd = sbA("maskadd", [128, 512], F32)
    qT = sbA("qT", [64, TOK], BF16)
    kT = sbA("kT", [64, T], BF16)
    vtm = sbA("vtm", [128, KCH * HS], BF16)
    o_tm = sbA("o_tm", [128, NCH * HS], F32)
    mA = sbA("mA", [128, 1], F32)
    mB = sbA("mB", [128, 1], F32)
    dm = sbA("dm", [128, 1], F32)
    edm = sbA("edm", [128, 1], F32)
    denA = sbA("denA", [128, 1], F32)
    denB = sbA("denB", [128, 1], F32)
    negm32 = sbA("negm32", [128, 1], F32)
    lden = sbA("lden", [128, 1], F32)
    bias2 = sbA("bias2", [128, 1], F32)
    bsum = [sbA("bsum0", [128, 1], F32), sbA("bsum1", [128, 1], F32)]

    esS = ExitStack()
    mks = lambda name: Sem(nc, esS, name)
    sC1 = mks("sC1")
    sC2 = mks("sC2")
    sTAP = mks("sTAP")
    sPE = mks("sPE")
    sAC = mks("sAC")
    sDV = mks("sDV")
    sBAR = mks("sBAR")
    dXB = DmaSlots(nc, esS, "dXB")
    dZC = DmaSlots(nc, esS, "dZC")
    dSPp = DmaSlots(nc, esS, "dSPp")
    dW1 = DmaSlots(nc, esS, "dW1")
    dW2 = DmaSlots(nc, esS, "dW2")
    dYC = DmaSlots(nc, esS, "dYC")
    dSTp = DmaSlots(nc, esS, "dSTp")

    c_inv1024 = cst[:, C_INV1024:C_INV1024 + 1]
    c_neg132 = cst[:, C_NEG132:C_NEG132 + 1]
    c_one = cst[:, C_ONE:C_ONE + 1]
    c_neg1e30 = cst[:, C_NEG1E30:C_NEG1E30 + 1]
    c_zero = cst[:, C_ZERO:C_ZERO + 1]

    barrier_count = [0]

    def barrier(P):
        barrier_count[0] += 1
        for eng in ENGINES:
            P.op(eng, lambda Eo: Eo.nop(), inc=sBAR)
        for eng in ENGINES:
            P.wait(eng, sBAR, 5 * barrier_count[0])

    es1 = ExitStack()
    sb1 = lambda name, shape, dt: es1.enter_context(nc.sbuf_tensor(name, shape, dt))
    ps1 = lambda name, shape, dt: es1.enter_context(nc.psum_tensor(name, shape, dt))
    xb = [sb1("xb0", [128, C], BF16), sb1("xb1", [128, C], BF16)]
    xT = [sb1("xT0", [128, C], BF16), sb1("xT1", [128, C], BF16)]
    wqkv = sb1("wqkv", [128, 8 * 192], BF16)
    bvr = sb1("bvr", [1, HS], BF16)
    qkb = sb1("qkb_s", [128, 2], F32)
    pXT = [ps1("pXT0", [128, C], BF16), ps1("pXT1", [128, C], BF16)]
    pQK = [ps1("pQK0", [64, 128], F32), ps1("pQK1", [64, 128], F32)]
    pV = [ps1("pV0", [128, HS], F32), ps1("pV1", [128, HS], F32)]

    P = Prog()
    for dst, src in [(idb, d_idb), (idf, d_idf), (qpi, d_qpi), (qoffb, d_qoff),
                     (cst, d_cst), (qkb[:], d_qkb),
                     (b1t, d_b1t.rearrange("p e k -> p (e k)")),
                     (rw, d_rw.rearrange("p c e -> p (c e)"))]:
        ap = dst[:] if hasattr(dst, "ap") else dst
        P.op("sync", lambda Eo, a=ap, s=src: Eo.dma_start(out=a, in_=s), inc=sC1, inc_n=16)
    for dst, src in [(onesb[:], d_ones), (selb[:], d_sel),
                     (wqkv[:], d_wqkv.rearrange("p c h -> p (c h)")),
                     (bvr[:], d_bvr), (b2b[:], d_b2w)]:
        P.op("gpsimd", lambda Eo, a=dst, s=src: Eo.dma_start(out=a, in_=s), inc=sC2, inc_n=16)
    ld_consts = sC1.n
    wl_consts = sC2.n
    for _eng in ("tensor", "scalar", "vector", "gpsimd"):
        P.wait(_eng, sC1, ld_consts)
        P.wait(_eng, sC2, wl_consts)
    P.wait("vector", sC1, ld_consts)
    P.eop("vector", lambda Eo: Eo.tensor_scalar(maskadd[:], qpi[:], qoffb[:, 0:1], None, op0=ALU.add), sDV)
    P.eop("vector", lambda Eo: Eo.tensor_scalar(maskadd[:], maskadd[:], c_zero, None, op0=ALU.is_lt), sDV)
    P.op("vector", lambda Eo: Eo.tensor_scalar(maskadd[:], maskadd[:], c_neg1e30, None, op0=ALU.mult), inc=sDV)

    rXB = Rot(2)
    rXT = Rot(2)
    rQK = Rot(2)
    rV = Rot(2)

    rPXT = Rot(2)

    def chunk_setup(i, src_ap, is_q, kc):
        pi = i % 2
        rXB.acquire(P, "gpsimd", i)
        dXB.load(P, "gpsimd", i, lambda Eo, a=xb[pi][:], s=src_ap: Eo.dma_start(out=a, in_=s))
        dXB.wait(P, "tensor", i)
        rPXT.acquire(P, "tensor", i)
        rXT_psum = pXT[pi]
        for dc in range(8):
            P.op("tensor", lambda Eo, o=rXT_psum[:, dc * 128:(dc + 1) * 128],
                 a=xb[pi][:, dc * 128:(dc + 1) * 128]: Eo.transpose(o, a, idb[:]),
                 inc=(sPE if dc == 7 else None))
        transposes_done = sPE.n
        P.wait("vector", sPE)
        rXT.acquire(P, "vector", i)
        P.op("vector", lambda Eo, o=xT[pi][:], a=rXT_psum[:]: Eo.tensor_copy(o, a), inc=sDV)
        rPXT.release(i, sDV)
        evac_cnt = sDV.n
        P.wait("tensor", sDV, evac_cnt)
        if is_q:
            rQK.acquire(P, "tensor", i)
            for dc in range(8):
                P.op("tensor", lambda Eo, p=pQK[pi][:],
                     w=wqkv[:, dc * 192:dc * 192 + 64],
                     x=xT[pi][:, dc * 128:(dc + 1) * 128], st=(dc == 0), sp=(dc == 7):
                     Eo.matmul(p, w, x, start=st, stop=sp),
                     inc=(sPE if dc == 7 else None))
            P.wait("scalar", sPE)
            P.op("scalar", lambda Eo, o=qT[:, i * 128:(i + 1) * 128], p=pQK[pi][:]:
                 Eo.activation(o, p, AF.Identity, bias=qkb[0:64, 0:1], scale=f32(1.0)),
                 inc=sAC)
            rQK.release(i, sAC)
        else:
            rQK.acquire(P, "tensor", i)
            for dc in range(8):
                P.op("tensor", lambda Eo, p=pQK[pi][:],
                     w=wqkv[:, dc * 192 + 64:dc * 192 + 128],
                     x=xT[pi][:, dc * 128:(dc + 1) * 128], st=(dc == 0), sp=(dc == 7):
                     Eo.matmul(p, w, x, start=st, stop=sp),
                     inc=(sPE if dc == 7 else None))
            P.wait("scalar", sPE)
            P.op("scalar", lambda Eo, o=kT[:, kc * 128:(kc + 1) * 128], p=pQK[pi][:]:
                 Eo.activation(o, p, AF.Identity, bias=qkb[0:64, 1:2], scale=f32(1.0)),
                 inc=sAC)
            rQK.release(i, sAC)
            rV.acquire(P, "tensor", i)
            for dc in range(8):
                P.op("tensor", lambda Eo, p=pV[pi][:],
                     x=xT[pi][:, dc * 128:(dc + 1) * 128],
                     w=wqkv[:, dc * 192 + 128:dc * 192 + 192], st=(dc == 0):
                     Eo.matmul(p, x, w, start=st, stop=False))
            P.op("tensor", lambda Eo, p=pV[pi][:]:
                 Eo.matmul(p, onesb[:], bvr[:], start=False, stop=True), inc=sPE)
            P.wait("vector", sPE)
            P.op("vector", lambda Eo, o=vtm[:, kc * HS:(kc + 1) * HS], p=pV[pi][:]:
                 Eo.tensor_copy(o, p), inc=sDV)
            rV.release(i, sDV)
        rXB.rel[pi] = [(sPE, transposes_done)]
        rXT.rel[pi] = [(sPE, sPE.n)]

    for i in range(NCH):
        chunk_setup(i, d_xown[i * 128:(i + 1) * 128, :], True, None)
    for kc in range(KCH):
        chunk_setup(NCH + kc, d_xbat[kc * 128:(kc + 1) * 128, :], False, kc)

    P.wait("sync", sTAP)
    barrier(P)
    with nc.Block("b1") as blk:
        P.run(blk)
    es1.close()

    es2 = ExitStack()
    sb2 = lambda name, shape, dt: es2.enter_context(nc.sbuf_tensor(name, shape, dt))
    ps2 = lambda name, shape, dt: es2.enter_context(nc.psum_tensor(name, shape, dt))
    Psb = sb2("Psb", [128, T], BF16)
    scr = sb2("scr", [128, 512], BF16)
    PT = [sb2("PT0", [128, 512], BF16), sb2("PT1", [128, 512], BF16)]
    ots = sb2("ots", [64, 128], F32)
    pS = [ps2("pS0", [128, 512], F32), ps2("pS1", [128, 512], F32)]
    pT = ps2("pT", [128, 512], BF16)
    pO = ps2("pO", [64, 128], F32)
    pOt = ps2("pOt", [128, HS], F32)

    P = Prog()
    for _eng in ENGINES:
        P.wait(_eng, sC1, ld_consts)
        P.wait(_eng, sC2, wl_consts)
    rS = Rot(2)
    rBS = Rot(2)
    rPT = Rot(2)
    rPstrip = Rot(1)
    rPT_psum = Rot(1)
    rPO = Rot(1)

    for j in range(NCH):
        nblk = j + 1
        mcur, mprev = (mA, mB)
        dcur, dprev = (denA, denB)
        for kb in range(nblk):
            par = kb % 2
            rS.acquire(P, "tensor", kb)
            P.op("tensor", lambda Eo, p=pS[par][:],
                 q=qT[:, j * 128:(j + 1) * 128],
                 k=kT[:, kb * 512:(kb + 1) * 512]:
                 Eo.matmul(p, q, k, start=True, stop=True), inc=sPE)
            P.wait("vector", sPE)
            if kb == j:
                P.eop("vector", lambda Eo, p=pS[par][:]:
                      Eo.tensor_add(p, p, maskadd[:]), sDV)
            P.eop("vector", lambda Eo, p=pS[par][:]: Eo.reduce_max(dm[:], p, axis=AX.X), sDV)
            if kb == 0:
                P.eop("vector", lambda Eo: Eo.tensor_copy(mcur[:], dm[:]), sDV)
            else:
                P.eop("vector", lambda Eo, mo=mprev[:], mn=mcur[:]:
                      Eo.tensor_tensor(mn, mo, dm[:], op=ALU.max), sDV)
            P.op("vector", lambda Eo, mn=mcur[:]:
                 Eo.tensor_scalar(negm32[:], mn, c_neg132, None, op0=ALU.mult), inc=sDV)
            P.wait("scalar", sDV)
            rBS.acquire(P, "scalar", kb)
            P.op("scalar", lambda Eo, p=pS[par][:], b=bsum[par][:]:
                 Eo.activation(scr[:], p, AF.Exp, bias=negm32[:], scale=f32(1.0 / 32.0),
                               accum_out=b), inc=sAC)
            rS.release(kb, sAC)
            P.wait("vector", sAC)
            if kb == 0:
                P.eop("vector", lambda Eo, b=bsum[par][:], d=dcur[:]:
                      Eo.tensor_copy(d, b), sDV)
            else:
                P.op("vector", lambda Eo, mo=mprev[:], mn=mcur[:]:
                     Eo.tensor_tensor(dm[:], mo, mn, op=ALU.subtract), inc=sDV)
                P.wait("scalar", sDV)
                P.op("scalar", lambda Eo: Eo.activation(edm[:], dm[:], AF.Exp,
                                                        scale=f32(1.0 / 32.0)), inc=sAC)
                P.wait("vector", sAC)
                P.eop("vector", lambda Eo, do=dprev[:], dn=dcur[:], b=bsum[par][:]:
                      Eo.scalar_tensor_tensor(dn, do, edm[:], b, op0=ALU.mult, op1=ALU.add),
                      sDV)
            rBS.release(kb, sDV)
            mcur, mprev = mprev, mcur
            dcur, dprev = dprev, dcur
        mfin = mprev
        dfin = dprev
        P.wait("scalar", sDV)
        P.op("scalar", lambda Eo, d=dfin[:]: Eo.activation(lden[:], d, AF.Ln), inc=sAC)
        P.wait("vector", sAC)
        P.op("vector", lambda Eo, m=mfin[:]:
             Eo.scalar_tensor_tensor(bias2[:], m, c_neg132, lden[:],
                                     op0=ALU.mult, op1=ALU.subtract), inc=sDV)
        rPstrip.acquire(P, "scalar", 0)
        P.wait("scalar", sDV)
        for kb in range(nblk):
            par = kb % 2
            rS.acquire(P, "tensor", kb)
            P.op("tensor", lambda Eo, p=pS[par][:],
                 q=qT[:, j * 128:(j + 1) * 128],
                 k=kT[:, kb * 512:(kb + 1) * 512]:
                 Eo.matmul(p, q, k, start=True, stop=True), inc=sPE)
            if kb == j:
                P.wait("vector", sPE)
                P.op("vector", lambda Eo, p=pS[par][:]:
                     Eo.tensor_add(p, p, maskadd[:]), inc=sDV)
                P.wait("scalar", sDV)
            else:
                P.wait("scalar", sPE)
            P.op("scalar", lambda Eo, o=Psb[:, kb * 512:(kb + 1) * 512], p=pS[par][:]:
                 Eo.activation(o, p, AF.Exp, bias=bias2[:], scale=f32(1.0 / 32.0)),
                 inc=sAC)
            rS.release(kb, sAC)
            P.wait("tensor", sAC)
            rPT_psum.acquire(P, "tensor", 0)
            for s in range(4):
                P.op("tensor", lambda Eo, o=pT[:, s * 128:(s + 1) * 128],
                     a=Psb[:, kb * 512 + s * 128:kb * 512 + (s + 1) * 128]:
                     Eo.transpose(o, a, idb[:]), inc=(sPE if s == 3 else None))
            P.wait("vector", sPE)
            rPT.acquire(P, "vector", kb)
            P.op("vector", lambda Eo, o=PT[par][:]: Eo.tensor_copy(o, pT[:]), inc=sDV)
            rPT_psum.rel[0] = [(sDV, sDV.n)]
            P.wait("tensor", sDV)
            if kb == 0:
                rPO.acquire(P, "tensor", 0)
            for s in range(4):
                P.op("tensor", lambda Eo, v=vtm[:, (kb * 4 + s) * HS:(kb * 4 + s + 1) * HS],
                     pp=PT[par][:, s * 128:(s + 1) * 128],
                     st=(kb == 0 and s == 0), sp=(kb == nblk - 1 and s == 3):
                     Eo.matmul(pO[:], v, pp, start=st, stop=sp),
                     inc=(sPE if s == 3 else None))
            rPT.release(kb, sPE)
        rPstrip.rel[0] = [(sPE, sPE.n)]
        P.wait("scalar", sPE)
        P.op("scalar", lambda Eo: Eo.activation(ots[:], pO[:], AF.Copy), inc=sAC)
        rPO.rel[0] = [(sAC, sAC.n)]
        P.wait("tensor", sAC)
        P.op("tensor", lambda Eo: Eo.transpose(pOt[:], ots[:], idf[0:64, 0:64]), inc=sPE)
        P.wait("vector", sPE)
        P.op("vector", lambda Eo, o=o_tm[:, j * HS:(j + 1) * HS]:
             Eo.tensor_copy(o, pOt[:]), inc=sDV)
        P.wait("tensor", sDV)
        P.wait("scalar", sDV)

    if "o_tm" in taps:
        P.wait("sync", sDV)
        P.op("sync", lambda Eo: Eo.dma_start(out=taps["o_tm"], in_=o_tm[:]),
             inc=sTAP, inc_n=16)
    if "qT" in taps:
        P.wait("gpsimd", sDV)
        P.op("gpsimd", lambda Eo: Eo.dma_start(out=taps["qT"], in_=qT[:]),
             inc=sTAP, inc_n=16)
    if "kT" in taps:
        P.wait("gpsimd", sDV)
        P.op("gpsimd", lambda Eo: Eo.dma_start(out=taps["kT"], in_=kT[:]),
             inc=sTAP, inc_n=16)
    if "vtm" in taps:
        P.wait("gpsimd", sDV)
        P.op("gpsimd", lambda Eo: Eo.dma_start(out=taps["vtm"], in_=vtm[:]),
             inc=sTAP, inc_n=16)
    P.wait("sync", sTAP)
    P.wait("gpsimd", sTAP)
    barrier(P)
    with nc.Block("b2") as blk:
        P.run(blk)
    es2.close()

    es3 = ExitStack()
    sb3 = lambda name, shape, dt: es3.enter_context(nc.sbuf_tensor(name, shape, dt))
    ps3 = lambda name, shape, dt: es3.enter_context(nc.psum_tensor(name, shape, dt))
    zch = [sb3("zch0", [128, C], F32), sb3("zch1", [128, C], F32)]
    sqb = [sb3("sqb0", [128, C], F32), sb3("sqb1", [128, C], F32)]
    tmp4 = [sb3("tmp40", [128, 512], F32), sb3("tmp41", [128, 512], F32)]
    logits = sb3("logits", [E, TOK], F32)
    gtm = sb3("gtm", [128, 8], F32)
    gates_tm = sb3("gates_tm", [128, 128], F32)
    muv = [sb3("mu0", [128, 1], F32), sb3("mu1", [128, 1], F32)]
    varv = sb3("varv", [128, 1], F32)
    lnvv = sb3("lnvv", [128, 1], F32)
    rstdv = [sb3("rstd0", [128, 1], F32), sb3("rstd1", [128, 1], F32)]
    eq1 = sb3("eq1", [128, 8], F32)
    eq2 = sb3("eq2", [128, 8], F32)
    mskl = sb3("mskl", [128, 8], F32)
    m1v = sb3("m1v", [128, 1], F32)
    m2v = sb3("m2v", [128, 1], F32)
    edv = sb3("edv", [128, 1], F32)
    edv2 = sb3("edv2", [128, 1], F32)
    s1v = sb3("s1v", [128, 1], F32)
    rv = sb3("rv", [128, 1], F32)
    w2v = sb3("w2v", [128, 1], F32)
    g1v = sb3("g1v", [128, 8], F32)
    pTf = [ps3("pTf0", [128, 512], F32), ps3("pTf1", [128, 512], F32)]
    pL = ps3("pL", [E, TOK], F32)
    pG8 = ps3("pG8", [128, 16], F32)
    pTg = ps3("pTg", [E, 256], F32)

    P = Prog()
    for _eng in ENGINES:
        P.wait(_eng, sC1, ld_consts)
        P.wait(_eng, sC2, wl_consts)
    rZ = Rot(2)
    rSQ = Rot(2)
    rMU = Rot(2)
    rRS = Rot(2)
    rTF = Rot(2)
    rT4 = Rot(2)

    for tc in range(NCH):
        par = tc % 2
        z = zch[par]
        rZ.acquire(P, "sync", tc)
        dZC.load(P, "sync", tc, lambda Eo, o=z[:], s=d_xown[tc * 128:(tc + 1) * 128, :]:
                 Eo.dma_start(out=o, in_=s))
        dZC.wait(P, "vector", tc)
        orep = o_tm[:, tc * HS:(tc + 1) * HS].unsqueeze(1).broadcast_to((128, 16, HS))
        P.eop("vector", lambda Eo, z_=z[:], r=orep:
              Eo.tensor_add(z_.rearrange("p (r h) -> p r h", r=16), z_.rearrange("p (r h) -> p r h", r=16), r), sDV)
        rMU.acquire(P, "vector", tc)
        mu = muv[par]
        P.eop("vector", lambda Eo, z_=z[:], m=mu[:]: Eo.reduce_sum(m, z_, axis=AX.X), sDV)
        P.eop("vector", lambda Eo, m=mu[:]:
              Eo.tensor_scalar(m, m, c_inv1024, None, op0=ALU.mult), sDV)
        P.wait("scalar", sDV)
        rSQ.acquire(P, "scalar", tc)
        P.op("scalar", lambda Eo, z_=z[:], m=mu[:], q=sqb[par][:]:
             Eo.activation(q, z_, AF.Square, bias=m, scale=f32(-1.0)), inc=sAC)
        P.wait("vector", sAC)
        P.eop("vector", lambda Eo, q=sqb[par][:]: Eo.reduce_sum(varv[:], q, axis=AX.X), sDV)
        P.op("vector", lambda Eo:
             Eo.tensor_scalar(varv[:], varv[:], c_inv1024, None, op0=ALU.mult), inc=sDV)
        rSQ.rel[par] = [(sDV, sDV.n)]
        P.wait("scalar", sDV)
        rRS.acquire(P, "scalar", tc)
        rstd = rstdv[par]
        P.eop("scalar", lambda Eo: Eo.activation(lnvv[:], varv[:], AF.Ln, bias=f32(EPS)), sAC)
        P.op("scalar", lambda Eo, r=rstd[:]:
             Eo.activation(r, lnvv[:], AF.Exp, scale=f32(-0.5)), inc=sAC)
        P.wait("vector", sAC)
        P.op("vector", lambda Eo, z_=z[:], m=mu[:], r=rstd[:]:
             Eo.tensor_scalar(z_, z_, m, r, op0=ALU.subtract, op1=ALU.mult), inc=sDV)
        rMU.release(tc, sDV)
        rRS.release(tc, sDV)
        y1_done = sDV.n
        P.wait("sync", sDV, y1_done)
        dSPp.load(P, "sync", tc, lambda Eo, o=d_y1sp[tc * 128:(tc + 1) * 128, :], z_=z[:]:
                  Eo.dma_start(out=o, in_=z_))
        P.wait("tensor", sDV, y1_done)
        for g in range(2):
            tfp = pTf[g]
            rTF.acquire(P, "tensor", tc * 2 + g)
            for t in range(4):
                dc = g * 4 + t
                P.op("tensor", lambda Eo, o=tfp[:, t * 128:(t + 1) * 128],
                     a=z[:, dc * 128:(dc + 1) * 128]: Eo.transpose(o, a, idf[:]),
                     inc=(sPE if t == 3 else None))
            P.wait("vector", sPE)
            rT4.acquire(P, "vector", tc * 2 + g)
            P.op("vector", lambda Eo, o=tmp4[g][:], a=tfp[:]: Eo.tensor_copy(o, a), inc=sDV)
            P.wait("scalar", sDV)
            y1T_view = y1T[:].rearrange("p (d t) -> p d t", d=8)[:, g * 4:(g + 1) * 4,
                                                               tc * 128:(tc + 1) * 128]
            P.op("scalar", lambda Eo, o=y1T_view, a=tfp[:].rearrange("p (d t) -> p d t", d=4):
                 Eo.activation(o, a, AF.Copy), inc=sAC)
            rTF.release(tc * 2 + g, sAC)
            rTF.rel[(tc * 2 + g) % 2].append((sDV, sDV.n))
            P.wait("tensor", sDV)
            for t in range(4):
                dc = g * 4 + t
                P.op("tensor", lambda Eo, p=pL[:, tc * 128:(tc + 1) * 128],
                     w=rw[:, dc * 8:(dc + 1) * 8],
                     x=tmp4[g][:, t * 128:(t + 1) * 128], st=(dc == 0), sp=(dc == 7):
                     Eo.matmul(p, w, x, start=st, stop=sp),
                     inc=(sPE if t == 3 else None))
            rT4.release(tc * 2 + g, sPE)
        rZ.release(tc, sPE)
        rZ.rel[par].append(dSPp.sem_count(tc))
    P.wait("scalar", sPE)
    P.op("scalar", lambda Eo: Eo.activation(logits[:], pL[:], AF.Copy), inc=sAC)
    logits_done = sAC.n
    gate_dve_done = {}
    for tc in range(NCH):
        sl = tc % 2
        P.wait("tensor", sAC)
        if tc >= 2:
            P.wait("tensor", sDV, gate_dve_done[sl])
        P.op("tensor", lambda Eo, o=pG8[:, sl * 8:(sl + 1) * 8],
             a=logits[:, tc * 128:(tc + 1) * 128]:
             Eo.transpose(o, a, idf[0:8, 0:8]), inc=sPE)
        P.wait("vector", sPE)
        P.eop("vector", lambda Eo, p=pG8[:, sl * 8:(sl + 1) * 8]: Eo.tensor_copy(gtm[:], p), sDV)
        P.eop("vector", lambda Eo: Eo.reduce_max(m1v[:], gtm[:], axis=AX.X), sDV)
        P.eop("vector", lambda Eo: Eo.tensor_scalar(eq1[:], gtm[:], m1v[:, 0:1], None, op0=ALU.is_equal), sDV)
        P.eop("vector", lambda Eo: Eo.scalar_tensor_tensor(mskl[:], eq1[:], c_neg1e30, gtm[:],
                                                           op0=ALU.mult, op1=ALU.add), sDV)
        P.eop("vector", lambda Eo: Eo.reduce_max(m2v[:], mskl[:], axis=AX.X), sDV)
        P.eop("vector", lambda Eo: Eo.tensor_scalar(eq2[:], mskl[:], m2v[:, 0:1], None, op0=ALU.is_equal), sDV)
        P.op("vector", lambda Eo: Eo.tensor_tensor(edv2[:], m2v[:], m1v[:], op=ALU.subtract), inc=sDV)
        P.wait("scalar", sDV)
        P.op("scalar", lambda Eo: Eo.activation(edv[:], edv2[:], AF.Exp), inc=sAC)
        P.wait("vector", sAC)
        P.eop("vector", lambda Eo: Eo.tensor_scalar(s1v[:], edv[:], c_one, None, op0=ALU.add), sDV)
        P.eop("vector", lambda Eo: Eo.reciprocal(rv[:], s1v[:]), sDV)
        P.eop("vector", lambda Eo: Eo.tensor_tensor(w2v[:], edv[:], rv[:], op=ALU.mult), sDV)
        P.eop("vector", lambda Eo: Eo.tensor_scalar(g1v[:], eq1[:], rv[:, 0:1], None, op0=ALU.mult), sDV)
        P.op("vector", lambda Eo, o=gates_tm[:, tc * 8:(tc + 1) * 8]:
             Eo.scalar_tensor_tensor(o, eq2[:], w2v[:, 0:1], g1v[:], op0=ALU.mult, op1=ALU.add),
             inc=sDV)
        gate_dve_done[sl] = sDV.n
    P.wait("tensor", sDV)
    for tc in range(NCH):
        sl = tc % 2
        P.op("tensor", lambda Eo, o=pTg[:, sl * 128:(sl + 1) * 128],
             a=gates_tm[:, tc * 8:(tc + 1) * 8]:
             Eo.transpose(o, a, idf[:]), inc=sPE)
        P.wait("scalar", sPE)
        P.op("scalar", lambda Eo, o=gatesT[:, tc * 128:(tc + 1) * 128],
             p=pTg[:, sl * 128:(sl + 1) * 128]: Eo.activation(o, p, AF.Copy), inc=sAC)
        P.wait("tensor", sAC)

    if "logits" in taps:
        P.wait("sync", sAC, logits_done)
        P.op("sync", lambda Eo: Eo.dma_start(out=taps["logits"], in_=logits[:]),
             inc=sTAP, inc_n=16)
    if "gates" in taps:
        P.wait("sync", sDV)
        P.op("sync", lambda Eo: Eo.dma_start(out=taps["gates"], in_=gates_tm[:]),
             inc=sTAP, inc_n=16)
    if "y1T" in taps:
        P.wait("gpsimd", sAC)
        P.op("gpsimd", lambda Eo: Eo.dma_start(out=taps["y1T"], in_=y1T[:]),
             inc=sTAP, inc_n=16)
    if "gatesT" in taps:
        P.wait("gpsimd", sAC)
        P.op("gpsimd", lambda Eo: Eo.dma_start(out=taps["gatesT"], in_=gatesT[:]),
             inc=sTAP, inc_n=16)
    if "y1" in taps:
        P.wait("sync", dSPp.sems[0])
        P.wait("sync", dSPp.sems[1])
        P.op("sync", lambda Eo: Eo.dma_start(out=taps["y1"], in_=d_y1sp), inc=sTAP, inc_n=16)
    P.wait("sync", dSPp.sems[0])
    P.wait("sync", dSPp.sems[1])
    P.wait("sync", sTAP)
    barrier(P)
    with nc.Block("b3") as blk:
        P.run(blk)
    es3.close()
    esA.close()

    esO = ExitStack()
    O_acc = esO.enter_context(nc.sbuf_tensor("O_acc", [128, 8 * TOK], F32))
    es4 = ExitStack()
    sb4 = lambda name, shape, dt: es4.enter_context(nc.sbuf_tensor(name, shape, dt))
    ps4 = lambda name, shape, dt: es4.enter_context(nc.psum_tensor(name, shape, dt))
    H = sb4("H", [128, 32 * NT], BF16)
    w1b = [sb4("w1b0", [128, 8 * 256], BF16), sb4("w1b1", [128, 8 * 256], BF16)]
    w2b = [sb4("w2b0", [128, 16 * 128], BF16), sb4("w2b1", [128, 16 * 128], BF16)]
    gbb = [sb4("gbb0", [128, NT], BF16), sb4("gbb1", [128, NT], BF16)]
    pH = [ps4("pH0", [128, 1024], F32), ps4("pH1", [128, 1024], F32)]
    pO2 = [ps4("pO20", [128, 512], F32), ps4("pO21", [128, 512], F32)]
    pGB = ps4("pGB", [128, NT], F32)

    P = Prog()
    for _eng in ENGINES:
        P.wait(_eng, sC1, ld_consts)
        P.wait(_eng, sC2, wl_consts)
    rW1 = Rot(2)
    rW2 = Rot(2)
    rGB = Rot(2)
    rPH = Rot(2)
    rPO2 = Rot(2)
    rGBpsum = Rot(1)

    w1_view = d_w1
    w2_view = d_w2
    ei = 0
    for nt in range(2):
        for e in range(E):
            gpar = ei % 2
            rGBpsum.acquire(P, "tensor", 0)
            for s in range(2):
                P.op("tensor", lambda Eo, p=pGB[:, s * 512:(s + 1) * 512],
                     sel=selb[:, e * 128:(e + 1) * 128],
                     g=gatesT[:, nt * NT + s * 512:nt * NT + (s + 1) * 512]:
                     Eo.matmul(p, sel, g, start=True, stop=True),
                     inc=(sPE if s == 1 else None))
            P.wait("vector", sPE)
            rGB.acquire(P, "vector", ei)
            P.op("vector", lambda Eo, o=gbb[gpar][:]: Eo.tensor_copy(o, pGB[:]), inc=sDV)
            rGBpsum.rel[0] = [(sDV, sDV.n)]
            gb_ready = sDV.n
            for fp in range(16):
                wpar = (ei * 16 + fp) % 2
                rW1.acquire(P, "gpsimd", ei * 16 + fp)
                src = w1_view[e, :, fp * 256:(fp + 1) * 256].rearrange(
                    "(kc p) f -> p kc f", p=128)
                dW1.load(P, "gpsimd", ei * 16 + fp,
                         lambda Eo, o=w1b[wpar][:].rearrange("p (kc f) -> p kc f", kc=8),
                         s=src: Eo.dma_start(out=o, in_=s))
                dW1.wait(P, "tensor", ei * 16 + fp)
                for fc in range(2):
                    fcg = fp * 2 + fc
                    hpar = fcg % 2
                    rPH.acquire(P, "tensor", fcg)
                    for nb in range(2):
                        for kc in range(8):
                            P.op("tensor", lambda Eo,
                                 p=pH[hpar][:, nb * 512:(nb + 1) * 512],
                                 w=w1b[wpar][:, kc * 256 + fc * 128:kc * 256 + (fc + 1) * 128],
                                 x=y1T[:, kc * TOK + nt * NT + nb * 512:
                                       kc * TOK + nt * NT + (nb + 1) * 512],
                                 st=(kc == 0), sp=(kc == 7):
                                 Eo.matmul(p, w, x, start=st, stop=sp),
                                 inc=(sPE if (nb == 1 and kc == 7) else None))
                    P.wait("scalar", sPE)
                    hsl = H[:, fcg * NT:(fcg + 1) * NT]
                    P.op("scalar", lambda Eo, o=hsl, p=pH[hpar][:],
                         b=b1t[:, e * 32 + fcg:e * 32 + fcg + 1]:
                         Eo.activation(o, p, AF.Gelu_apprx_tanh, bias=b, scale=f32(1.0)),
                         inc=sAC)
                    rPH.release(fcg, sAC)
                    P.wait("vector", sAC)
                    P.wait("vector", sDV, gb_ready)
                    P.op("vector", lambda Eo, h=hsl, g=gbb[gpar][:]:
                         Eo.tensor_tensor(h, h, g, op=ALU.mult), inc=sDV)
                rW1.release(ei * 16 + fp, sPE)
            h_gated = sDV.n
            rGB.release(ei, sDV)
            P.wait("tensor", sDV, h_gated)
            for dc in range(8):
                for fh in range(2):
                    w2i = (ei * 16 + dc * 2 + fh) % 2
                    rW2.acquire(P, "gpsimd", ei * 16 + dc * 2 + fh)
                    src = w2_view[e, fh * 2048:(fh + 1) * 2048,
                                  dc * 128:(dc + 1) * 128].rearrange(
                        "(fc p) d -> p fc d", p=128)
                    dW2.load(P, "gpsimd", ei * 16 + dc * 2 + fh,
                             lambda Eo, o=w2b[w2i][:].rearrange("p (fc d) -> p fc d", fc=16),
                             s=src: Eo.dma_start(out=o, in_=s))
                    dW2.wait(P, "tensor", ei * 16 + dc * 2 + fh)
                    for nb in range(2):
                        opar = nb
                        if fh == 0:
                            rPO2.acquire(P, "tensor", ei * 16 + dc * 2 + nb)
                        for fcL in range(16):
                            fcg = fh * 16 + fcL
                            if e == 0 and fh == 0 and fcL == 0:
                                P.op("tensor", lambda Eo, p=pO2[opar][:],
                                     b=b2b[:, dc * 128:(dc + 1) * 128],
                                     g=gatesT[:, nt * NT + nb * 512:nt * NT + (nb + 1) * 512]:
                                     Eo.matmul(p, b, g, start=True, stop=False))
                            P.op("tensor", lambda Eo, p=pO2[opar][:],
                                 w=w2b[w2i][:, fcL * 128:(fcL + 1) * 128],
                                 x=H[:, fcg * NT + nb * 512:fcg * NT + (nb + 1) * 512],
                                 st=(fh == 0 and fcL == 0 and e != 0),
                                 sp=(fh == 1 and fcL == 15):
                                 Eo.matmul(p, w, x, start=st, stop=sp),
                                 inc=(sPE if fcL == 15 else None))
                        if fh == 1:
                            P.wait("vector", sPE)
                            osl = O_acc[:, dc * TOK + nt * NT + nb * 512:
                                        dc * TOK + nt * NT + (nb + 1) * 512]
                            if e == 0:
                                P.op("vector", lambda Eo, o=osl, p=pO2[opar][:]:
                                     Eo.tensor_copy(o, p), inc=sDV)
                            else:
                                P.op("vector", lambda Eo, o=osl, p=pO2[opar][:]:
                                     Eo.tensor_add(o, o, p), inc=sDV)
                            rPO2.release(ei * 16 + dc * 2 + nb, sDV)
                    rW2.release(ei * 16 + dc * 2 + fh, sPE)
            ei += 1

    if "O_acc" in taps:
        P.wait("sync", sDV)
        P.op("sync", lambda Eo: Eo.dma_start(out=taps["O_acc"], in_=O_acc[:]),
             inc=sTAP, inc_n=16)
    P.wait("sync", sTAP)
    barrier(P)
    with nc.Block("b4") as blk:
        P.run(blk)
    es4.close()

    es5 = ExitStack()
    sb5 = lambda name, shape, dt: es5.enter_context(nc.sbuf_tensor(name, shape, dt))
    ps5 = lambda name, shape, dt: es5.enter_context(nc.psum_tensor(name, shape, dt))
    y1ch = [sb5("y1ch0", [128, C], F32), sb5("y1ch1", [128, C], F32)]
    z2ch = [sb5("z2ch0", [128, C], F32), sb5("z2ch1", [128, C], F32)]
    sq2 = [sb5("sq20", [128, C], F32), sb5("sq21", [128, C], F32)]
    mu2 = [sb5("mu20", [128, 1], F32), sb5("mu21", [128, 1], F32)]
    var2 = sb5("var2", [128, 1], F32)
    lnv2 = sb5("lnv2", [128, 1], F32)
    rstd2 = [sb5("rstd20", [128, 1], F32), sb5("rstd21", [128, 1], F32)]
    pTf2 = [ps5("pTf20", [128, 512], F32), ps5("pTf21", [128, 512], F32)]

    P = Prog()
    for _eng in ENGINES:
        P.wait(_eng, sC1, ld_consts)
        P.wait(_eng, sC2, wl_consts)
    rY1 = Rot(2)
    rZ2 = Rot(2)
    rSQ2 = Rot(2)
    rMU2 = Rot(2)
    rRS2 = Rot(2)
    rTF2 = Rot(2)
    P.wait("sync", dSPp.sems[0])
    P.wait("sync", dSPp.sems[1])
    for tc in range(NCH):
        par = tc % 2
        rY1.acquire(P, "sync", tc)
        dYC.load(P, "sync", tc, lambda Eo, o=y1ch[par][:], s=d_y1sp[tc * 128:(tc + 1) * 128, :]:
                 Eo.dma_start(out=o, in_=s))
        for g in range(2):
            tfp = pTf2[g]
            rTF2.acquire(P, "tensor", tc * 2 + g)
            for t in range(4):
                dc = g * 4 + t
                P.op("tensor", lambda Eo, o=tfp[:, t * 128:(t + 1) * 128],
                     a=O_acc[:, dc * TOK + tc * 128:dc * TOK + (tc + 1) * 128]:
                     Eo.transpose(o, a, idf[:]), inc=(sPE if t == 3 else None))
            P.wait("vector", sPE)
            dYC.wait(P, "vector", tc)
            rZ2.acquire(P, "vector", tc) if g == 0 else None
            P.op("vector", lambda Eo, o=z2ch[par][:, g * 512:(g + 1) * 512],
                 a=tfp[:], b=y1ch[par][:, g * 512:(g + 1) * 512]:
                 Eo.tensor_add(o, a, b), inc=sDV)
            rTF2.release(tc * 2 + g, sDV)
        rY1.release(tc, sDV)
        z2 = z2ch[par]
        rMU2.acquire(P, "vector", tc)
        mu = mu2[par]
        P.wait("vector", sDV)
        P.eop("vector", lambda Eo, m=mu[:], z_=z2[:]: Eo.reduce_sum(m, z_, axis=AX.X), sDV)
        P.eop("vector", lambda Eo, m=mu[:]:
              Eo.tensor_scalar(m, m, c_inv1024, None, op0=ALU.mult), sDV)
        P.wait("scalar", sDV)
        rSQ2.acquire(P, "scalar", tc)
        P.op("scalar", lambda Eo, q=sq2[par][:], z_=z2[:], m=mu[:]:
             Eo.activation(q, z_, AF.Square, bias=m, scale=f32(-1.0)), inc=sAC)
        P.wait("vector", sAC)
        P.eop("vector", lambda Eo, q=sq2[par][:]: Eo.reduce_sum(var2[:], q, axis=AX.X), sDV)
        P.op("vector", lambda Eo:
             Eo.tensor_scalar(var2[:], var2[:], c_inv1024, None, op0=ALU.mult), inc=sDV)
        rSQ2.rel[par] = [(sDV, sDV.n)]
        P.wait("scalar", sDV)
        rRS2.acquire(P, "scalar", tc)
        P.eop("scalar", lambda Eo: Eo.activation(lnv2[:], var2[:], AF.Ln, bias=f32(EPS)), sAC)
        P.op("scalar", lambda Eo, r=rstd2[par][:]:
             Eo.activation(r, lnv2[:], AF.Exp, scale=f32(-0.5)), inc=sAC)
        P.wait("vector", sAC)
        P.op("vector", lambda Eo, z_=z2[:], m=mu[:], r=rstd2[par][:]:
             Eo.tensor_scalar(z_, z_, m, r, op0=ALU.subtract, op1=ALU.mult), inc=sDV)
        rMU2.release(tc, sDV)
        rRS2.release(tc, sDV)
        P.wait("sync", sDV)
        dSTp.load(P, "sync", tc, lambda Eo, o=d_y2[tc * 128:(tc + 1) * 128, :], z_=z2[:]:
                  Eo.dma_start(out=o, in_=z_))
        rZ2.rel[par] = [dSTp.sem_count(tc)]
    P.wait("sync", dSTp.sems[0])
    P.wait("sync", dSTp.sems[1])
    P.wait("sync", sTAP)
    with nc.Block("b5") as blk:
        P.run(blk)
    es5.close()
    esO.close()
    esP.close()
    esS.close()
    return nc


_CACHE = {}


def kernel(L0_attn_x, L0_attn_Wq, L0_attn_bq, L0_attn_Wk, L0_attn_bk,
           L0_attn_Wv, L0_attn_bv, L0_ffn_router_w, L0_ffn_w1, L0_ffn_b1,
           L0_ffn_w2, L0_ffn_b2):
    x = np.ascontiguousarray(np.asarray(L0_attn_x, dtype=np.float32))
    Wq = np.asarray(L0_attn_Wq, np.float32)
    Wk = np.asarray(L0_attn_Wk, np.float32)
    Wv = np.asarray(L0_attn_Wv, np.float32)
    bq = np.asarray(L0_attn_bq, np.float32)
    bk = np.asarray(L0_attn_bk, np.float32)
    bv = np.asarray(L0_attn_bv, np.float32)
    rw_in = np.asarray(L0_ffn_router_w, np.float32)
    w1 = np.ascontiguousarray(np.asarray(L0_ffn_w1, np.float32))
    b1 = np.asarray(L0_ffn_b1, np.float32)
    w2 = np.ascontiguousarray(np.asarray(L0_ffn_w2, np.float32))
    b2 = np.asarray(L0_ffn_b2, np.float32)

    wqkvT = np.concatenate([Wq.T, Wk.T, Wv.T], axis=1)
    wqkvT = np.ascontiguousarray(wqkvT.reshape(8, 128, 192).transpose(1, 0, 2))
    qkb = np.zeros((128, 2), np.float32)
    qkb[0:64, 0] = bq
    qkb[0:64, 1] = bk
    bvrow = bv.reshape(1, HS)
    b1t = np.ascontiguousarray(
        b1.reshape(E, 32, 128).transpose(2, 0, 1))
    rw_pre = np.ascontiguousarray(rw_in.reshape(8, 128, E).transpose(1, 0, 2))
    identb = np.eye(128, dtype=bf16)
    identf = np.eye(128, dtype=np.float32)
    qpiota = (np.arange(128, dtype=np.float32)[:, None]
              - np.arange(512, dtype=np.float32)[None, :])
    onesb = np.ones((1, 128), dtype=bf16)
    consts = np.zeros((128, 8), np.float32)
    consts[:, C_INV1024] = 1.0 / 1024.0
    consts[:, C_NEG132] = -1.0 / 32.0
    consts[:, C_ONE] = 1.0
    consts[:, C_NEG1E30] = -1e30
    selb = np.zeros((E, E * 128), dtype=bf16)
    for e in range(E):
        selb[e, e * 128:(e + 1) * 128] = 1

    nc = build_nc()
    in_maps = []
    for c in range(NCORES):
        b = c // 4
        ph = c % 4
        chunks = [4 * j + ph for j in range(NCH)]
        x_b = x[b]
        x_own = np.ascontiguousarray(
            x_b.reshape(64, 128, C)[chunks].reshape(TOK, C))
        qoffb = np.full((128, 1), 128.0 * ph, np.float32)
        in_maps.append(dict(
            x_own=x_own, x_batch=x_b, wqkvT=wqkvT, qkb=qkb, bvrow=bvrow,
            b1t=b1t, b2w=b2, rw=rw_pre, w1=w1, w2=w2, identb=identb,
            identf=identf, qpiota=qpiota, onesb=onesb, qoffb=qoffb,
            consts=consts, selb=selb))
    res = run_bass_kernel_spmd(nc, in_maps, core_ids=list(range(NCORES)))
    out = np.zeros((B, T, C), np.float32)
    for c in range(NCORES):
        b = c // 4
        ph = c % 4
        y2 = np.asarray(res.results[c]["y2"]).reshape(NCH, 128, C)
        for j in range(NCH):
            out[b, (4 * j + ph) * 128:(4 * j + ph + 1) * 128, :] = y2[j]
    return out


# version 04
# speedup vs baseline: 1.0734x; 1.0734x over previous
import numpy as np
import ml_dtypes
from contextlib import ExitStack

import concourse.bass as bass
import concourse.mybir as mybir
from concourse.bass_utils import run_bass_kernel_spmd

F32 = mybir.dt.float32
BF16 = mybir.dt.bfloat16
AF = mybir.ActivationFunctionType
ALU = mybir.AluOpType
AX = mybir.AxisListType
f32 = np.float32
bf16 = ml_dtypes.bfloat16

B, T, C = 2, 8192, 1024
HS = 64
E, FF = 8, 4096
NCORES = 8
TOK = 2048
NCH = 16
KCH = 64
NT = 1024
EPS = 1e-5

C_INV1024, C_NEG132, C_ONE, C_NEG1E30, C_ZERO = 0, 1, 2, 3, 4

ENGINES = ("sync", "scalar", "vector", "tensor", "gpsimd")


class Sem:
    def __init__(self, nc, es, name):
        self.h = es.enter_context(nc.semaphore(name))
        self.n = 0

    def bump(self, k):
        self.n += k
        return self.n


class Prog:

    def __init__(self):
        self.streams = {e: [] for e in ENGINES}

    def op(self, eng, fn, inc=None, inc_n=1):
        if inc is not None:
            inc.bump(inc_n)
            self.streams[eng].append(
                lambda Eo, fn=fn, s=inc.h, n=inc_n: fn(Eo).then_inc(s, n))
        else:
            self.streams[eng].append(lambda Eo, fn=fn: fn(Eo))

    def eop(self, eng, fn, sem):
        self.op(eng, fn, inc=sem)
        self.wait(eng, sem)

    def wait(self, eng, sem, val=None):
        v = sem.n if val is None else val
        if v > 0:
            self.streams[eng].append(lambda Eo, s=sem.h, v=v: Eo.wait_ge(s, v))

    def run(self, block):
        for eng in ENGINES:
            fns = tuple(self.streams[eng])

            def body(Eo, fns=fns):
                for f in fns:
                    f(Eo)

            getattr(block, eng)(body)


class Rot:

    def __init__(self, n):
        self.n = n
        self.rel = [[] for _ in range(n)]

    def acquire(self, P, eng, i):
        for sem, cnt in self.rel[i % self.n]:
            P.wait(eng, sem, cnt)
        self.rel[i % self.n] = []

    def release(self, i, sem):
        self.rel[i % self.n].append((sem, sem.n))


class DmaSlots:

    def __init__(self, nc, es, name, n=2):
        self.sems = [Sem(nc, es, "%s%d" % (name, i)) for i in range(n)]
        self.n = n

    def load(self, P, eng, i, fn):
        P.op(eng, fn, inc=self.sems[i % self.n], inc_n=16)

    def wait(self, P, eng, i):
        P.wait(eng, self.sems[i % self.n])

    def sem_count(self, i):
        s = self.sems[i % self.n]
        return (s, s.n)


def build_nc(debug_taps=()):
    nc = bass.Bass("TRN2", target_bir_lowering=False)

    d_xown = nc.dram_tensor("x_own", [TOK, C], F32, kind="ExternalInput").ap()
    d_xbat = nc.dram_tensor("x_batch", [T, C], F32, kind="ExternalInput").ap()
    d_wqkv = nc.dram_tensor("wqkvT", [128, 8, 192], F32, kind="ExternalInput").ap()
    d_qkb = nc.dram_tensor("qkb", [128, 2], F32, kind="ExternalInput").ap()
    d_bvr = nc.dram_tensor("bvrow", [1, HS], F32, kind="ExternalInput").ap()
    d_b1t = nc.dram_tensor("b1t", [128, E, 32], F32, kind="ExternalInput").ap()
    d_b2w = nc.dram_tensor("b2w", [E, C], F32, kind="ExternalInput").ap()
    d_rw = nc.dram_tensor("rw", [128, 8, E], F32, kind="ExternalInput").ap()
    d_w1 = nc.dram_tensor("w1", [E, C, FF], F32, kind="ExternalInput").ap()
    d_w2 = nc.dram_tensor("w2", [E, FF, C], F32, kind="ExternalInput").ap()
    d_idb = nc.dram_tensor("identb", [128, 128], BF16, kind="ExternalInput").ap()
    d_idf = nc.dram_tensor("identf", [128, 128], F32, kind="ExternalInput").ap()
    d_qpi = nc.dram_tensor("qpiota", [128, 512], F32, kind="ExternalInput").ap()
    d_ones = nc.dram_tensor("onesb", [1, 128], BF16, kind="ExternalInput").ap()
    d_qoff = nc.dram_tensor("qoffb", [128, 1], F32, kind="ExternalInput").ap()
    d_cst = nc.dram_tensor("consts", [128, 8], F32, kind="ExternalInput").ap()
    d_sel = nc.dram_tensor("selb", [E, E * 128], BF16, kind="ExternalInput").ap()
    d_y1sp = nc.dram_tensor("y1sp", [TOK, C], F32, kind="Internal").ap()
    d_y2 = nc.dram_tensor("y2", [TOK, C], F32, kind="ExternalOutput").ap()
    taps = {}
    for name, shape in debug_taps:
        taps[name] = nc.dram_tensor("tap_" + name, shape, F32, kind="ExternalOutput").ap()

    esP = ExitStack()
    sb = lambda name, shape, dt: esP.enter_context(nc.sbuf_tensor(name, shape, dt))

    idb = sb("idb", [128, 128], BF16)
    idf = sb("idf", [128, 128], F32)
    onesb = sb("onesb_s", [1, 128], BF16)
    qoffb = sb("qoffb_s", [128, 1], F32)
    cst = sb("cst", [128, 8], F32)
    b1t = sb("b1t_s", [128, E * 32], F32)
    b2b = sb("b2b", [E, C], BF16)
    rw = sb("rw_s", [128, 8 * E], F32)
    y1T = sb("y1T", [128, 8 * TOK], BF16)
    gatesT = sb("gatesT", [E, TOK], BF16)
    selb = sb("selb_s", [E, E * 128], BF16)

    esA = ExitStack()
    sbA = lambda name, shape, dt: esA.enter_context(nc.sbuf_tensor(name, shape, dt))
    qpi = sbA("qpi", [128, 512], F32)
    maskadd = sbA("maskadd", [128, 512], F32)
    qT = sbA("qT", [64, TOK], BF16)
    kT = sbA("kT", [64, T], BF16)
    vtm = sbA("vtm", [128, KCH * HS], BF16)
    o_tm = sbA("o_tm", [128, NCH * HS], F32)
    mA = sbA("mA", [128, 1], F32)
    mB = sbA("mB", [128, 1], F32)
    dm = sbA("dm", [128, 1], F32)
    edm = sbA("edm", [128, 1], F32)
    denA = sbA("denA", [128, 1], F32)
    denB = sbA("denB", [128, 1], F32)
    negm32 = sbA("negm32", [128, 1], F32)
    lden = sbA("lden", [128, 1], F32)
    bias2 = sbA("bias2", [128, 1], F32)
    bsum = [sbA("bsum0", [128, 1], F32), sbA("bsum1", [128, 1], F32)]

    esS = ExitStack()
    mks = lambda name: Sem(nc, esS, name)
    sC1 = mks("sC1")
    sC2 = mks("sC2")
    sTAP = mks("sTAP")
    sPE = mks("sPE")
    sAC = mks("sAC")
    sDV = mks("sDV")
    sBAR = mks("sBAR")
    dXB = DmaSlots(nc, esS, "dXB")
    dZC = DmaSlots(nc, esS, "dZC")
    dSPp = DmaSlots(nc, esS, "dSPp")
    dW1 = DmaSlots(nc, esS, "dW1")
    dW2 = DmaSlots(nc, esS, "dW2")
    dYC = DmaSlots(nc, esS, "dYC")
    dSTp = DmaSlots(nc, esS, "dSTp")

    c_inv1024 = cst[:, C_INV1024:C_INV1024 + 1]
    c_neg132 = cst[:, C_NEG132:C_NEG132 + 1]
    c_one = cst[:, C_ONE:C_ONE + 1]
    c_neg1e30 = cst[:, C_NEG1E30:C_NEG1E30 + 1]
    c_zero = cst[:, C_ZERO:C_ZERO + 1]

    barrier_count = [0]

    def barrier(P):
        barrier_count[0] += 1
        for eng in ENGINES:
            P.op(eng, lambda Eo: Eo.nop(), inc=sBAR)
        for eng in ENGINES:
            P.wait(eng, sBAR, 5 * barrier_count[0])

    es1 = ExitStack()
    sb1 = lambda name, shape, dt: es1.enter_context(nc.sbuf_tensor(name, shape, dt))
    ps1 = lambda name, shape, dt: es1.enter_context(nc.psum_tensor(name, shape, dt))
    xb = [sb1("xb0", [128, C], BF16), sb1("xb1", [128, C], BF16)]
    xT = [sb1("xT0", [128, C], BF16), sb1("xT1", [128, C], BF16)]
    wqkv = sb1("wqkv", [128, 8 * 192], BF16)
    bvr = sb1("bvr", [1, HS], BF16)
    qkb = sb1("qkb_s", [128, 2], F32)
    pXT = [ps1("pXT0", [128, C], BF16), ps1("pXT1", [128, C], BF16)]
    pQK = [ps1("pQK0", [64, 128], F32), ps1("pQK1", [64, 128], F32)]
    pV = [ps1("pV0", [128, HS], F32), ps1("pV1", [128, HS], F32)]

    P = Prog()
    for dst, src in [(idb, d_idb), (idf, d_idf), (qpi, d_qpi), (qoffb, d_qoff),
                     (cst, d_cst), (qkb[:], d_qkb),
                     (b1t, d_b1t.rearrange("p e k -> p (e k)")),
                     (rw, d_rw.rearrange("p c e -> p (c e)"))]:
        ap = dst[:] if hasattr(dst, "ap") else dst
        P.op("sync", lambda Eo, a=ap, s=src: Eo.dma_start(out=a, in_=s), inc=sC1, inc_n=16)
    for dst, src in [(onesb[:], d_ones), (selb[:], d_sel),
                     (wqkv[:], d_wqkv.rearrange("p c h -> p (c h)")),
                     (bvr[:], d_bvr), (b2b[:], d_b2w)]:
        P.op("gpsimd", lambda Eo, a=dst, s=src: Eo.dma_start(out=a, in_=s), inc=sC2, inc_n=16)
    ld_consts = sC1.n
    wl_consts = sC2.n
    for _eng in ("tensor", "scalar", "vector", "gpsimd"):
        P.wait(_eng, sC1, ld_consts)
        P.wait(_eng, sC2, wl_consts)
    P.wait("vector", sC1, ld_consts)
    P.eop("vector", lambda Eo: Eo.tensor_scalar(maskadd[:], qpi[:], qoffb[:, 0:1], None, op0=ALU.add), sDV)
    P.eop("vector", lambda Eo: Eo.tensor_scalar(maskadd[:], maskadd[:], c_zero, None, op0=ALU.is_lt), sDV)
    P.op("vector", lambda Eo: Eo.tensor_scalar(maskadd[:], maskadd[:], c_neg1e30, None, op0=ALU.mult), inc=sDV)

    rXB = Rot(2)
    rXT = Rot(2)
    rQK = Rot(2)
    rV = Rot(2)

    rPXT = Rot(2)

    def chunk_setup(i, src_ap, is_q, kc):
        pi = i % 2
        rXB.acquire(P, "gpsimd", i)
        dXB.load(P, "gpsimd", i, lambda Eo, a=xb[pi][:], s=src_ap: Eo.dma_start(out=a, in_=s))
        dXB.wait(P, "tensor", i)
        rPXT.acquire(P, "tensor", i)
        rXT_psum = pXT[pi]
        for dc in range(8):
            P.op("tensor", lambda Eo, o=rXT_psum[:, dc * 128:(dc + 1) * 128],
                 a=xb[pi][:, dc * 128:(dc + 1) * 128]: Eo.transpose(o, a, idb[:]),
                 inc=(sPE if dc == 7 else None))
        transposes_done = sPE.n
        P.wait("vector", sPE)
        rXT.acquire(P, "vector", i)
        P.op("vector", lambda Eo, o=xT[pi][:], a=rXT_psum[:]: Eo.tensor_copy(o, a), inc=sDV)
        rPXT.release(i, sDV)
        evac_cnt = sDV.n
        P.wait("tensor", sDV, evac_cnt)
        if is_q:
            rQK.acquire(P, "tensor", i)
            for dc in range(8):
                P.op("tensor", lambda Eo, p=pQK[pi][:],
                     w=wqkv[:, dc * 192:dc * 192 + 64],
                     x=xT[pi][:, dc * 128:(dc + 1) * 128], st=(dc == 0), sp=(dc == 7):
                     Eo.matmul(p, w, x, start=st, stop=sp),
                     inc=(sPE if dc == 7 else None))
            P.wait("scalar", sPE)
            P.op("scalar", lambda Eo, o=qT[:, i * 128:(i + 1) * 128], p=pQK[pi][:]:
                 Eo.activation(o, p, AF.Identity, bias=qkb[0:64, 0:1], scale=f32(1.0)),
                 inc=sAC)
            rQK.release(i, sAC)
        else:
            rQK.acquire(P, "tensor", i)
            for dc in range(8):
                P.op("tensor", lambda Eo, p=pQK[pi][:],
                     w=wqkv[:, dc * 192 + 64:dc * 192 + 128],
                     x=xT[pi][:, dc * 128:(dc + 1) * 128], st=(dc == 0), sp=(dc == 7):
                     Eo.matmul(p, w, x, start=st, stop=sp),
                     inc=(sPE if dc == 7 else None))
            P.wait("scalar", sPE)
            P.op("scalar", lambda Eo, o=kT[:, kc * 128:(kc + 1) * 128], p=pQK[pi][:]:
                 Eo.activation(o, p, AF.Identity, bias=qkb[0:64, 1:2], scale=f32(1.0)),
                 inc=sAC)
            rQK.release(i, sAC)
            rV.acquire(P, "tensor", i)
            for dc in range(8):
                P.op("tensor", lambda Eo, p=pV[pi][:],
                     x=xT[pi][:, dc * 128:(dc + 1) * 128],
                     w=wqkv[:, dc * 192 + 128:dc * 192 + 192], st=(dc == 0):
                     Eo.matmul(p, x, w, start=st, stop=False))
            P.op("tensor", lambda Eo, p=pV[pi][:]:
                 Eo.matmul(p, onesb[:], bvr[:], start=False, stop=True), inc=sPE)
            P.wait("vector", sPE)
            P.op("vector", lambda Eo, o=vtm[:, kc * HS:(kc + 1) * HS], p=pV[pi][:]:
                 Eo.tensor_copy(o, p), inc=sDV)
            rV.release(i, sDV)
        rXB.rel[pi] = [(sPE, transposes_done)]
        rXT.rel[pi] = [(sPE, sPE.n)]

    for i in range(NCH):
        chunk_setup(i, d_xown[i * 128:(i + 1) * 128, :], True, None)
    for kc in range(KCH):
        chunk_setup(NCH + kc, d_xbat[kc * 128:(kc + 1) * 128, :], False, kc)

    P.wait("sync", sTAP)
    barrier(P)
    with nc.Block("b1") as blk:
        P.run(blk)
    es1.close()

    es2 = ExitStack()
    sb2 = lambda name, shape, dt: es2.enter_context(nc.sbuf_tensor(name, shape, dt))
    ps2 = lambda name, shape, dt: es2.enter_context(nc.psum_tensor(name, shape, dt))
    Psb = [sb2("Psb0", [128, T], BF16), sb2("Psb1", [128, T], BF16)]
    PT = [sb2("PT0", [128, 512], BF16), sb2("PT1", [128, 512], BF16)]
    ots = sb2("ots", [64, 128], F32)
    mtab = [sb2("mtab0", [128, NCH], F32), sb2("mtab1", [128, NCH], F32)]
    dentab = [sb2("dentab0", [128, NCH], F32), sb2("dentab1", [128, NCH], F32)]
    negm32 = [sb2("negm32a", [128, 1], F32), sb2("negm32b", [128, 1], F32)]
    recipd = [sb2("recipd0", [128, 1], F32), sb2("recipd1", [128, 1], F32)]
    mfin = sb2("mfin", [128, 1], F32)
    denf = sb2("denf", [128, 1], F32)
    pSA = [ps2("pSA0", [128, 512], F32), ps2("pSA1", [128, 512], F32)]
    pSB = [ps2("pSB0", [128, 512], F32), ps2("pSB1", [128, 512], F32)]
    pT = ps2("pT", [128, 512], BF16)
    pO = ps2("pO", [64, 128], F32)
    pOt = ps2("pOt", [128, HS], F32)

    P = Prog()
    for _eng in ENGINES:
        P.wait(_eng, sC1, ld_consts)
        P.wait(_eng, sC2, wl_consts)
    rSA = Rot(2)
    rSB = Rot(2)
    rPT = Rot(2)
    rPT_psum = Rot(1)
    rPO = Rot(1)
    rPstrip = Rot(2)
    rMT = Rot(2)
    rNM = Rot(2)
    rDT = Rot(2)
    rRD = Rot(2)

    def pass_a(j):
        par2 = j % 2
        rMT.acquire(P, "vector", j)
        for kb in range(j + 1):
            par = kb % 2
            rSA.acquire(P, "tensor", kb)
            P.op("tensor", lambda Eo, p=pSA[par][:],
                 q=qT[:, j * 128:(j + 1) * 128],
                 k=kT[:, kb * 512:(kb + 1) * 512]:
                 Eo.matmul(p, q, k, start=True, stop=True), inc=sPE)
            P.wait("vector", sPE)
            if kb == j:
                P.eop("vector", lambda Eo, p=pSA[par][:]:
                      Eo.tensor_add(p, p, maskadd[:]), sDV)
            P.op("vector", lambda Eo, o=mtab[par2][:, kb:kb + 1], p=pSA[par][:]:
                 Eo.reduce_max(o, p, axis=AX.X), inc=sDV)
            rSA.release(kb, sDV)
        P.wait("vector", sDV)
        P.eop("vector", lambda Eo, mt=mtab[par2][:, 0:j + 1]:
              Eo.reduce_max(mfin[:], mt, axis=AX.X), sDV)
        rNM.acquire(P, "vector", j)
        P.op("vector", lambda Eo, o=negm32[par2][:]:
             Eo.tensor_scalar(o, mfin[:], c_neg132, None, op0=ALU.mult), inc=sDV)
        nm_done[j] = sDV.n

    def pass_b(j):
        par2 = j % 2
        rPstrip.acquire(P, "scalar", j)
        P.wait("scalar", sDV, nm_done[j])
        rDT.acquire(P, "scalar", j)
        for kb in range(j + 1):
            par = kb % 2
            rSB.acquire(P, "tensor", kb)
            P.op("tensor", lambda Eo, p=pSB[par][:],
                 q=qT[:, j * 128:(j + 1) * 128],
                 k=kT[:, kb * 512:(kb + 1) * 512]:
                 Eo.matmul(p, q, k, start=True, stop=True), inc=sPE)
            if kb == j:
                P.wait("vector", sPE)
                P.op("vector", lambda Eo, p=pSB[par][:]:
                     Eo.tensor_add(p, p, maskadd[:]), inc=sDV)
                P.wait("scalar", sDV)
            else:
                P.wait("scalar", sPE)
            P.op("scalar", lambda Eo, o=Psb[par2][:, kb * 512:(kb + 1) * 512], p=pSB[par][:],
                 nm=negm32[par2][:], acc=dentab[par2][:, kb:kb + 1]:
                 Eo.activation(o, p, AF.Exp, bias=nm, scale=f32(1.0 / 32.0),
                               accum_out=acc), inc=sAC)
            rSB.release(kb, sAC)
            P.wait("tensor", sAC)
            rPT_psum.acquire(P, "tensor", 0)
            for s in range(4):
                P.op("tensor", lambda Eo, o=pT[:, s * 128:(s + 1) * 128],
                     a=Psb[par2][:, kb * 512 + s * 128:kb * 512 + (s + 1) * 128]:
                     Eo.transpose(o, a, idb[:]), inc=(sPE if s == 3 else None))
            P.wait("vector", sPE)
            rPT.acquire(P, "vector", kb)
            P.op("vector", lambda Eo, o=PT[par][:]: Eo.tensor_copy(o, pT[:]), inc=sDV)
            rPT_psum.rel[0] = [(sDV, sDV.n)]
            P.wait("tensor", sDV)
            if kb == 0:
                rPO.acquire(P, "tensor", 0)
            for s in range(4):
                P.op("tensor", lambda Eo, v=vtm[:, (kb * 4 + s) * HS:(kb * 4 + s + 1) * HS],
                     pp=PT[par][:, s * 128:(s + 1) * 128],
                     st=(kb == 0 and s == 0), sp=(kb == j and s == 3):
                     Eo.matmul(pO[:], v, pp, start=st, stop=sp),
                     inc=(sPE if s == 3 else None))
            rPT.release(kb, sPE)
        rPstrip.rel[par2] = [(sPE, sPE.n)]
        rNM.rel[par2] = [(sAC, sAC.n)]
        P.wait("vector", sAC)
        P.wait("vector", sDV)
        P.eop("vector", lambda Eo, dt_=dentab[par2][:, 0:j + 1]:
              Eo.reduce_sum(denf[:], dt_, axis=AX.X), sDV)
        rRD.acquire(P, "vector", j)
        P.eop("vector", lambda Eo, o=recipd[par2][:]:
              Eo.reciprocal(o, denf[:]), sDV)
        rDT.rel[par2] = [(sDV, sDV.n)]
        P.wait("scalar", sPE)
        P.op("scalar", lambda Eo: Eo.activation(ots[:], pO[:], AF.Copy), inc=sAC)
        rPO.rel[0] = [(sAC, sAC.n)]
        P.wait("tensor", sAC)
        P.op("tensor", lambda Eo: Eo.transpose(pOt[:], ots[:], idf[0:64, 0:64]), inc=sPE)
        P.wait("vector", sPE)
        P.op("vector", lambda Eo, o=o_tm[:, j * HS:(j + 1) * HS], r=recipd[par2][:]:
             Eo.tensor_scalar(o, pOt[:], r[:, 0:1], None, op0=ALU.mult), inc=sDV)
        rRD.rel[par2] = [(sDV, sDV.n)]
        P.wait("tensor", sDV)
        P.wait("scalar", sDV)

    nm_done = {}
    for j in range(NCH):
        pass_a(j)
        if j > 0:
            pass_b(j - 1)
    pass_b(NCH - 1)

    if "o_tm" in taps:
        P.wait("sync", sDV)
        P.op("sync", lambda Eo: Eo.dma_start(out=taps["o_tm"], in_=o_tm[:]),
             inc=sTAP, inc_n=16)
    if "qT" in taps:
        P.wait("gpsimd", sDV)
        P.op("gpsimd", lambda Eo: Eo.dma_start(out=taps["qT"], in_=qT[:]),
             inc=sTAP, inc_n=16)
    if "kT" in taps:
        P.wait("gpsimd", sDV)
        P.op("gpsimd", lambda Eo: Eo.dma_start(out=taps["kT"], in_=kT[:]),
             inc=sTAP, inc_n=16)
    if "vtm" in taps:
        P.wait("gpsimd", sDV)
        P.op("gpsimd", lambda Eo: Eo.dma_start(out=taps["vtm"], in_=vtm[:]),
             inc=sTAP, inc_n=16)
    P.wait("sync", sTAP)
    P.wait("gpsimd", sTAP)
    barrier(P)
    with nc.Block("b2") as blk:
        P.run(blk)
    es2.close()

    es3 = ExitStack()
    sb3 = lambda name, shape, dt: es3.enter_context(nc.sbuf_tensor(name, shape, dt))
    ps3 = lambda name, shape, dt: es3.enter_context(nc.psum_tensor(name, shape, dt))
    zch = [sb3("zch0", [128, C], F32), sb3("zch1", [128, C], F32)]
    sqb = [sb3("sqb0", [128, C], F32), sb3("sqb1", [128, C], F32)]
    tmp4 = [sb3("tmp40", [128, 512], F32), sb3("tmp41", [128, 512], F32)]
    logits = sb3("logits", [E, TOK], F32)
    gtm = sb3("gtm", [128, 8], F32)
    gates_tm = sb3("gates_tm", [128, 128], F32)
    muv = [sb3("mu0", [128, 1], F32), sb3("mu1", [128, 1], F32)]
    varv = sb3("varv", [128, 1], F32)
    lnvv = sb3("lnvv", [128, 1], F32)
    rstdv = [sb3("rstd0", [128, 1], F32), sb3("rstd1", [128, 1], F32)]
    eq1 = sb3("eq1", [128, 8], F32)
    eq2 = sb3("eq2", [128, 8], F32)
    mskl = sb3("mskl", [128, 8], F32)
    m1v = sb3("m1v", [128, 1], F32)
    m2v = sb3("m2v", [128, 1], F32)
    edv = sb3("edv", [128, 1], F32)
    edv2 = sb3("edv2", [128, 1], F32)
    s1v = sb3("s1v", [128, 1], F32)
    rv = sb3("rv", [128, 1], F32)
    w2v = sb3("w2v", [128, 1], F32)
    g1v = sb3("g1v", [128, 8], F32)
    pTf = [ps3("pTf0", [128, 512], F32), ps3("pTf1", [128, 512], F32)]
    pL = ps3("pL", [E, TOK], F32)
    pG8 = ps3("pG8", [128, 16], F32)
    pTg = ps3("pTg", [E, 256], F32)

    P = Prog()
    for _eng in ENGINES:
        P.wait(_eng, sC1, ld_consts)
        P.wait(_eng, sC2, wl_consts)
    rZ = Rot(2)
    rSQ = Rot(2)
    rMU = Rot(2)
    rRS = Rot(2)
    rTF = Rot(2)
    rT4 = Rot(2)

    for tc in range(NCH):
        par = tc % 2
        z = zch[par]
        rZ.acquire(P, "sync", tc)
        dZC.load(P, "sync", tc, lambda Eo, o=z[:], s=d_xown[tc * 128:(tc + 1) * 128, :]:
                 Eo.dma_start(out=o, in_=s))
        dZC.wait(P, "vector", tc)
        orep = o_tm[:, tc * HS:(tc + 1) * HS].unsqueeze(1).broadcast_to((128, 16, HS))
        P.eop("vector", lambda Eo, z_=z[:], r=orep:
              Eo.tensor_add(z_.rearrange("p (r h) -> p r h", r=16), z_.rearrange("p (r h) -> p r h", r=16), r), sDV)
        rMU.acquire(P, "vector", tc)
        mu = muv[par]
        P.eop("vector", lambda Eo, z_=z[:], m=mu[:]: Eo.reduce_sum(m, z_, axis=AX.X), sDV)
        P.eop("vector", lambda Eo, m=mu[:]:
              Eo.tensor_scalar(m, m, c_inv1024, None, op0=ALU.mult), sDV)
        P.wait("scalar", sDV)
        rSQ.acquire(P, "scalar", tc)
        P.op("scalar", lambda Eo, z_=z[:], m=mu[:], q=sqb[par][:]:
             Eo.activation(q, z_, AF.Square, bias=m, scale=f32(-1.0)), inc=sAC)
        P.wait("vector", sAC)
        P.eop("vector", lambda Eo, q=sqb[par][:]: Eo.reduce_sum(varv[:], q, axis=AX.X), sDV)
        P.op("vector", lambda Eo:
             Eo.tensor_scalar(varv[:], varv[:], c_inv1024, None, op0=ALU.mult), inc=sDV)
        rSQ.rel[par] = [(sDV, sDV.n)]
        P.wait("scalar", sDV)
        rRS.acquire(P, "scalar", tc)
        rstd = rstdv[par]
        P.eop("scalar", lambda Eo: Eo.activation(lnvv[:], varv[:], AF.Ln, bias=f32(EPS)), sAC)
        P.op("scalar", lambda Eo, r=rstd[:]:
             Eo.activation(r, lnvv[:], AF.Exp, scale=f32(-0.5)), inc=sAC)
        P.wait("vector", sAC)
        P.op("vector", lambda Eo, z_=z[:], m=mu[:], r=rstd[:]:
             Eo.tensor_scalar(z_, z_, m, r, op0=ALU.subtract, op1=ALU.mult), inc=sDV)
        rMU.release(tc, sDV)
        rRS.release(tc, sDV)
        y1_done = sDV.n
        P.wait("sync", sDV, y1_done)
        dSPp.load(P, "sync", tc, lambda Eo, o=d_y1sp[tc * 128:(tc + 1) * 128, :], z_=z[:]:
                  Eo.dma_start(out=o, in_=z_))
        P.wait("tensor", sDV, y1_done)
        for g in range(2):
            tfp = pTf[g]
            rTF.acquire(P, "tensor", tc * 2 + g)
            for t in range(4):
                dc = g * 4 + t
                P.op("tensor", lambda Eo, o=tfp[:, t * 128:(t + 1) * 128],
                     a=z[:, dc * 128:(dc + 1) * 128]: Eo.transpose(o, a, idf[:]),
                     inc=(sPE if t == 3 else None))
            P.wait("vector", sPE)
            rT4.acquire(P, "vector", tc * 2 + g)
            P.op("vector", lambda Eo, o=tmp4[g][:], a=tfp[:]: Eo.tensor_copy(o, a), inc=sDV)
            P.wait("scalar", sDV)
            y1T_view = y1T[:].rearrange("p (d t) -> p d t", d=8)[:, g * 4:(g + 1) * 4,
                                                               tc * 128:(tc + 1) * 128]
            P.op("scalar", lambda Eo, o=y1T_view, a=tfp[:].rearrange("p (d t) -> p d t", d=4):
                 Eo.activation(o, a, AF.Copy), inc=sAC)
            rTF.release(tc * 2 + g, sAC)
            rTF.rel[(tc * 2 + g) % 2].append((sDV, sDV.n))
            P.wait("tensor", sDV)
            for t in range(4):
                dc = g * 4 + t
                P.op("tensor", lambda Eo, p=pL[:, tc * 128:(tc + 1) * 128],
                     w=rw[:, dc * 8:(dc + 1) * 8],
                     x=tmp4[g][:, t * 128:(t + 1) * 128], st=(dc == 0), sp=(dc == 7):
                     Eo.matmul(p, w, x, start=st, stop=sp),
                     inc=(sPE if t == 3 else None))
            rT4.release(tc * 2 + g, sPE)
        rZ.release(tc, sPE)
        rZ.rel[par].append(dSPp.sem_count(tc))
    P.wait("scalar", sPE)
    P.op("scalar", lambda Eo: Eo.activation(logits[:], pL[:], AF.Copy), inc=sAC)
    logits_done = sAC.n
    gate_dve_done = {}
    for tc in range(NCH):
        sl = tc % 2
        P.wait("tensor", sAC)
        if tc >= 2:
            P.wait("tensor", sDV, gate_dve_done[sl])
        P.op("tensor", lambda Eo, o=pG8[:, sl * 8:(sl + 1) * 8],
             a=logits[:, tc * 128:(tc + 1) * 128]:
             Eo.transpose(o, a, idf[0:8, 0:8]), inc=sPE)
        P.wait("vector", sPE)
        P.eop("vector", lambda Eo, p=pG8[:, sl * 8:(sl + 1) * 8]: Eo.tensor_copy(gtm[:], p), sDV)
        P.eop("vector", lambda Eo: Eo.reduce_max(m1v[:], gtm[:], axis=AX.X), sDV)
        P.eop("vector", lambda Eo: Eo.tensor_scalar(eq1[:], gtm[:], m1v[:, 0:1], None, op0=ALU.is_equal), sDV)
        P.eop("vector", lambda Eo: Eo.scalar_tensor_tensor(mskl[:], eq1[:], c_neg1e30, gtm[:],
                                                           op0=ALU.mult, op1=ALU.add), sDV)
        P.eop("vector", lambda Eo: Eo.reduce_max(m2v[:], mskl[:], axis=AX.X), sDV)
        P.eop("vector", lambda Eo: Eo.tensor_scalar(eq2[:], mskl[:], m2v[:, 0:1], None, op0=ALU.is_equal), sDV)
        P.op("vector", lambda Eo: Eo.tensor_tensor(edv2[:], m2v[:], m1v[:], op=ALU.subtract), inc=sDV)
        P.wait("scalar", sDV)
        P.op("scalar", lambda Eo: Eo.activation(edv[:], edv2[:], AF.Exp), inc=sAC)
        P.wait("vector", sAC)
        P.eop("vector", lambda Eo: Eo.tensor_scalar(s1v[:], edv[:], c_one, None, op0=ALU.add), sDV)
        P.eop("vector", lambda Eo: Eo.reciprocal(rv[:], s1v[:]), sDV)
        P.eop("vector", lambda Eo: Eo.tensor_tensor(w2v[:], edv[:], rv[:], op=ALU.mult), sDV)
        P.eop("vector", lambda Eo: Eo.tensor_scalar(g1v[:], eq1[:], rv[:, 0:1], None, op0=ALU.mult), sDV)
        P.op("vector", lambda Eo, o=gates_tm[:, tc * 8:(tc + 1) * 8]:
             Eo.scalar_tensor_tensor(o, eq2[:], w2v[:, 0:1], g1v[:], op0=ALU.mult, op1=ALU.add),
             inc=sDV)
        gate_dve_done[sl] = sDV.n
    P.wait("tensor", sDV)
    for tc in range(NCH):
        sl = tc % 2
        P.op("tensor", lambda Eo, o=pTg[:, sl * 128:(sl + 1) * 128],
             a=gates_tm[:, tc * 8:(tc + 1) * 8]:
             Eo.transpose(o, a, idf[:]), inc=sPE)
        P.wait("scalar", sPE)
        P.op("scalar", lambda Eo, o=gatesT[:, tc * 128:(tc + 1) * 128],
             p=pTg[:, sl * 128:(sl + 1) * 128]: Eo.activation(o, p, AF.Copy), inc=sAC)
        P.wait("tensor", sAC)

    if "logits" in taps:
        P.wait("sync", sAC, logits_done)
        P.op("sync", lambda Eo: Eo.dma_start(out=taps["logits"], in_=logits[:]),
             inc=sTAP, inc_n=16)
    if "gates" in taps:
        P.wait("sync", sDV)
        P.op("sync", lambda Eo: Eo.dma_start(out=taps["gates"], in_=gates_tm[:]),
             inc=sTAP, inc_n=16)
    if "y1T" in taps:
        P.wait("gpsimd", sAC)
        P.op("gpsimd", lambda Eo: Eo.dma_start(out=taps["y1T"], in_=y1T[:]),
             inc=sTAP, inc_n=16)
    if "gatesT" in taps:
        P.wait("gpsimd", sAC)
        P.op("gpsimd", lambda Eo: Eo.dma_start(out=taps["gatesT"], in_=gatesT[:]),
             inc=sTAP, inc_n=16)
    if "y1" in taps:
        P.wait("sync", dSPp.sems[0])
        P.wait("sync", dSPp.sems[1])
        P.op("sync", lambda Eo: Eo.dma_start(out=taps["y1"], in_=d_y1sp), inc=sTAP, inc_n=16)
    P.wait("sync", dSPp.sems[0])
    P.wait("sync", dSPp.sems[1])
    P.wait("sync", sTAP)
    barrier(P)
    with nc.Block("b3") as blk:
        P.run(blk)
    es3.close()
    esA.close()

    esO = ExitStack()
    O_acc = esO.enter_context(nc.sbuf_tensor("O_acc", [128, 8 * TOK], F32))
    es4 = ExitStack()
    sb4 = lambda name, shape, dt: es4.enter_context(nc.sbuf_tensor(name, shape, dt))
    ps4 = lambda name, shape, dt: es4.enter_context(nc.psum_tensor(name, shape, dt))
    H = sb4("H", [128, 32 * NT], BF16)
    w1b = [sb4("w1b0", [128, 8 * 256], BF16), sb4("w1b1", [128, 8 * 256], BF16)]
    w2b = [sb4("w2b0", [128, 16 * 128], BF16), sb4("w2b1", [128, 16 * 128], BF16)]
    gbb = [sb4("gbb0", [128, NT], BF16), sb4("gbb1", [128, NT], BF16)]
    pH = [ps4("pH0", [128, 1024], F32), ps4("pH1", [128, 1024], F32)]
    pO2 = [ps4("pO20", [128, 512], F32), ps4("pO21", [128, 512], F32)]
    pGB = ps4("pGB", [128, NT], F32)

    P = Prog()
    for _eng in ENGINES:
        P.wait(_eng, sC1, ld_consts)
        P.wait(_eng, sC2, wl_consts)
    rW1 = Rot(2)
    rW2 = Rot(2)
    rGB = Rot(2)
    rPH = Rot(2)
    rPO2 = Rot(2)
    rGBpsum = Rot(1)

    w1_view = d_w1
    w2_view = d_w2
    ei = 0
    for nt in range(2):
        for e in range(E):
            gpar = ei % 2
            rGBpsum.acquire(P, "tensor", 0)
            for s in range(2):
                P.op("tensor", lambda Eo, p=pGB[:, s * 512:(s + 1) * 512],
                     sel=selb[:, e * 128:(e + 1) * 128],
                     g=gatesT[:, nt * NT + s * 512:nt * NT + (s + 1) * 512]:
                     Eo.matmul(p, sel, g, start=True, stop=True),
                     inc=(sPE if s == 1 else None))
            P.wait("vector", sPE)
            rGB.acquire(P, "vector", ei)
            P.op("vector", lambda Eo, o=gbb[gpar][:]: Eo.tensor_copy(o, pGB[:]), inc=sDV)
            rGBpsum.rel[0] = [(sDV, sDV.n)]
            gb_ready = sDV.n
            for fp in range(16):
                wpar = (ei * 16 + fp) % 2
                rW1.acquire(P, "gpsimd", ei * 16 + fp)
                src = w1_view[e, :, fp * 256:(fp + 1) * 256].rearrange(
                    "(kc p) f -> p kc f", p=128)
                dW1.load(P, "gpsimd", ei * 16 + fp,
                         lambda Eo, o=w1b[wpar][:].rearrange("p (kc f) -> p kc f", kc=8),
                         s=src: Eo.dma_start(out=o, in_=s))
                dW1.wait(P, "tensor", ei * 16 + fp)
                for fc in range(2):
                    fcg = fp * 2 + fc
                    hpar = fcg % 2
                    rPH.acquire(P, "tensor", fcg)
                    for nb in range(2):
                        for kc in range(8):
                            P.op("tensor", lambda Eo,
                                 p=pH[hpar][:, nb * 512:(nb + 1) * 512],
                                 w=w1b[wpar][:, kc * 256 + fc * 128:kc * 256 + (fc + 1) * 128],
                                 x=y1T[:, kc * TOK + nt * NT + nb * 512:
                                       kc * TOK + nt * NT + (nb + 1) * 512],
                                 st=(kc == 0), sp=(kc == 7):
                                 Eo.matmul(p, w, x, start=st, stop=sp),
                                 inc=(sPE if (nb == 1 and kc == 7) else None))
                    P.wait("scalar", sPE)
                    hsl = H[:, fcg * NT:(fcg + 1) * NT]
                    P.op("scalar", lambda Eo, o=hsl, p=pH[hpar][:],
                         b=b1t[:, e * 32 + fcg:e * 32 + fcg + 1]:
                         Eo.activation(o, p, AF.Gelu_apprx_tanh, bias=b, scale=f32(1.0)),
                         inc=sAC)
                    rPH.release(fcg, sAC)
                    P.wait("vector", sAC)
                    P.wait("vector", sDV, gb_ready)
                    P.op("vector", lambda Eo, h=hsl, g=gbb[gpar][:]:
                         Eo.tensor_tensor(h, h, g, op=ALU.mult), inc=sDV)
                rW1.release(ei * 16 + fp, sPE)
            h_gated = sDV.n
            rGB.release(ei, sDV)
            P.wait("tensor", sDV, h_gated)
            for dc in range(8):
                for fh in range(2):
                    w2i = (ei * 16 + dc * 2 + fh) % 2
                    rW2.acquire(P, "gpsimd", ei * 16 + dc * 2 + fh)
                    src = w2_view[e, fh * 2048:(fh + 1) * 2048,
                                  dc * 128:(dc + 1) * 128].rearrange(
                        "(fc p) d -> p fc d", p=128)
                    dW2.load(P, "gpsimd", ei * 16 + dc * 2 + fh,
                             lambda Eo, o=w2b[w2i][:].rearrange("p (fc d) -> p fc d", fc=16),
                             s=src: Eo.dma_start(out=o, in_=s))
                    dW2.wait(P, "tensor", ei * 16 + dc * 2 + fh)
                    for nb in range(2):
                        opar = nb
                        if fh == 0:
                            rPO2.acquire(P, "tensor", ei * 16 + dc * 2 + nb)
                        for fcL in range(16):
                            fcg = fh * 16 + fcL
                            if e == 0 and fh == 0 and fcL == 0:
                                P.op("tensor", lambda Eo, p=pO2[opar][:],
                                     b=b2b[:, dc * 128:(dc + 1) * 128],
                                     g=gatesT[:, nt * NT + nb * 512:nt * NT + (nb + 1) * 512]:
                                     Eo.matmul(p, b, g, start=True, stop=False))
                            P.op("tensor", lambda Eo, p=pO2[opar][:],
                                 w=w2b[w2i][:, fcL * 128:(fcL + 1) * 128],
                                 x=H[:, fcg * NT + nb * 512:fcg * NT + (nb + 1) * 512],
                                 st=(fh == 0 and fcL == 0 and e != 0),
                                 sp=(fh == 1 and fcL == 15):
                                 Eo.matmul(p, w, x, start=st, stop=sp),
                                 inc=(sPE if fcL == 15 else None))
                        if fh == 1:
                            P.wait("vector", sPE)
                            osl = O_acc[:, dc * TOK + nt * NT + nb * 512:
                                        dc * TOK + nt * NT + (nb + 1) * 512]
                            if e == 0:
                                P.op("vector", lambda Eo, o=osl, p=pO2[opar][:]:
                                     Eo.tensor_copy(o, p), inc=sDV)
                            else:
                                P.op("vector", lambda Eo, o=osl, p=pO2[opar][:]:
                                     Eo.tensor_add(o, o, p), inc=sDV)
                            rPO2.release(ei * 16 + dc * 2 + nb, sDV)
                    rW2.release(ei * 16 + dc * 2 + fh, sPE)
            ei += 1

    if "O_acc" in taps:
        P.wait("sync", sDV)
        P.op("sync", lambda Eo: Eo.dma_start(out=taps["O_acc"], in_=O_acc[:]),
             inc=sTAP, inc_n=16)
    P.wait("sync", sTAP)
    barrier(P)
    with nc.Block("b4") as blk:
        P.run(blk)
    es4.close()

    es5 = ExitStack()
    sb5 = lambda name, shape, dt: es5.enter_context(nc.sbuf_tensor(name, shape, dt))
    ps5 = lambda name, shape, dt: es5.enter_context(nc.psum_tensor(name, shape, dt))
    y1ch = [sb5("y1ch0", [128, C], F32), sb5("y1ch1", [128, C], F32)]
    z2ch = [sb5("z2ch0", [128, C], F32), sb5("z2ch1", [128, C], F32)]
    sq2 = [sb5("sq20", [128, C], F32), sb5("sq21", [128, C], F32)]
    mu2 = [sb5("mu20", [128, 1], F32), sb5("mu21", [128, 1], F32)]
    var2 = sb5("var2", [128, 1], F32)
    lnv2 = sb5("lnv2", [128, 1], F32)
    rstd2 = [sb5("rstd20", [128, 1], F32), sb5("rstd21", [128, 1], F32)]
    pTf2 = [ps5("pTf20", [128, 512], F32), ps5("pTf21", [128, 512], F32)]

    P = Prog()
    for _eng in ENGINES:
        P.wait(_eng, sC1, ld_consts)
        P.wait(_eng, sC2, wl_consts)
    rY1 = Rot(2)
    rZ2 = Rot(2)
    rSQ2 = Rot(2)
    rMU2 = Rot(2)
    rRS2 = Rot(2)
    rTF2 = Rot(2)
    P.wait("sync", dSPp.sems[0])
    P.wait("sync", dSPp.sems[1])
    for tc in range(NCH):
        par = tc % 2
        rY1.acquire(P, "sync", tc)
        dYC.load(P, "sync", tc, lambda Eo, o=y1ch[par][:], s=d_y1sp[tc * 128:(tc + 1) * 128, :]:
                 Eo.dma_start(out=o, in_=s))
        for g in range(2):
            tfp = pTf2[g]
            rTF2.acquire(P, "tensor", tc * 2 + g)
            for t in range(4):
                dc = g * 4 + t
                P.op("tensor", lambda Eo, o=tfp[:, t * 128:(t + 1) * 128],
                     a=O_acc[:, dc * TOK + tc * 128:dc * TOK + (tc + 1) * 128]:
                     Eo.transpose(o, a, idf[:]), inc=(sPE if t == 3 else None))
            P.wait("vector", sPE)
            dYC.wait(P, "vector", tc)
            rZ2.acquire(P, "vector", tc) if g == 0 else None
            P.op("vector", lambda Eo, o=z2ch[par][:, g * 512:(g + 1) * 512],
                 a=tfp[:], b=y1ch[par][:, g * 512:(g + 1) * 512]:
                 Eo.tensor_add(o, a, b), inc=sDV)
            rTF2.release(tc * 2 + g, sDV)
        rY1.release(tc, sDV)
        z2 = z2ch[par]
        rMU2.acquire(P, "vector", tc)
        mu = mu2[par]
        P.wait("vector", sDV)
        P.eop("vector", lambda Eo, m=mu[:], z_=z2[:]: Eo.reduce_sum(m, z_, axis=AX.X), sDV)
        P.eop("vector", lambda Eo, m=mu[:]:
              Eo.tensor_scalar(m, m, c_inv1024, None, op0=ALU.mult), sDV)
        P.wait("scalar", sDV)
        rSQ2.acquire(P, "scalar", tc)
        P.op("scalar", lambda Eo, q=sq2[par][:], z_=z2[:], m=mu[:]:
             Eo.activation(q, z_, AF.Square, bias=m, scale=f32(-1.0)), inc=sAC)
        P.wait("vector", sAC)
        P.eop("vector", lambda Eo, q=sq2[par][:]: Eo.reduce_sum(var2[:], q, axis=AX.X), sDV)
        P.op("vector", lambda Eo:
             Eo.tensor_scalar(var2[:], var2[:], c_inv1024, None, op0=ALU.mult), inc=sDV)
        rSQ2.rel[par] = [(sDV, sDV.n)]
        P.wait("scalar", sDV)
        rRS2.acquire(P, "scalar", tc)
        P.eop("scalar", lambda Eo: Eo.activation(lnv2[:], var2[:], AF.Ln, bias=f32(EPS)), sAC)
        P.op("scalar", lambda Eo, r=rstd2[par][:]:
             Eo.activation(r, lnv2[:], AF.Exp, scale=f32(-0.5)), inc=sAC)
        P.wait("vector", sAC)
        P.op("vector", lambda Eo, z_=z2[:], m=mu[:], r=rstd2[par][:]:
             Eo.tensor_scalar(z_, z_, m, r, op0=ALU.subtract, op1=ALU.mult), inc=sDV)
        rMU2.release(tc, sDV)
        rRS2.release(tc, sDV)
        P.wait("sync", sDV)
        dSTp.load(P, "sync", tc, lambda Eo, o=d_y2[tc * 128:(tc + 1) * 128, :], z_=z2[:]:
                  Eo.dma_start(out=o, in_=z_))
        rZ2.rel[par] = [dSTp.sem_count(tc)]
    P.wait("sync", dSTp.sems[0])
    P.wait("sync", dSTp.sems[1])
    P.wait("sync", sTAP)
    with nc.Block("b5") as blk:
        P.run(blk)
    es5.close()
    esO.close()
    esP.close()
    esS.close()
    return nc


_CACHE = {}


def kernel(L0_attn_x, L0_attn_Wq, L0_attn_bq, L0_attn_Wk, L0_attn_bk,
           L0_attn_Wv, L0_attn_bv, L0_ffn_router_w, L0_ffn_w1, L0_ffn_b1,
           L0_ffn_w2, L0_ffn_b2):
    x = np.ascontiguousarray(np.asarray(L0_attn_x, dtype=np.float32))
    Wq = np.asarray(L0_attn_Wq, np.float32)
    Wk = np.asarray(L0_attn_Wk, np.float32)
    Wv = np.asarray(L0_attn_Wv, np.float32)
    bq = np.asarray(L0_attn_bq, np.float32)
    bk = np.asarray(L0_attn_bk, np.float32)
    bv = np.asarray(L0_attn_bv, np.float32)
    rw_in = np.asarray(L0_ffn_router_w, np.float32)
    w1 = np.ascontiguousarray(np.asarray(L0_ffn_w1, np.float32))
    b1 = np.asarray(L0_ffn_b1, np.float32)
    w2 = np.ascontiguousarray(np.asarray(L0_ffn_w2, np.float32))
    b2 = np.asarray(L0_ffn_b2, np.float32)

    wqkvT = np.concatenate([Wq.T, Wk.T, Wv.T], axis=1)
    wqkvT = np.ascontiguousarray(wqkvT.reshape(8, 128, 192).transpose(1, 0, 2))
    qkb = np.zeros((128, 2), np.float32)
    qkb[0:64, 0] = bq
    qkb[0:64, 1] = bk
    bvrow = bv.reshape(1, HS)
    b1t = np.ascontiguousarray(
        b1.reshape(E, 32, 128).transpose(2, 0, 1))
    rw_pre = np.ascontiguousarray(rw_in.reshape(8, 128, E).transpose(1, 0, 2))
    identb = np.eye(128, dtype=bf16)
    identf = np.eye(128, dtype=np.float32)
    qpiota = (np.arange(128, dtype=np.float32)[:, None]
              - np.arange(512, dtype=np.float32)[None, :])
    onesb = np.ones((1, 128), dtype=bf16)
    consts = np.zeros((128, 8), np.float32)
    consts[:, C_INV1024] = 1.0 / 1024.0
    consts[:, C_NEG132] = -1.0 / 32.0
    consts[:, C_ONE] = 1.0
    consts[:, C_NEG1E30] = -1e30
    selb = np.zeros((E, E * 128), dtype=bf16)
    for e in range(E):
        selb[e, e * 128:(e + 1) * 128] = 1

    nc = build_nc()
    in_maps = []
    for c in range(NCORES):
        b = c // 4
        ph = c % 4
        chunks = [4 * j + ph for j in range(NCH)]
        x_b = x[b]
        x_own = np.ascontiguousarray(
            x_b.reshape(64, 128, C)[chunks].reshape(TOK, C))
        qoffb = np.full((128, 1), 128.0 * ph, np.float32)
        in_maps.append(dict(
            x_own=x_own, x_batch=x_b, wqkvT=wqkvT, qkb=qkb, bvrow=bvrow,
            b1t=b1t, b2w=b2, rw=rw_pre, w1=w1, w2=w2, identb=identb,
            identf=identf, qpiota=qpiota, onesb=onesb, qoffb=qoffb,
            consts=consts, selb=selb))
    res = run_bass_kernel_spmd(nc, in_maps, core_ids=list(range(NCORES)))
    out = np.zeros((B, T, C), np.float32)
    for c in range(NCORES):
        b = c // 4
        ph = c % 4
        y2 = np.asarray(res.results[c]["y2"]).reshape(NCH, 128, C)
        for j in range(NCH):
            out[b, (4 * j + ph) * 128:(4 * j + ph + 1) * 128, :] = y2[j]
    return out


# version 05
# speedup vs baseline: 1.1146x; 1.0384x over previous
import numpy as np
import ml_dtypes
from contextlib import ExitStack

import concourse.bass as bass
import concourse.mybir as mybir
from concourse.bass_utils import run_bass_kernel_spmd

F32 = mybir.dt.float32
BF16 = mybir.dt.bfloat16
AF = mybir.ActivationFunctionType
ALU = mybir.AluOpType
AX = mybir.AxisListType
f32 = np.float32
bf16 = ml_dtypes.bfloat16

B, T, C = 2, 8192, 1024
HS = 64
E, FF = 8, 4096
NCORES = 8
TOK = 2048
NCH = 16
KCH = 64
NT = 1024
EPS = 1e-5

C_INV1024, C_NEG132, C_ONE, C_NEG1E30, C_ZERO = 0, 1, 2, 3, 4

ENGINES = ("sync", "scalar", "vector", "tensor", "gpsimd")


class Sem:
    def __init__(self, nc, es, name):
        self.h = es.enter_context(nc.semaphore(name))
        self.n = 0

    def bump(self, k):
        self.n += k
        return self.n


class Prog:

    def __init__(self):
        self.streams = {e: [] for e in ENGINES}

    def op(self, eng, fn, inc=None, inc_n=1):
        if inc is not None:
            inc.bump(inc_n)
            self.streams[eng].append(
                lambda Eo, fn=fn, s=inc.h, n=inc_n: fn(Eo).then_inc(s, n))
        else:
            self.streams[eng].append(lambda Eo, fn=fn: fn(Eo))

    def eop(self, eng, fn, sem):
        self.op(eng, fn, inc=sem)
        self.wait(eng, sem)

    def wait(self, eng, sem, val=None):
        v = sem.n if val is None else val
        if v > 0:
            self.streams[eng].append(lambda Eo, s=sem.h, v=v: Eo.wait_ge(s, v))

    def run(self, block):
        for eng in ENGINES:
            fns = tuple(self.streams[eng])

            def body(Eo, fns=fns):
                for f in fns:
                    f(Eo)

            getattr(block, eng)(body)


class Rot:

    def __init__(self, n):
        self.n = n
        self.rel = [[] for _ in range(n)]

    def acquire(self, P, eng, i):
        for sem, cnt in self.rel[i % self.n]:
            P.wait(eng, sem, cnt)
        self.rel[i % self.n] = []

    def release(self, i, sem):
        self.rel[i % self.n].append((sem, sem.n))


class DmaSlots:

    def __init__(self, nc, es, name, n=2):
        self.sems = [Sem(nc, es, "%s%d" % (name, i)) for i in range(n)]
        self.n = n

    def load(self, P, eng, i, fn):
        P.op(eng, fn, inc=self.sems[i % self.n], inc_n=16)

    def wait(self, P, eng, i):
        P.wait(eng, self.sems[i % self.n])

    def sem_count(self, i):
        s = self.sems[i % self.n]
        return (s, s.n)


def build_nc(debug_taps=()):
    nc = bass.Bass("TRN2", target_bir_lowering=False)

    d_xown = nc.dram_tensor("x_own", [TOK, C], F32, kind="ExternalInput").ap()
    d_xbat = nc.dram_tensor("x_batch", [T, C], F32, kind="ExternalInput").ap()
    d_wqkv = nc.dram_tensor("wqkvT", [128, 8, 192], F32, kind="ExternalInput").ap()
    d_qkb = nc.dram_tensor("qkb", [128, 2], F32, kind="ExternalInput").ap()
    d_bvr = nc.dram_tensor("bvrow", [1, HS], F32, kind="ExternalInput").ap()
    d_b1t = nc.dram_tensor("b1t", [128, E, 32], F32, kind="ExternalInput").ap()
    d_b2w = nc.dram_tensor("b2w", [E, C], F32, kind="ExternalInput").ap()
    d_rw = nc.dram_tensor("rw", [128, 8, E], F32, kind="ExternalInput").ap()
    d_w1 = nc.dram_tensor("w1", [E, C, FF], F32, kind="ExternalInput").ap()
    d_w2 = nc.dram_tensor("w2", [E, FF, C], F32, kind="ExternalInput").ap()
    d_idb = nc.dram_tensor("identb", [128, 128], BF16, kind="ExternalInput").ap()
    d_idf = nc.dram_tensor("identf", [128, 128], F32, kind="ExternalInput").ap()
    d_qpi = nc.dram_tensor("qpiota", [128, 512], F32, kind="ExternalInput").ap()
    d_ones = nc.dram_tensor("onesb", [1, 128], BF16, kind="ExternalInput").ap()
    d_qoff = nc.dram_tensor("qoffb", [128, 1], F32, kind="ExternalInput").ap()
    d_cst = nc.dram_tensor("consts", [128, 8], F32, kind="ExternalInput").ap()
    d_sel = nc.dram_tensor("selb", [E, E * 128], BF16, kind="ExternalInput").ap()
    d_y1sp = nc.dram_tensor("y1sp", [TOK, C], F32, kind="Internal").ap()
    d_y2 = nc.dram_tensor("y2", [TOK, C], F32, kind="ExternalOutput").ap()
    taps = {}
    for name, shape in debug_taps:
        taps[name] = nc.dram_tensor("tap_" + name, shape, F32, kind="ExternalOutput").ap()

    esP = ExitStack()
    sb = lambda name, shape, dt: esP.enter_context(nc.sbuf_tensor(name, shape, dt))

    idb = sb("idb", [128, 128], BF16)
    idf = sb("idf", [128, 128], F32)
    onesb = sb("onesb_s", [1, 128], BF16)
    qoffb = sb("qoffb_s", [128, 1], F32)
    cst = sb("cst", [128, 8], F32)
    b1t = sb("b1t_s", [128, E * 32], F32)
    b2b = sb("b2b", [E, C], BF16)
    rw = sb("rw_s", [128, 8 * E], F32)
    y1T = sb("y1T", [128, 8 * TOK], BF16)
    gatesT = sb("gatesT", [E, TOK], BF16)
    selb = sb("selb_s", [E, E * 128], BF16)

    esA = ExitStack()
    sbA = lambda name, shape, dt: esA.enter_context(nc.sbuf_tensor(name, shape, dt))
    qpi = sbA("qpi", [128, 512], F32)
    maskadd = sbA("maskadd", [128, 512], F32)
    qT = sbA("qT", [64, TOK], BF16)
    kT = sbA("kT", [64, T], BF16)
    vtm = sbA("vtm", [128, KCH * HS], BF16)
    o_tm = sbA("o_tm", [128, NCH * HS], F32)
    mA = sbA("mA", [128, 1], F32)
    mB = sbA("mB", [128, 1], F32)
    dm = sbA("dm", [128, 1], F32)
    edm = sbA("edm", [128, 1], F32)
    denA = sbA("denA", [128, 1], F32)
    denB = sbA("denB", [128, 1], F32)
    negm32 = sbA("negm32", [128, 1], F32)
    lden = sbA("lden", [128, 1], F32)
    bias2 = sbA("bias2", [128, 1], F32)
    bsum = [sbA("bsum0", [128, 1], F32), sbA("bsum1", [128, 1], F32)]

    esS = ExitStack()
    mks = lambda name: Sem(nc, esS, name)
    sC1 = mks("sC1")
    sC2 = mks("sC2")
    sTAP = mks("sTAP")
    sPE = mks("sPE")
    sAC = mks("sAC")
    sDV = mks("sDV")
    sBAR = mks("sBAR")
    dXB = DmaSlots(nc, esS, "dXB")
    dZC = DmaSlots(nc, esS, "dZC")
    dSPp = DmaSlots(nc, esS, "dSPp")
    dW1 = DmaSlots(nc, esS, "dW1")
    dW2 = DmaSlots(nc, esS, "dW2", n=4)
    dYC = DmaSlots(nc, esS, "dYC")
    dSTp = DmaSlots(nc, esS, "dSTp")

    c_inv1024 = cst[:, C_INV1024:C_INV1024 + 1]
    c_neg132 = cst[:, C_NEG132:C_NEG132 + 1]
    c_one = cst[:, C_ONE:C_ONE + 1]
    c_neg1e30 = cst[:, C_NEG1E30:C_NEG1E30 + 1]
    c_zero = cst[:, C_ZERO:C_ZERO + 1]

    barrier_count = [0]

    def barrier(P):
        barrier_count[0] += 1
        for eng in ENGINES:
            P.op(eng, lambda Eo: Eo.nop(), inc=sBAR)
        for eng in ENGINES:
            P.wait(eng, sBAR, 5 * barrier_count[0])

    es1 = ExitStack()
    sb1 = lambda name, shape, dt: es1.enter_context(nc.sbuf_tensor(name, shape, dt))
    ps1 = lambda name, shape, dt: es1.enter_context(nc.psum_tensor(name, shape, dt))
    xb = [sb1("xb0", [128, C], F32), sb1("xb1", [128, C], F32)]
    xT = [sb1("xT0", [128, C], BF16), sb1("xT1", [128, C], BF16)]
    wqkv = sb1("wqkv", [128, 8 * 192], BF16)
    bvr = sb1("bvr", [1, HS], BF16)
    qkb = sb1("qkb_s", [128, 2], F32)
    pXT = [ps1("pXT0", [128, C], F32), ps1("pXT1", [128, C], F32)]
    pQK = [ps1("pQK0", [64, 128], F32), ps1("pQK1", [64, 128], F32)]
    pV = [ps1("pV0", [128, HS], F32), ps1("pV1", [128, HS], F32)]

    P = Prog()
    for dst, src in [(idb, d_idb), (idf, d_idf), (qpi, d_qpi), (qoffb, d_qoff),
                     (cst, d_cst), (qkb[:], d_qkb),
                     (b1t, d_b1t.rearrange("p e k -> p (e k)")),
                     (rw, d_rw.rearrange("p c e -> p (c e)"))]:
        ap = dst[:] if hasattr(dst, "ap") else dst
        P.op("sync", lambda Eo, a=ap, s=src: Eo.dma_start(out=a, in_=s), inc=sC1, inc_n=16)
    for dst, src in [(onesb[:], d_ones), (selb[:], d_sel),
                     (wqkv[:], d_wqkv.rearrange("p c h -> p (c h)")),
                     (bvr[:], d_bvr), (b2b[:], d_b2w)]:
        P.op("gpsimd", lambda Eo, a=dst, s=src: Eo.dma_start(out=a, in_=s), inc=sC2, inc_n=16)
    ld_consts = sC1.n
    wl_consts = sC2.n
    for _eng in ("tensor", "scalar", "vector", "gpsimd"):
        P.wait(_eng, sC1, ld_consts)
        P.wait(_eng, sC2, wl_consts)
    P.wait("vector", sC1, ld_consts)
    P.eop("vector", lambda Eo: Eo.tensor_scalar(maskadd[:], qpi[:], qoffb[:, 0:1], None, op0=ALU.add), sDV)
    P.eop("vector", lambda Eo: Eo.tensor_scalar(maskadd[:], maskadd[:], c_zero, None, op0=ALU.is_lt), sDV)
    P.op("vector", lambda Eo: Eo.tensor_scalar(maskadd[:], maskadd[:], c_neg1e30, None, op0=ALU.mult), inc=sDV)

    rXB = Rot(2)
    rXT = Rot(2)
    rQK = Rot(2)
    rV = Rot(2)

    rPXT = Rot(2)

    def chunk_setup(i, src_ap, is_q, kc):
        pi = i % 2
        rXB.acquire(P, "sync", i)
        dXB.load(P, "sync", i, lambda Eo, a=xb[pi][:], s=src_ap: Eo.dma_start(out=a, in_=s))
        dXB.wait(P, "tensor", i)
        rPXT.acquire(P, "tensor", i)
        rXT_psum = pXT[pi]
        for dc in range(8):
            P.op("tensor", lambda Eo, o=rXT_psum[:, dc * 128:(dc + 1) * 128],
                 a=xb[pi][:, dc * 128:(dc + 1) * 128]: Eo.transpose(o, a, idf[:]),
                 inc=(sPE if dc == 7 else None))
        transposes_done = sPE.n
        P.wait("vector", sPE)
        rXT.acquire(P, "vector", i)
        P.op("vector", lambda Eo, o=xT[pi][:], a=rXT_psum[:]: Eo.tensor_copy(o, a), inc=sDV)
        rPXT.release(i, sDV)
        evac_cnt = sDV.n
        P.wait("tensor", sDV, evac_cnt)
        if is_q:
            rQK.acquire(P, "tensor", i)
            for dc in range(8):
                P.op("tensor", lambda Eo, p=pQK[pi][:],
                     w=wqkv[:, dc * 192:dc * 192 + 64],
                     x=xT[pi][:, dc * 128:(dc + 1) * 128], st=(dc == 0), sp=(dc == 7):
                     Eo.matmul(p, w, x, start=st, stop=sp),
                     inc=(sPE if dc == 7 else None))
            P.wait("scalar", sPE)
            P.op("scalar", lambda Eo, o=qT[:, i * 128:(i + 1) * 128], p=pQK[pi][:]:
                 Eo.activation(o, p, AF.Identity, bias=qkb[0:64, 0:1], scale=f32(1.0)),
                 inc=sAC)
            rQK.release(i, sAC)
        else:
            rQK.acquire(P, "tensor", i)
            for dc in range(8):
                P.op("tensor", lambda Eo, p=pQK[pi][:],
                     w=wqkv[:, dc * 192 + 64:dc * 192 + 128],
                     x=xT[pi][:, dc * 128:(dc + 1) * 128], st=(dc == 0), sp=(dc == 7):
                     Eo.matmul(p, w, x, start=st, stop=sp),
                     inc=(sPE if dc == 7 else None))
            P.wait("scalar", sPE)
            P.op("scalar", lambda Eo, o=kT[:, kc * 128:(kc + 1) * 128], p=pQK[pi][:]:
                 Eo.activation(o, p, AF.Identity, bias=qkb[0:64, 1:2], scale=f32(1.0)),
                 inc=sAC)
            rQK.release(i, sAC)
            rV.acquire(P, "tensor", i)
            for dc in range(8):
                P.op("tensor", lambda Eo, p=pV[pi][:],
                     x=xT[pi][:, dc * 128:(dc + 1) * 128],
                     w=wqkv[:, dc * 192 + 128:dc * 192 + 192], st=(dc == 0):
                     Eo.matmul(p, x, w, start=st, stop=False))
            P.op("tensor", lambda Eo, p=pV[pi][:]:
                 Eo.matmul(p, onesb[:], bvr[:], start=False, stop=True), inc=sPE)
            P.wait("vector", sPE)
            P.op("vector", lambda Eo, o=vtm[:, kc * HS:(kc + 1) * HS], p=pV[pi][:]:
                 Eo.tensor_copy(o, p), inc=sDV)
            rV.release(i, sDV)
        rXB.rel[pi] = [(sPE, transposes_done)]
        rXT.rel[pi] = [(sPE, sPE.n)]

    for i in range(NCH):
        chunk_setup(i, d_xown[i * 128:(i + 1) * 128, :], True, None)
    for kc in range(KCH):
        chunk_setup(NCH + kc, d_xbat[kc * 128:(kc + 1) * 128, :], False, kc)

    P.wait("sync", sTAP)
    barrier(P)
    with nc.Block("b1") as blk:
        P.run(blk)
    es1.close()

    es2 = ExitStack()
    sb2 = lambda name, shape, dt: es2.enter_context(nc.sbuf_tensor(name, shape, dt))
    ps2 = lambda name, shape, dt: es2.enter_context(nc.psum_tensor(name, shape, dt))
    Psb = [sb2("Psb0", [128, T], BF16), sb2("Psb1", [128, T], BF16)]
    PT = [sb2("PT0", [128, 512], BF16), sb2("PT1", [128, 512], BF16)]
    ots = sb2("ots", [64, 128], F32)
    mtab = [sb2("mtab0", [128, NCH], F32), sb2("mtab1", [128, NCH], F32)]
    dentab = [sb2("dentab0", [128, NCH], F32), sb2("dentab1", [128, NCH], F32)]
    negm32 = [sb2("negm32a", [128, 1], F32), sb2("negm32b", [128, 1], F32)]
    recipd = [sb2("recipd0", [128, 1], F32), sb2("recipd1", [128, 1], F32)]
    mfin = sb2("mfin", [128, 1], F32)
    denf = sb2("denf", [128, 1], F32)
    pSA = [ps2("pSA0", [128, 512], F32), ps2("pSA1", [128, 512], F32)]
    pSB = [ps2("pSB0", [128, 512], F32), ps2("pSB1", [128, 512], F32)]
    pT = ps2("pT", [128, 512], BF16)
    pO = ps2("pO", [64, 128], F32)
    pOt = ps2("pOt", [128, HS], F32)

    P = Prog()
    for _eng in ENGINES:
        P.wait(_eng, sC1, ld_consts)
        P.wait(_eng, sC2, wl_consts)
    rSA = Rot(2)
    rSB = Rot(2)
    rPT = Rot(2)
    rPT_psum = Rot(1)
    rPO = Rot(1)
    rPstrip = Rot(2)
    rMT = Rot(2)
    rNM = Rot(2)
    rDT = Rot(2)
    rRD = Rot(2)

    def pass_a(j):
        par2 = j % 2
        rMT.acquire(P, "vector", j)
        for kb in range(j + 1):
            par = kb % 2
            rSA.acquire(P, "tensor", kb)
            P.op("tensor", lambda Eo, p=pSA[par][:],
                 q=qT[:, j * 128:(j + 1) * 128],
                 k=kT[:, kb * 512:(kb + 1) * 512]:
                 Eo.matmul(p, q, k, start=True, stop=True), inc=sPE)
            P.wait("vector", sPE)
            if kb == j:
                P.eop("vector", lambda Eo, p=pSA[par][:]:
                      Eo.tensor_add(p, p, maskadd[:]), sDV)
            P.op("vector", lambda Eo, o=mtab[par2][:, kb:kb + 1], p=pSA[par][:]:
                 Eo.reduce_max(o, p, axis=AX.X), inc=sDV)
            rSA.release(kb, sDV)
        P.wait("vector", sDV)
        P.eop("vector", lambda Eo, mt=mtab[par2][:, 0:j + 1]:
              Eo.reduce_max(mfin[:], mt, axis=AX.X), sDV)
        rNM.acquire(P, "vector", j)
        P.op("vector", lambda Eo, o=negm32[par2][:]:
             Eo.tensor_scalar(o, mfin[:], c_neg132, None, op0=ALU.mult), inc=sDV)
        nm_done[j] = sDV.n

    def pass_b(j):
        par2 = j % 2
        rPstrip.acquire(P, "scalar", j)
        P.wait("scalar", sDV, nm_done[j])
        rDT.acquire(P, "scalar", j)
        def b_mm(kb):
            par = kb % 2
            rSB.acquire(P, "tensor", kb)
            P.op("tensor", lambda Eo, p=pSB[par][:],
                 q=qT[:, j * 128:(j + 1) * 128],
                 k=kT[:, kb * 512:(kb + 1) * 512]:
                 Eo.matmul(p, q, k, start=True, stop=True), inc=sPE)
            return sPE.n

        mm_done = {0: b_mm(0)}
        for kb in range(j + 1):
            par = kb % 2
            if kb + 1 <= j:
                mm_done[kb + 1] = b_mm(kb + 1)
            if kb == j:
                P.wait("vector", sPE, mm_done[kb])
                P.op("vector", lambda Eo, p=pSB[par][:]:
                     Eo.tensor_add(p, p, maskadd[:]), inc=sDV)
                P.wait("scalar", sDV)
            else:
                P.wait("scalar", sPE, mm_done[kb])
            P.op("scalar", lambda Eo, o=Psb[par2][:, kb * 512:(kb + 1) * 512], p=pSB[par][:],
                 nm=negm32[par2][:], acc=dentab[par2][:, kb:kb + 1]:
                 Eo.activation(o, p, AF.Exp, bias=nm, scale=f32(1.0 / 32.0),
                               accum_out=acc), inc=sAC)
            rSB.release(kb, sAC)
            P.wait("tensor", sAC)
            rPT_psum.acquire(P, "tensor", 0)
            for s in range(4):
                P.op("tensor", lambda Eo, o=pT[:, s * 128:(s + 1) * 128],
                     a=Psb[par2][:, kb * 512 + s * 128:kb * 512 + (s + 1) * 128]:
                     Eo.transpose(o, a, idb[:]), inc=(sPE if s == 3 else None))
            P.wait("vector", sPE)
            rPT.acquire(P, "vector", kb)
            P.op("vector", lambda Eo, o=PT[par][:]: Eo.tensor_copy(o, pT[:]), inc=sDV)
            rPT_psum.rel[0] = [(sDV, sDV.n)]
            P.wait("tensor", sDV)
            if kb == 0:
                rPO.acquire(P, "tensor", 0)
            for s in range(4):
                P.op("tensor", lambda Eo, v=vtm[:, (kb * 4 + s) * HS:(kb * 4 + s + 1) * HS],
                     pp=PT[par][:, s * 128:(s + 1) * 128],
                     st=(kb == 0 and s == 0), sp=(kb == j and s == 3):
                     Eo.matmul(pO[:], v, pp, start=st, stop=sp),
                     inc=(sPE if s == 3 else None))
            rPT.release(kb, sPE)
        rPstrip.rel[par2] = [(sPE, sPE.n)]
        rNM.rel[par2] = [(sAC, sAC.n)]
        P.wait("vector", sAC)
        P.wait("vector", sDV)
        P.eop("vector", lambda Eo, dt_=dentab[par2][:, 0:j + 1]:
              Eo.reduce_sum(denf[:], dt_, axis=AX.X), sDV)
        rRD.acquire(P, "vector", j)
        P.eop("vector", lambda Eo, o=recipd[par2][:]:
              Eo.reciprocal(o, denf[:]), sDV)
        rDT.rel[par2] = [(sDV, sDV.n)]
        P.wait("scalar", sPE)
        P.op("scalar", lambda Eo: Eo.activation(ots[:], pO[:], AF.Copy), inc=sAC)
        rPO.rel[0] = [(sAC, sAC.n)]
        P.wait("tensor", sAC)
        P.op("tensor", lambda Eo: Eo.transpose(pOt[:], ots[:], idf[0:64, 0:64]), inc=sPE)
        P.wait("vector", sPE)
        P.op("vector", lambda Eo, o=o_tm[:, j * HS:(j + 1) * HS], r=recipd[par2][:]:
             Eo.tensor_scalar(o, pOt[:], r[:, 0:1], None, op0=ALU.mult), inc=sDV)
        rRD.rel[par2] = [(sDV, sDV.n)]
        P.wait("tensor", sDV)
        P.wait("scalar", sDV)

    nm_done = {}
    for j in range(NCH):
        pass_a(j)
        if j > 0:
            pass_b(j - 1)
    pass_b(NCH - 1)

    if "o_tm" in taps:
        P.wait("sync", sDV)
        P.op("sync", lambda Eo: Eo.dma_start(out=taps["o_tm"], in_=o_tm[:]),
             inc=sTAP, inc_n=16)
    if "qT" in taps:
        P.wait("gpsimd", sDV)
        P.op("gpsimd", lambda Eo: Eo.dma_start(out=taps["qT"], in_=qT[:]),
             inc=sTAP, inc_n=16)
    if "kT" in taps:
        P.wait("gpsimd", sDV)
        P.op("gpsimd", lambda Eo: Eo.dma_start(out=taps["kT"], in_=kT[:]),
             inc=sTAP, inc_n=16)
    if "vtm" in taps:
        P.wait("gpsimd", sDV)
        P.op("gpsimd", lambda Eo: Eo.dma_start(out=taps["vtm"], in_=vtm[:]),
             inc=sTAP, inc_n=16)
    P.wait("sync", sTAP)
    P.wait("gpsimd", sTAP)
    barrier(P)
    with nc.Block("b2") as blk:
        P.run(blk)
    es2.close()

    es3 = ExitStack()
    sb3 = lambda name, shape, dt: es3.enter_context(nc.sbuf_tensor(name, shape, dt))
    ps3 = lambda name, shape, dt: es3.enter_context(nc.psum_tensor(name, shape, dt))
    zch = [sb3("zch0", [128, C], F32), sb3("zch1", [128, C], F32)]
    sqb = [sb3("sqb0", [128, C], F32), sb3("sqb1", [128, C], F32)]
    tmp4 = [sb3("tmp40", [128, 512], F32), sb3("tmp41", [128, 512], F32)]
    logits = sb3("logits", [E, TOK], F32)
    gtm = sb3("gtm", [128, 8], F32)
    gates_tm = sb3("gates_tm", [128, 128], F32)
    muv = [sb3("mu0", [128, 1], F32), sb3("mu1", [128, 1], F32)]
    varv_ = [sb3("varv0", [128, 1], F32), sb3("varv1", [128, 1], F32)]
    lnvv_ = [sb3("lnvv0", [128, 1], F32), sb3("lnvv1", [128, 1], F32)]
    rstdv = [sb3("rstd0", [128, 1], F32), sb3("rstd1", [128, 1], F32)]
    eq1 = sb3("eq1", [128, 8], F32)
    eq2 = sb3("eq2", [128, 8], F32)
    mskl = sb3("mskl", [128, 8], F32)
    m1v = sb3("m1v", [128, 1], F32)
    m2v = sb3("m2v", [128, 1], F32)
    edv = sb3("edv", [128, 1], F32)
    edv2 = sb3("edv2", [128, 1], F32)
    s1v = sb3("s1v", [128, 1], F32)
    rv = sb3("rv", [128, 1], F32)
    w2v = sb3("w2v", [128, 1], F32)
    g1v = sb3("g1v", [128, 8], F32)
    pTf = [ps3("pTf0", [128, 512], F32), ps3("pTf1", [128, 512], F32)]
    pL = ps3("pL", [E, TOK], F32)
    pG8 = ps3("pG8", [128, 16], F32)
    pTg = ps3("pTg", [E, 256], F32)

    P = Prog()
    for _eng in ENGINES:
        P.wait(_eng, sC1, ld_consts)
        P.wait(_eng, sC2, wl_consts)
    rZ = Rot(2)
    rSQ = Rot(2)
    rMU = Rot(2)
    rRS = Rot(2)
    rTF = Rot(2)
    rT4 = Rot(2)

    for tc in range(NCH):
        par = tc % 2
        z = zch[par]
        rZ.acquire(P, "sync", tc)
        dZC.load(P, "sync", tc, lambda Eo, o=z[:], s=d_xown[tc * 128:(tc + 1) * 128, :]:
                 Eo.dma_start(out=o, in_=s))
        dZC.wait(P, "vector", tc)
        orep = o_tm[:, tc * HS:(tc + 1) * HS].unsqueeze(1).broadcast_to((128, 16, HS))
        P.eop("vector", lambda Eo, z_=z[:], r=orep:
              Eo.tensor_add(z_.rearrange("p (r h) -> p r h", r=16), z_.rearrange("p (r h) -> p r h", r=16), r), sDV)
        rMU.acquire(P, "vector", tc)
        mu = muv[par]
        P.eop("vector", lambda Eo, z_=z[:], m=mu[:]: Eo.reduce_sum(m, z_, axis=AX.X), sDV)
        P.eop("vector", lambda Eo, m=mu[:]:
              Eo.tensor_scalar(m, m, c_inv1024, None, op0=ALU.mult), sDV)
        P.wait("scalar", sDV)
        rSQ.acquire(P, "scalar", tc)
        P.op("scalar", lambda Eo, z_=z[:], m=mu[:], q=sqb[par][:]:
             Eo.activation(q, z_, AF.Square, bias=m, scale=f32(-1.0)), inc=sAC)
        P.wait("vector", sAC)
        varv = varv_[par]
        lnvv = lnvv_[par]
        P.eop("vector", lambda Eo, vv=varv[:], q=sqb[par][:]: Eo.reduce_sum(vv, q, axis=AX.X), sDV)
        P.op("vector", lambda Eo, vv=varv[:]:
             Eo.tensor_scalar(vv, vv, c_inv1024, None, op0=ALU.mult), inc=sDV)
        rSQ.rel[par] = [(sDV, sDV.n)]
        P.wait("scalar", sDV)
        rRS.acquire(P, "scalar", tc)
        rstd = rstdv[par]
        P.eop("scalar", lambda Eo, lv=lnvv[:], vv=varv[:]: Eo.activation(lv, vv, AF.Ln, bias=f32(EPS)), sAC)
        P.op("scalar", lambda Eo, r=rstd[:], lv=lnvv[:]:
             Eo.activation(r, lv, AF.Exp, scale=f32(-0.5)), inc=sAC)
        P.wait("vector", sAC)
        P.op("vector", lambda Eo, z_=z[:], m=mu[:], r=rstd[:]:
             Eo.tensor_scalar(z_, z_, m, r, op0=ALU.subtract, op1=ALU.mult), inc=sDV)
        rMU.release(tc, sDV)
        rRS.release(tc, sDV)
        y1_done = sDV.n
        P.wait("sync", sDV, y1_done)
        dSPp.load(P, "sync", tc, lambda Eo, o=d_y1sp[tc * 128:(tc + 1) * 128, :], z_=z[:]:
                  Eo.dma_start(out=o, in_=z_))
        P.wait("tensor", sDV, y1_done)
        for g in range(2):
            tfp = pTf[g]
            rTF.acquire(P, "tensor", tc * 2 + g)
            for t in range(4):
                dc = g * 4 + t
                P.op("tensor", lambda Eo, o=tfp[:, t * 128:(t + 1) * 128],
                     a=z[:, dc * 128:(dc + 1) * 128]: Eo.transpose(o, a, idf[:]),
                     inc=(sPE if t == 3 else None))
            P.wait("vector", sPE)
            rT4.acquire(P, "vector", tc * 2 + g)
            P.op("vector", lambda Eo, o=tmp4[g][:], a=tfp[:]: Eo.tensor_copy(o, a), inc=sDV)
            P.wait("scalar", sDV)
            y1T_view = y1T[:].rearrange("p (d t) -> p d t", d=8)[:, g * 4:(g + 1) * 4,
                                                               tc * 128:(tc + 1) * 128]
            P.op("scalar", lambda Eo, o=y1T_view, a=tfp[:].rearrange("p (d t) -> p d t", d=4):
                 Eo.activation(o, a, AF.Copy), inc=sAC)
            rTF.release(tc * 2 + g, sAC)
            rTF.rel[(tc * 2 + g) % 2].append((sDV, sDV.n))
            P.wait("tensor", sDV)
            for t in range(4):
                dc = g * 4 + t
                P.op("tensor", lambda Eo, p=pL[:, tc * 128:(tc + 1) * 128],
                     w=rw[:, dc * 8:(dc + 1) * 8],
                     x=tmp4[g][:, t * 128:(t + 1) * 128], st=(dc == 0), sp=(dc == 7):
                     Eo.matmul(p, w, x, start=st, stop=sp),
                     inc=(sPE if t == 3 else None))
            rT4.release(tc * 2 + g, sPE)
        rZ.release(tc, sPE)
        rZ.rel[par].append(dSPp.sem_count(tc))
    P.wait("scalar", sPE)
    P.op("scalar", lambda Eo: Eo.activation(logits[:], pL[:], AF.Copy), inc=sAC)
    logits_done = sAC.n
    gate_dve_done = {}
    for tc in range(NCH):
        sl = tc % 2
        P.wait("tensor", sAC)
        if tc >= 2:
            P.wait("tensor", sDV, gate_dve_done[sl])
        P.op("tensor", lambda Eo, o=pG8[:, sl * 8:(sl + 1) * 8],
             a=logits[:, tc * 128:(tc + 1) * 128]:
             Eo.transpose(o, a, idf[0:8, 0:8]), inc=sPE)
        P.wait("vector", sPE)
        P.eop("vector", lambda Eo, p=pG8[:, sl * 8:(sl + 1) * 8]: Eo.tensor_copy(gtm[:], p), sDV)
        P.eop("vector", lambda Eo: Eo.reduce_max(m1v[:], gtm[:], axis=AX.X), sDV)
        P.eop("vector", lambda Eo: Eo.tensor_scalar(eq1[:], gtm[:], m1v[:, 0:1], None, op0=ALU.is_equal), sDV)
        P.eop("vector", lambda Eo: Eo.scalar_tensor_tensor(mskl[:], eq1[:], c_neg1e30, gtm[:],
                                                           op0=ALU.mult, op1=ALU.add), sDV)
        P.eop("vector", lambda Eo: Eo.reduce_max(m2v[:], mskl[:], axis=AX.X), sDV)
        P.eop("vector", lambda Eo: Eo.tensor_scalar(eq2[:], mskl[:], m2v[:, 0:1], None, op0=ALU.is_equal), sDV)
        P.op("vector", lambda Eo: Eo.tensor_tensor(edv2[:], m2v[:], m1v[:], op=ALU.subtract), inc=sDV)
        P.wait("scalar", sDV)
        P.op("scalar", lambda Eo: Eo.activation(edv[:], edv2[:], AF.Exp), inc=sAC)
        P.wait("vector", sAC)
        P.eop("vector", lambda Eo: Eo.tensor_scalar(s1v[:], edv[:], c_one, None, op0=ALU.add), sDV)
        P.eop("vector", lambda Eo: Eo.reciprocal(rv[:], s1v[:]), sDV)
        P.eop("vector", lambda Eo: Eo.tensor_tensor(w2v[:], edv[:], rv[:], op=ALU.mult), sDV)
        P.eop("vector", lambda Eo: Eo.tensor_scalar(g1v[:], eq1[:], rv[:, 0:1], None, op0=ALU.mult), sDV)
        P.op("vector", lambda Eo, o=gates_tm[:, tc * 8:(tc + 1) * 8]:
             Eo.scalar_tensor_tensor(o, eq2[:], w2v[:, 0:1], g1v[:], op0=ALU.mult, op1=ALU.add),
             inc=sDV)
        gate_dve_done[sl] = sDV.n
    P.wait("tensor", sDV)
    for tc in range(NCH):
        sl = tc % 2
        P.op("tensor", lambda Eo, o=pTg[:, sl * 128:(sl + 1) * 128],
             a=gates_tm[:, tc * 8:(tc + 1) * 8]:
             Eo.transpose(o, a, idf[:]), inc=sPE)
        P.wait("scalar", sPE)
        P.op("scalar", lambda Eo, o=gatesT[:, tc * 128:(tc + 1) * 128],
             p=pTg[:, sl * 128:(sl + 1) * 128]: Eo.activation(o, p, AF.Copy), inc=sAC)
        P.wait("tensor", sAC)

    if "logits" in taps:
        P.wait("sync", sAC, logits_done)
        P.op("sync", lambda Eo: Eo.dma_start(out=taps["logits"], in_=logits[:]),
             inc=sTAP, inc_n=16)
    if "gates" in taps:
        P.wait("sync", sDV)
        P.op("sync", lambda Eo: Eo.dma_start(out=taps["gates"], in_=gates_tm[:]),
             inc=sTAP, inc_n=16)
    if "y1T" in taps:
        P.wait("gpsimd", sAC)
        P.op("gpsimd", lambda Eo: Eo.dma_start(out=taps["y1T"], in_=y1T[:]),
             inc=sTAP, inc_n=16)
    if "gatesT" in taps:
        P.wait("gpsimd", sAC)
        P.op("gpsimd", lambda Eo: Eo.dma_start(out=taps["gatesT"], in_=gatesT[:]),
             inc=sTAP, inc_n=16)
    if "y1" in taps:
        P.wait("sync", dSPp.sems[0])
        P.wait("sync", dSPp.sems[1])
        P.op("sync", lambda Eo: Eo.dma_start(out=taps["y1"], in_=d_y1sp), inc=sTAP, inc_n=16)
    P.wait("sync", dSPp.sems[0])
    P.wait("sync", dSPp.sems[1])
    P.wait("sync", sTAP)
    barrier(P)
    with nc.Block("b3") as blk:
        P.run(blk)
    es3.close()
    esA.close()

    esO = ExitStack()
    O_acc = esO.enter_context(nc.sbuf_tensor("O_acc", [128, 8 * TOK], F32))
    es4 = ExitStack()
    sb4 = lambda name, shape, dt: es4.enter_context(nc.sbuf_tensor(name, shape, dt))
    ps4 = lambda name, shape, dt: es4.enter_context(nc.psum_tensor(name, shape, dt))
    H = sb4("H", [128, 32 * NT], BF16)
    w1b = [sb4("w1b0", [128, 8 * 256], BF16), sb4("w1b1", [128, 8 * 256], BF16)]
    w2b = [sb4("w2b%d" % i, [128, 16 * 128], BF16) for i in range(4)]
    gbb = [sb4("gbb0", [128, NT], BF16), sb4("gbb1", [128, NT], BF16)]
    pH = [ps4("pH0", [128, 1024], F32), ps4("pH1", [128, 1024], F32)]
    pO2 = [ps4("pO20", [128, 512], F32), ps4("pO21", [128, 512], F32)]
    pGB = ps4("pGB", [128, NT], F32)

    P = Prog()
    for _eng in ENGINES:
        P.wait(_eng, sC1, ld_consts)
        P.wait(_eng, sC2, wl_consts)
    rW1 = Rot(2)
    rW2 = Rot(4)
    rGB = Rot(2)
    rPH = Rot(2)
    rPO2 = Rot(2)
    rGBpsum = Rot(1)

    w1_view = d_w1
    w2_view = d_w2
    ei = 0
    for nt in range(2):
        for e in range(E):
            gpar = ei % 2
            rGBpsum.acquire(P, "tensor", 0)
            for s in range(2):
                P.op("tensor", lambda Eo, p=pGB[:, s * 512:(s + 1) * 512],
                     sel=selb[:, e * 128:(e + 1) * 128],
                     g=gatesT[:, nt * NT + s * 512:nt * NT + (s + 1) * 512]:
                     Eo.matmul(p, sel, g, start=True, stop=True),
                     inc=(sPE if s == 1 else None))
            P.wait("vector", sPE)
            rGB.acquire(P, "vector", ei)
            P.op("vector", lambda Eo, o=gbb[gpar][:]: Eo.tensor_copy(o, pGB[:]), inc=sDV)
            rGBpsum.rel[0] = [(sDV, sDV.n)]
            gb_ready = sDV.n
            def issue_w2(k, e=e, ei=ei):
                gk = ei * 16 + k
                dc, fh = k // 2, k % 2
                w2i = gk % 4
                rW2.acquire(P, "gpsimd", gk)
                srcw = w2_view[e, fh * 2048:(fh + 1) * 2048,
                               dc * 128:(dc + 1) * 128].rearrange(
                    "(fc p) d -> p fc d", p=128)
                dW2.load(P, "gpsimd", gk,
                         lambda Eo, o=w2b[w2i][:].rearrange("p (fc d) -> p fc d", fc=16),
                         s=srcw: Eo.dma_start(out=o, in_=s))

            for fp in range(16):
                wpar = (ei * 16 + fp) % 2
                rW1.acquire(P, "gpsimd", ei * 16 + fp)
                src = w1_view[e, :, fp * 256:(fp + 1) * 256].rearrange(
                    "(kc p) f -> p kc f", p=128)
                dW1.load(P, "gpsimd", ei * 16 + fp,
                         lambda Eo, o=w1b[wpar][:].rearrange("p (kc f) -> p kc f", kc=8),
                         s=src: Eo.dma_start(out=o, in_=s))
                dW1.wait(P, "tensor", ei * 16 + fp)
                for fc in range(2):
                    fcg = fp * 2 + fc
                    hpar = fcg % 2
                    rPH.acquire(P, "tensor", fcg)
                    for nb in range(2):
                        for kc in range(8):
                            P.op("tensor", lambda Eo,
                                 p=pH[hpar][:, nb * 512:(nb + 1) * 512],
                                 w=w1b[wpar][:, kc * 256 + fc * 128:kc * 256 + (fc + 1) * 128],
                                 x=y1T[:, kc * TOK + nt * NT + nb * 512:
                                       kc * TOK + nt * NT + (nb + 1) * 512],
                                 st=(kc == 0), sp=(kc == 7):
                                 Eo.matmul(p, w, x, start=st, stop=sp),
                                 inc=(sPE if (nb == 1 and kc == 7) else None))
                    P.wait("scalar", sPE)
                    hsl = H[:, fcg * NT:(fcg + 1) * NT]
                    P.op("scalar", lambda Eo, o=hsl, p=pH[hpar][:],
                         b=b1t[:, e * 32 + fcg:e * 32 + fcg + 1]:
                         Eo.activation(o, p, AF.Gelu_apprx_tanh, bias=b, scale=f32(1.0)),
                         inc=sAC)
                    rPH.release(fcg, sAC)
                    P.wait("vector", sAC)
                    P.wait("vector", sDV, gb_ready)
                    P.op("vector", lambda Eo, h=hsl, g=gbb[gpar][:]:
                         Eo.tensor_tensor(h, h, g, op=ALU.mult), inc=sDV)
                rW1.release(ei * 16 + fp, sPE)
                if fp >= 12:
                    issue_w2(fp - 12)
            h_gated = sDV.n
            rGB.release(ei, sDV)
            P.wait("tensor", sDV, h_gated)
            for dc in range(8):
                for fh in range(2):
                    k = dc * 2 + fh
                    w2i = (ei * 16 + k) % 4
                    dW2.wait(P, "tensor", ei * 16 + k)
                    for nb in range(2):
                        opar = nb
                        if fh == 0:
                            rPO2.acquire(P, "tensor", ei * 16 + dc * 2 + nb)
                        for fcL in range(16):
                            fcg = fh * 16 + fcL
                            if e == 0 and fh == 0 and fcL == 0:
                                P.op("tensor", lambda Eo, p=pO2[opar][:],
                                     b=b2b[:, dc * 128:(dc + 1) * 128],
                                     g=gatesT[:, nt * NT + nb * 512:nt * NT + (nb + 1) * 512]:
                                     Eo.matmul(p, b, g, start=True, stop=False))
                            P.op("tensor", lambda Eo, p=pO2[opar][:],
                                 w=w2b[w2i][:, fcL * 128:(fcL + 1) * 128],
                                 x=H[:, fcg * NT + nb * 512:fcg * NT + (nb + 1) * 512],
                                 st=(fh == 0 and fcL == 0 and e != 0),
                                 sp=(fh == 1 and fcL == 15):
                                 Eo.matmul(p, w, x, start=st, stop=sp),
                                 inc=(sPE if fcL == 15 else None))
                        if fh == 1:
                            P.wait("vector", sPE)
                            osl = O_acc[:, dc * TOK + nt * NT + nb * 512:
                                        dc * TOK + nt * NT + (nb + 1) * 512]
                            if e == 0:
                                P.op("vector", lambda Eo, o=osl, p=pO2[opar][:]:
                                     Eo.tensor_copy(o, p), inc=sDV)
                            else:
                                P.op("vector", lambda Eo, o=osl, p=pO2[opar][:]:
                                     Eo.tensor_add(o, o, p), inc=sDV)
                            rPO2.release(ei * 16 + dc * 2 + nb, sDV)
                    rW2.release(ei * 16 + k, sPE)
                    if k + 4 < 16:
                        issue_w2(k + 4)
            ei += 1

    if "O_acc" in taps:
        P.wait("sync", sDV)
        P.op("sync", lambda Eo: Eo.dma_start(out=taps["O_acc"], in_=O_acc[:]),
             inc=sTAP, inc_n=16)
    P.wait("sync", sTAP)
    barrier(P)
    with nc.Block("b4") as blk:
        P.run(blk)
    es4.close()

    es5 = ExitStack()
    sb5 = lambda name, shape, dt: es5.enter_context(nc.sbuf_tensor(name, shape, dt))
    ps5 = lambda name, shape, dt: es5.enter_context(nc.psum_tensor(name, shape, dt))
    y1ch = [sb5("y1ch0", [128, C], F32), sb5("y1ch1", [128, C], F32)]
    z2ch = [sb5("z2ch0", [128, C], F32), sb5("z2ch1", [128, C], F32)]
    sq2 = [sb5("sq20", [128, C], F32), sb5("sq21", [128, C], F32)]
    mu2 = [sb5("mu20", [128, 1], F32), sb5("mu21", [128, 1], F32)]
    var2_ = [sb5("var20", [128, 1], F32), sb5("var21", [128, 1], F32)]
    lnv2_ = [sb5("lnv20", [128, 1], F32), sb5("lnv21", [128, 1], F32)]
    rstd2 = [sb5("rstd20", [128, 1], F32), sb5("rstd21", [128, 1], F32)]
    pTf2 = [ps5("pTf20", [128, 512], F32), ps5("pTf21", [128, 512], F32)]

    P = Prog()
    for _eng in ENGINES:
        P.wait(_eng, sC1, ld_consts)
        P.wait(_eng, sC2, wl_consts)
    rY1 = Rot(2)
    rZ2 = Rot(2)
    rSQ2 = Rot(2)
    rMU2 = Rot(2)
    rRS2 = Rot(2)
    rTF2 = Rot(2)
    P.wait("sync", dSPp.sems[0])
    P.wait("sync", dSPp.sems[1])
    for tc in range(NCH):
        par = tc % 2
        rY1.acquire(P, "sync", tc)
        dYC.load(P, "sync", tc, lambda Eo, o=y1ch[par][:], s=d_y1sp[tc * 128:(tc + 1) * 128, :]:
                 Eo.dma_start(out=o, in_=s))
        for g in range(2):
            tfp = pTf2[g]
            rTF2.acquire(P, "tensor", tc * 2 + g)
            for t in range(4):
                dc = g * 4 + t
                P.op("tensor", lambda Eo, o=tfp[:, t * 128:(t + 1) * 128],
                     a=O_acc[:, dc * TOK + tc * 128:dc * TOK + (tc + 1) * 128]:
                     Eo.transpose(o, a, idf[:]), inc=(sPE if t == 3 else None))
            P.wait("vector", sPE)
            dYC.wait(P, "vector", tc)
            rZ2.acquire(P, "vector", tc) if g == 0 else None
            P.op("vector", lambda Eo, o=z2ch[par][:, g * 512:(g + 1) * 512],
                 a=tfp[:], b=y1ch[par][:, g * 512:(g + 1) * 512]:
                 Eo.tensor_add(o, a, b), inc=sDV)
            rTF2.release(tc * 2 + g, sDV)
        rY1.release(tc, sDV)
        z2 = z2ch[par]
        rMU2.acquire(P, "vector", tc)
        mu = mu2[par]
        P.wait("vector", sDV)
        P.eop("vector", lambda Eo, m=mu[:], z_=z2[:]: Eo.reduce_sum(m, z_, axis=AX.X), sDV)
        P.eop("vector", lambda Eo, m=mu[:]:
              Eo.tensor_scalar(m, m, c_inv1024, None, op0=ALU.mult), sDV)
        P.wait("scalar", sDV)
        rSQ2.acquire(P, "scalar", tc)
        P.op("scalar", lambda Eo, q=sq2[par][:], z_=z2[:], m=mu[:]:
             Eo.activation(q, z_, AF.Square, bias=m, scale=f32(-1.0)), inc=sAC)
        P.wait("vector", sAC)
        var2 = var2_[par]
        lnv2 = lnv2_[par]
        P.eop("vector", lambda Eo, vv=var2[:], q=sq2[par][:]: Eo.reduce_sum(vv, q, axis=AX.X), sDV)
        P.op("vector", lambda Eo, vv=var2[:]:
             Eo.tensor_scalar(vv, vv, c_inv1024, None, op0=ALU.mult), inc=sDV)
        rSQ2.rel[par] = [(sDV, sDV.n)]
        P.wait("scalar", sDV)
        rRS2.acquire(P, "scalar", tc)
        P.eop("scalar", lambda Eo, lv=lnv2[:], vv=var2[:]: Eo.activation(lv, vv, AF.Ln, bias=f32(EPS)), sAC)
        P.op("scalar", lambda Eo, r=rstd2[par][:], lv=lnv2[:]:
             Eo.activation(r, lv, AF.Exp, scale=f32(-0.5)), inc=sAC)
        P.wait("vector", sAC)
        P.op("vector", lambda Eo, z_=z2[:], m=mu[:], r=rstd2[par][:]:
             Eo.tensor_scalar(z_, z_, m, r, op0=ALU.subtract, op1=ALU.mult), inc=sDV)
        rMU2.release(tc, sDV)
        rRS2.release(tc, sDV)
        P.wait("sync", sDV)
        dSTp.load(P, "sync", tc, lambda Eo, o=d_y2[tc * 128:(tc + 1) * 128, :], z_=z2[:]:
                  Eo.dma_start(out=o, in_=z_))
        rZ2.rel[par] = [dSTp.sem_count(tc)]
    P.wait("sync", dSTp.sems[0])
    P.wait("sync", dSTp.sems[1])
    P.wait("sync", sTAP)
    with nc.Block("b5") as blk:
        P.run(blk)
    es5.close()
    esO.close()
    esP.close()
    esS.close()
    return nc


_CACHE = {}


def kernel(L0_attn_x, L0_attn_Wq, L0_attn_bq, L0_attn_Wk, L0_attn_bk,
           L0_attn_Wv, L0_attn_bv, L0_ffn_router_w, L0_ffn_w1, L0_ffn_b1,
           L0_ffn_w2, L0_ffn_b2):
    x = np.ascontiguousarray(np.asarray(L0_attn_x, dtype=np.float32))
    Wq = np.asarray(L0_attn_Wq, np.float32)
    Wk = np.asarray(L0_attn_Wk, np.float32)
    Wv = np.asarray(L0_attn_Wv, np.float32)
    bq = np.asarray(L0_attn_bq, np.float32)
    bk = np.asarray(L0_attn_bk, np.float32)
    bv = np.asarray(L0_attn_bv, np.float32)
    rw_in = np.asarray(L0_ffn_router_w, np.float32)
    w1 = np.ascontiguousarray(np.asarray(L0_ffn_w1, np.float32))
    b1 = np.asarray(L0_ffn_b1, np.float32)
    w2 = np.ascontiguousarray(np.asarray(L0_ffn_w2, np.float32))
    b2 = np.asarray(L0_ffn_b2, np.float32)

    wqkvT = np.concatenate([Wq.T, Wk.T, Wv.T], axis=1)
    wqkvT = np.ascontiguousarray(wqkvT.reshape(8, 128, 192).transpose(1, 0, 2))
    qkb = np.zeros((128, 2), np.float32)
    qkb[0:64, 0] = bq
    qkb[0:64, 1] = bk
    bvrow = bv.reshape(1, HS)
    b1t = np.ascontiguousarray(
        b1.reshape(E, 32, 128).transpose(2, 0, 1))
    rw_pre = np.ascontiguousarray(rw_in.reshape(8, 128, E).transpose(1, 0, 2))
    identb = np.eye(128, dtype=bf16)
    identf = np.eye(128, dtype=np.float32)
    qpiota = (np.arange(128, dtype=np.float32)[:, None]
              - np.arange(512, dtype=np.float32)[None, :])
    onesb = np.ones((1, 128), dtype=bf16)
    consts = np.zeros((128, 8), np.float32)
    consts[:, C_INV1024] = 1.0 / 1024.0
    consts[:, C_NEG132] = -1.0 / 32.0
    consts[:, C_ONE] = 1.0
    consts[:, C_NEG1E30] = -1e30
    selb = np.zeros((E, E * 128), dtype=bf16)
    for e in range(E):
        selb[e, e * 128:(e + 1) * 128] = 1

    nc = build_nc()
    in_maps = []
    for c in range(NCORES):
        b = c // 4
        ph = c % 4
        chunks = [4 * j + ph for j in range(NCH)]
        x_b = x[b]
        x_own = np.ascontiguousarray(
            x_b.reshape(64, 128, C)[chunks].reshape(TOK, C))
        qoffb = np.full((128, 1), 128.0 * ph, np.float32)
        in_maps.append(dict(
            x_own=x_own, x_batch=x_b, wqkvT=wqkvT, qkb=qkb, bvrow=bvrow,
            b1t=b1t, b2w=b2, rw=rw_pre, w1=w1, w2=w2, identb=identb,
            identf=identf, qpiota=qpiota, onesb=onesb, qoffb=qoffb,
            consts=consts, selb=selb))
    res = run_bass_kernel_spmd(nc, in_maps, core_ids=list(range(NCORES)))
    out = np.zeros((B, T, C), np.float32)
    for c in range(NCORES):
        b = c // 4
        ph = c % 4
        y2 = np.asarray(res.results[c]["y2"]).reshape(NCH, 128, C)
        for j in range(NCH):
            out[b, (4 * j + ph) * 128:(4 * j + ph + 1) * 128, :] = y2[j]
    return out


# version 06
# speedup vs baseline: 1.1199x; 1.0048x over previous
import numpy as np
import ml_dtypes
from contextlib import ExitStack

import concourse.bass as bass
import concourse.mybir as mybir
from concourse.bass_utils import run_bass_kernel_spmd

F32 = mybir.dt.float32
BF16 = mybir.dt.bfloat16
AF = mybir.ActivationFunctionType
ALU = mybir.AluOpType
AX = mybir.AxisListType
f32 = np.float32
bf16 = ml_dtypes.bfloat16

B, T, C = 2, 8192, 1024
HS = 64
E, FF = 8, 4096
NCORES = 8
TOK = 2048
NCH = 16
KCH = 64
NT = 1024
EPS = 1e-5

C_INV1024, C_NEG132, C_ONE, C_NEG1E30, C_ZERO = 0, 1, 2, 3, 4

ENGINES = ("sync", "scalar", "vector", "tensor", "gpsimd")


class Sem:
    def __init__(self, nc, es, name):
        self.h = es.enter_context(nc.semaphore(name))
        self.n = 0

    def bump(self, k):
        self.n += k
        return self.n


class Prog:

    def __init__(self):
        self.streams = {e: [] for e in ENGINES}

    def op(self, eng, fn, inc=None, inc_n=1):
        if inc is not None:
            inc.bump(inc_n)
            self.streams[eng].append(
                lambda Eo, fn=fn, s=inc.h, n=inc_n: fn(Eo).then_inc(s, n))
        else:
            self.streams[eng].append(lambda Eo, fn=fn: fn(Eo))

    def eop(self, eng, fn, sem):
        self.op(eng, fn, inc=sem)
        self.wait(eng, sem)

    def wait(self, eng, sem, val=None):
        v = sem.n if val is None else val
        if v > 0:
            self.streams[eng].append(lambda Eo, s=sem.h, v=v: Eo.wait_ge(s, v))

    def run(self, block):
        for eng in ENGINES:
            fns = tuple(self.streams[eng])

            def body(Eo, fns=fns):
                for f in fns:
                    f(Eo)

            getattr(block, eng)(body)


class Rot:

    def __init__(self, n):
        self.n = n
        self.rel = [[] for _ in range(n)]

    def acquire(self, P, eng, i):
        for sem, cnt in self.rel[i % self.n]:
            P.wait(eng, sem, cnt)
        self.rel[i % self.n] = []

    def release(self, i, sem):
        self.rel[i % self.n].append((sem, sem.n))


class DmaSlots:

    def __init__(self, nc, es, name, n=2):
        self.sems = [Sem(nc, es, "%s%d" % (name, i)) for i in range(n)]
        self.n = n

    def load(self, P, eng, i, fn):
        P.op(eng, fn, inc=self.sems[i % self.n], inc_n=16)

    def wait(self, P, eng, i):
        P.wait(eng, self.sems[i % self.n])

    def sem_count(self, i):
        s = self.sems[i % self.n]
        return (s, s.n)


def build_nc(debug_taps=()):
    nc = bass.Bass("TRN2", target_bir_lowering=False)

    d_xown = nc.dram_tensor("x_own", [TOK, C], F32, kind="ExternalInput").ap()
    d_xbat = nc.dram_tensor("x_batch", [T, C], F32, kind="ExternalInput").ap()
    d_wqkv = nc.dram_tensor("wqkvT", [128, 8, 192], F32, kind="ExternalInput").ap()
    d_qkb = nc.dram_tensor("qkb", [128, 2], F32, kind="ExternalInput").ap()
    d_bvr = nc.dram_tensor("bvrow", [1, HS], F32, kind="ExternalInput").ap()
    d_b1t = nc.dram_tensor("b1t", [128, E, 32], F32, kind="ExternalInput").ap()
    d_b2w = nc.dram_tensor("b2w", [E, C], F32, kind="ExternalInput").ap()
    d_rw = nc.dram_tensor("rw", [128, 8, E], F32, kind="ExternalInput").ap()
    d_w1 = nc.dram_tensor("w1", [E, C, FF], F32, kind="ExternalInput").ap()
    d_w2 = nc.dram_tensor("w2", [E, FF, C], F32, kind="ExternalInput").ap()
    d_idb = nc.dram_tensor("identb", [128, 128], BF16, kind="ExternalInput").ap()
    d_idf = nc.dram_tensor("identf", [128, 128], F32, kind="ExternalInput").ap()
    d_qpi = nc.dram_tensor("qpiota", [128, 512], F32, kind="ExternalInput").ap()
    d_ones = nc.dram_tensor("onesb", [1, 128], BF16, kind="ExternalInput").ap()
    d_qoff = nc.dram_tensor("qoffb", [128, 1], F32, kind="ExternalInput").ap()
    d_cst = nc.dram_tensor("consts", [128, 8], F32, kind="ExternalInput").ap()
    d_sel = nc.dram_tensor("selb", [E, E * 128], BF16, kind="ExternalInput").ap()
    d_y1sp = nc.dram_tensor("y1sp", [TOK, C], F32, kind="Internal").ap()
    d_y2 = nc.dram_tensor("y2", [TOK, C], F32, kind="ExternalOutput").ap()
    taps = {}
    for name, shape in debug_taps:
        taps[name] = nc.dram_tensor("tap_" + name, shape, F32, kind="ExternalOutput").ap()

    esP = ExitStack()
    sb = lambda name, shape, dt: esP.enter_context(nc.sbuf_tensor(name, shape, dt))

    idb = sb("idb", [128, 128], BF16)
    idf = sb("idf", [128, 128], F32)
    onesb = sb("onesb_s", [1, 128], BF16)
    qoffb = sb("qoffb_s", [128, 1], F32)
    cst = sb("cst", [128, 8], F32)
    b1t = sb("b1t_s", [128, E * 32], F32)
    b2b = sb("b2b", [E, C], BF16)
    rw = sb("rw_s", [128, 8 * E], F32)
    y1T = sb("y1T", [128, 8 * TOK], BF16)
    gatesT = sb("gatesT", [E, TOK], BF16)
    selb = sb("selb_s", [E, E * 128], BF16)

    esA = ExitStack()
    sbA = lambda name, shape, dt: esA.enter_context(nc.sbuf_tensor(name, shape, dt))
    qpi = sbA("qpi", [128, 512], F32)
    maskadd = sbA("maskadd", [128, 512], F32)
    qT = sbA("qT", [64, TOK], BF16)
    kT = sbA("kT", [64, T], BF16)
    vtm = sbA("vtm", [128, KCH * HS], BF16)
    o_tm = sbA("o_tm", [128, NCH * HS], F32)
    mA = sbA("mA", [128, 1], F32)
    mB = sbA("mB", [128, 1], F32)
    dm = sbA("dm", [128, 1], F32)
    edm = sbA("edm", [128, 1], F32)
    denA = sbA("denA", [128, 1], F32)
    denB = sbA("denB", [128, 1], F32)
    negm32 = sbA("negm32", [128, 1], F32)
    lden = sbA("lden", [128, 1], F32)
    bias2 = sbA("bias2", [128, 1], F32)
    bsum = [sbA("bsum0", [128, 1], F32), sbA("bsum1", [128, 1], F32)]

    esS = ExitStack()
    mks = lambda name: Sem(nc, esS, name)
    sC1 = mks("sC1")
    sC2 = mks("sC2")
    sTAP = mks("sTAP")
    sPE = mks("sPE")
    sAC = mks("sAC")
    sDV = mks("sDV")
    sBAR = mks("sBAR")
    dXB = DmaSlots(nc, esS, "dXB")
    dZC = DmaSlots(nc, esS, "dZC", n=3)
    dSPp = DmaSlots(nc, esS, "dSPp", n=3)
    dW1 = DmaSlots(nc, esS, "dW1")
    dW2 = DmaSlots(nc, esS, "dW2", n=4)
    dYC = DmaSlots(nc, esS, "dYC", n=3)
    dSTp = DmaSlots(nc, esS, "dSTp", n=3)

    c_inv1024 = cst[:, C_INV1024:C_INV1024 + 1]
    c_neg132 = cst[:, C_NEG132:C_NEG132 + 1]
    c_one = cst[:, C_ONE:C_ONE + 1]
    c_neg1e30 = cst[:, C_NEG1E30:C_NEG1E30 + 1]
    c_zero = cst[:, C_ZERO:C_ZERO + 1]

    barrier_count = [0]

    def barrier(P):
        barrier_count[0] += 1
        for eng in ENGINES:
            P.op(eng, lambda Eo: Eo.nop(), inc=sBAR)
        for eng in ENGINES:
            P.wait(eng, sBAR, 5 * barrier_count[0])

    es1 = ExitStack()
    sb1 = lambda name, shape, dt: es1.enter_context(nc.sbuf_tensor(name, shape, dt))
    ps1 = lambda name, shape, dt: es1.enter_context(nc.psum_tensor(name, shape, dt))
    xb = [sb1("xb0", [128, C], F32), sb1("xb1", [128, C], F32)]
    xT = [sb1("xT0", [128, C], BF16), sb1("xT1", [128, C], BF16)]
    wqkv = sb1("wqkv", [128, 8 * 192], BF16)
    bvr = sb1("bvr", [1, HS], BF16)
    qkb = sb1("qkb_s", [128, 2], F32)
    pXT = [ps1("pXT0", [128, C], F32), ps1("pXT1", [128, C], F32)]
    pQK = [ps1("pQK0", [64, 128], F32), ps1("pQK1", [64, 128], F32)]
    pV = [ps1("pV0", [128, HS], F32), ps1("pV1", [128, HS], F32)]

    P = Prog()
    for dst, src in [(idb, d_idb), (idf, d_idf), (qpi, d_qpi), (qoffb, d_qoff),
                     (cst, d_cst), (qkb[:], d_qkb),
                     (b1t, d_b1t.rearrange("p e k -> p (e k)")),
                     (rw, d_rw.rearrange("p c e -> p (c e)"))]:
        ap = dst[:] if hasattr(dst, "ap") else dst
        P.op("sync", lambda Eo, a=ap, s=src: Eo.dma_start(out=a, in_=s), inc=sC1, inc_n=16)
    for dst, src in [(onesb[:], d_ones), (selb[:], d_sel),
                     (wqkv[:], d_wqkv.rearrange("p c h -> p (c h)")),
                     (bvr[:], d_bvr), (b2b[:], d_b2w)]:
        P.op("gpsimd", lambda Eo, a=dst, s=src: Eo.dma_start(out=a, in_=s), inc=sC2, inc_n=16)
    ld_consts = sC1.n
    wl_consts = sC2.n
    for _eng in ("tensor", "scalar", "vector", "gpsimd"):
        P.wait(_eng, sC1, ld_consts)
        P.wait(_eng, sC2, wl_consts)
    P.wait("vector", sC1, ld_consts)
    P.eop("vector", lambda Eo: Eo.tensor_scalar(maskadd[:], qpi[:], qoffb[:, 0:1], None, op0=ALU.add), sDV)
    P.eop("vector", lambda Eo: Eo.tensor_scalar(maskadd[:], maskadd[:], c_zero, None, op0=ALU.is_lt), sDV)
    P.op("vector", lambda Eo: Eo.tensor_scalar(maskadd[:], maskadd[:], c_neg1e30, None, op0=ALU.mult), inc=sDV)

    rXB = Rot(2)
    rXT = Rot(2)
    rQK = Rot(2)
    rV = Rot(2)

    rPXT = Rot(2)

    def chunk_setup(i, src_ap, is_q, kc):
        pi = i % 2
        rXB.acquire(P, "sync", i)
        dXB.load(P, "sync", i, lambda Eo, a=xb[pi][:], s=src_ap: Eo.dma_start(out=a, in_=s))
        dXB.wait(P, "tensor", i)
        rPXT.acquire(P, "tensor", i)
        rXT_psum = pXT[pi]
        for dc in range(8):
            P.op("tensor", lambda Eo, o=rXT_psum[:, dc * 128:(dc + 1) * 128],
                 a=xb[pi][:, dc * 128:(dc + 1) * 128]: Eo.transpose(o, a, idf[:]),
                 inc=(sPE if dc == 7 else None))
        transposes_done = sPE.n
        P.wait("vector", sPE)
        rXT.acquire(P, "vector", i)
        P.op("vector", lambda Eo, o=xT[pi][:], a=rXT_psum[:]: Eo.tensor_copy(o, a), inc=sDV)
        rPXT.release(i, sDV)
        evac_cnt = sDV.n
        P.wait("tensor", sDV, evac_cnt)
        if is_q:
            rQK.acquire(P, "tensor", i)
            for dc in range(8):
                P.op("tensor", lambda Eo, p=pQK[pi][:],
                     w=wqkv[:, dc * 192:dc * 192 + 64],
                     x=xT[pi][:, dc * 128:(dc + 1) * 128], st=(dc == 0), sp=(dc == 7):
                     Eo.matmul(p, w, x, start=st, stop=sp),
                     inc=(sPE if dc == 7 else None))
            P.wait("scalar", sPE)
            P.op("scalar", lambda Eo, o=qT[:, i * 128:(i + 1) * 128], p=pQK[pi][:]:
                 Eo.activation(o, p, AF.Identity, bias=qkb[0:64, 0:1], scale=f32(1.0)),
                 inc=sAC)
            rQK.release(i, sAC)
        else:
            rQK.acquire(P, "tensor", i)
            for dc in range(8):
                P.op("tensor", lambda Eo, p=pQK[pi][:],
                     w=wqkv[:, dc * 192 + 64:dc * 192 + 128],
                     x=xT[pi][:, dc * 128:(dc + 1) * 128], st=(dc == 0), sp=(dc == 7):
                     Eo.matmul(p, w, x, start=st, stop=sp),
                     inc=(sPE if dc == 7 else None))
            P.wait("scalar", sPE)
            P.op("scalar", lambda Eo, o=kT[:, kc * 128:(kc + 1) * 128], p=pQK[pi][:]:
                 Eo.activation(o, p, AF.Identity, bias=qkb[0:64, 1:2], scale=f32(1.0)),
                 inc=sAC)
            rQK.release(i, sAC)
            rV.acquire(P, "tensor", i)
            for dc in range(8):
                P.op("tensor", lambda Eo, p=pV[pi][:],
                     x=xT[pi][:, dc * 128:(dc + 1) * 128],
                     w=wqkv[:, dc * 192 + 128:dc * 192 + 192], st=(dc == 0):
                     Eo.matmul(p, x, w, start=st, stop=False))
            P.op("tensor", lambda Eo, p=pV[pi][:]:
                 Eo.matmul(p, onesb[:], bvr[:], start=False, stop=True), inc=sPE)
            P.wait("vector", sPE)
            P.op("vector", lambda Eo, o=vtm[:, kc * HS:(kc + 1) * HS], p=pV[pi][:]:
                 Eo.tensor_copy(o, p), inc=sDV)
            rV.release(i, sDV)
        rXB.rel[pi] = [(sPE, transposes_done)]
        rXT.rel[pi] = [(sPE, sPE.n)]

    for i in range(NCH):
        chunk_setup(i, d_xown[i * 128:(i + 1) * 128, :], True, None)
    for kc in range(KCH):
        chunk_setup(NCH + kc, d_xbat[kc * 128:(kc + 1) * 128, :], False, kc)

    P.wait("sync", sTAP)
    barrier(P)
    with nc.Block("b1") as blk:
        P.run(blk)
    es1.close()

    es2 = ExitStack()
    sb2 = lambda name, shape, dt: es2.enter_context(nc.sbuf_tensor(name, shape, dt))
    ps2 = lambda name, shape, dt: es2.enter_context(nc.psum_tensor(name, shape, dt))
    Psb = [sb2("Psb0", [128, T], BF16), sb2("Psb1", [128, T], BF16)]
    PT = [sb2("PT0", [128, 512], BF16), sb2("PT1", [128, 512], BF16)]
    ots = sb2("ots", [64, 128], F32)
    mtab = [sb2("mtab0", [128, NCH], F32), sb2("mtab1", [128, NCH], F32)]
    dentab = [sb2("dentab0", [128, NCH], F32), sb2("dentab1", [128, NCH], F32)]
    negm32 = [sb2("negm32a", [128, 1], F32), sb2("negm32b", [128, 1], F32)]
    recipd = [sb2("recipd0", [128, 1], F32), sb2("recipd1", [128, 1], F32)]
    mfin = sb2("mfin", [128, 1], F32)
    denf = sb2("denf", [128, 1], F32)
    pSA = [ps2("pSA0", [128, 512], F32), ps2("pSA1", [128, 512], F32)]
    pSB = [ps2("pSB0", [128, 512], F32), ps2("pSB1", [128, 512], F32)]
    pT = ps2("pT", [128, 512], BF16)
    pO = ps2("pO", [64, 128], F32)
    pOt = ps2("pOt", [128, HS], F32)

    P = Prog()
    for _eng in ENGINES:
        P.wait(_eng, sC1, ld_consts)
        P.wait(_eng, sC2, wl_consts)
    rSA = Rot(2)
    rSB = Rot(2)
    rPT = Rot(2)
    rPT_psum = Rot(1)
    rPO = Rot(1)
    rPstrip = Rot(2)
    rMT = Rot(2)
    rNM = Rot(2)
    rDT = Rot(2)
    rRD = Rot(2)

    def pass_a(j):
        par2 = j % 2
        rMT.acquire(P, "vector", j)
        for kb in range(j + 1):
            par = kb % 2
            rSA.acquire(P, "tensor", kb)
            P.op("tensor", lambda Eo, p=pSA[par][:],
                 q=qT[:, j * 128:(j + 1) * 128],
                 k=kT[:, kb * 512:(kb + 1) * 512]:
                 Eo.matmul(p, q, k, start=True, stop=True), inc=sPE)
            P.wait("vector", sPE)
            if kb == j:
                P.eop("vector", lambda Eo, p=pSA[par][:]:
                      Eo.tensor_add(p, p, maskadd[:]), sDV)
            P.op("vector", lambda Eo, o=mtab[par2][:, kb:kb + 1], p=pSA[par][:]:
                 Eo.reduce_max(o, p, axis=AX.X), inc=sDV)
            rSA.release(kb, sDV)
        P.wait("vector", sDV)
        P.eop("vector", lambda Eo, mt=mtab[par2][:, 0:j + 1]:
              Eo.reduce_max(mfin[:], mt, axis=AX.X), sDV)
        rNM.acquire(P, "vector", j)
        P.op("vector", lambda Eo, o=negm32[par2][:]:
             Eo.tensor_scalar(o, mfin[:], c_neg132, None, op0=ALU.mult), inc=sDV)
        nm_done[j] = sDV.n

    def pass_b(j):
        par2 = j % 2
        rPstrip.acquire(P, "scalar", j)
        P.wait("scalar", sDV, nm_done[j])
        rDT.acquire(P, "scalar", j)
        def b_mm(kb):
            par = kb % 2
            rSB.acquire(P, "tensor", kb)
            P.op("tensor", lambda Eo, p=pSB[par][:],
                 q=qT[:, j * 128:(j + 1) * 128],
                 k=kT[:, kb * 512:(kb + 1) * 512]:
                 Eo.matmul(p, q, k, start=True, stop=True), inc=sPE)
            return sPE.n

        mm_done = {0: b_mm(0)}
        for kb in range(j + 1):
            par = kb % 2
            if kb + 1 <= j:
                mm_done[kb + 1] = b_mm(kb + 1)
            if kb == j:
                P.wait("vector", sPE, mm_done[kb])
                P.op("vector", lambda Eo, p=pSB[par][:]:
                     Eo.tensor_add(p, p, maskadd[:]), inc=sDV)
                P.wait("scalar", sDV)
            else:
                P.wait("scalar", sPE, mm_done[kb])
            P.op("scalar", lambda Eo, o=Psb[par2][:, kb * 512:(kb + 1) * 512], p=pSB[par][:],
                 nm=negm32[par2][:], acc=dentab[par2][:, kb:kb + 1]:
                 Eo.activation(o, p, AF.Exp, bias=nm, scale=f32(1.0 / 32.0),
                               accum_out=acc), inc=sAC)
            rSB.release(kb, sAC)
            P.wait("tensor", sAC)
            rPT_psum.acquire(P, "tensor", 0)
            for s in range(4):
                P.op("tensor", lambda Eo, o=pT[:, s * 128:(s + 1) * 128],
                     a=Psb[par2][:, kb * 512 + s * 128:kb * 512 + (s + 1) * 128]:
                     Eo.transpose(o, a, idb[:]), inc=(sPE if s == 3 else None))
            P.wait("vector", sPE)
            rPT.acquire(P, "vector", kb)
            P.op("vector", lambda Eo, o=PT[par][:]: Eo.tensor_copy(o, pT[:]), inc=sDV)
            rPT_psum.rel[0] = [(sDV, sDV.n)]
            P.wait("tensor", sDV)
            if kb == 0:
                rPO.acquire(P, "tensor", 0)
            for s in range(4):
                P.op("tensor", lambda Eo, v=vtm[:, (kb * 4 + s) * HS:(kb * 4 + s + 1) * HS],
                     pp=PT[par][:, s * 128:(s + 1) * 128],
                     st=(kb == 0 and s == 0), sp=(kb == j and s == 3):
                     Eo.matmul(pO[:], v, pp, start=st, stop=sp),
                     inc=(sPE if s == 3 else None))
            rPT.release(kb, sPE)
        rPstrip.rel[par2] = [(sPE, sPE.n)]
        rNM.rel[par2] = [(sAC, sAC.n)]
        P.wait("vector", sAC)
        P.wait("vector", sDV)
        P.eop("vector", lambda Eo, dt_=dentab[par2][:, 0:j + 1]:
              Eo.reduce_sum(denf[:], dt_, axis=AX.X), sDV)
        rRD.acquire(P, "vector", j)
        P.eop("vector", lambda Eo, o=recipd[par2][:]:
              Eo.reciprocal(o, denf[:]), sDV)
        rDT.rel[par2] = [(sDV, sDV.n)]
        P.wait("scalar", sPE)
        P.op("scalar", lambda Eo: Eo.activation(ots[:], pO[:], AF.Copy), inc=sAC)
        rPO.rel[0] = [(sAC, sAC.n)]
        P.wait("tensor", sAC)
        P.op("tensor", lambda Eo: Eo.transpose(pOt[:], ots[:], idf[0:64, 0:64]), inc=sPE)
        P.wait("vector", sPE)
        P.op("vector", lambda Eo, o=o_tm[:, j * HS:(j + 1) * HS], r=recipd[par2][:]:
             Eo.tensor_scalar(o, pOt[:], r[:, 0:1], None, op0=ALU.mult), inc=sDV)
        rRD.rel[par2] = [(sDV, sDV.n)]
        P.wait("tensor", sDV)
        P.wait("scalar", sDV)

    nm_done = {}
    for j in range(NCH):
        pass_a(j)
        if j > 0:
            pass_b(j - 1)
    pass_b(NCH - 1)

    if "o_tm" in taps:
        P.wait("sync", sDV)
        P.op("sync", lambda Eo: Eo.dma_start(out=taps["o_tm"], in_=o_tm[:]),
             inc=sTAP, inc_n=16)
    if "qT" in taps:
        P.wait("gpsimd", sDV)
        P.op("gpsimd", lambda Eo: Eo.dma_start(out=taps["qT"], in_=qT[:]),
             inc=sTAP, inc_n=16)
    if "kT" in taps:
        P.wait("gpsimd", sDV)
        P.op("gpsimd", lambda Eo: Eo.dma_start(out=taps["kT"], in_=kT[:]),
             inc=sTAP, inc_n=16)
    if "vtm" in taps:
        P.wait("gpsimd", sDV)
        P.op("gpsimd", lambda Eo: Eo.dma_start(out=taps["vtm"], in_=vtm[:]),
             inc=sTAP, inc_n=16)
    P.wait("sync", sTAP)
    P.wait("gpsimd", sTAP)
    barrier(P)
    with nc.Block("b2") as blk:
        P.run(blk)
    es2.close()

    es3 = ExitStack()
    sb3 = lambda name, shape, dt: es3.enter_context(nc.sbuf_tensor(name, shape, dt))
    ps3 = lambda name, shape, dt: es3.enter_context(nc.psum_tensor(name, shape, dt))
    zch = [sb3("zch%d" % i, [128, C], F32) for i in range(3)]
    sqb = [sb3("sqb%d" % i, [128, C], F32) for i in range(3)]
    tmp4 = [sb3("tmp4%d" % i, [128, 512], F32) for i in range(3)]
    logits = sb3("logits", [E, TOK], F32)
    gtm = sb3("gtm", [128, 8], F32)
    gates_tm = sb3("gates_tm", [128, 128], F32)
    muv = [sb3("mu%d" % i, [128, 1], F32) for i in range(3)]
    varv_ = [sb3("varv%d" % i, [128, 1], F32) for i in range(3)]
    lnvv_ = [sb3("lnvv%d" % i, [128, 1], F32) for i in range(3)]
    rstdv = [sb3("rstd%d" % i, [128, 1], F32) for i in range(3)]
    eq1 = sb3("eq1", [128, 8], F32)
    eq2 = sb3("eq2", [128, 8], F32)
    mskl = sb3("mskl", [128, 8], F32)
    m1v = sb3("m1v", [128, 1], F32)
    m2v = sb3("m2v", [128, 1], F32)
    edv = sb3("edv", [128, 1], F32)
    edv2 = sb3("edv2", [128, 1], F32)
    s1v = sb3("s1v", [128, 1], F32)
    rv = sb3("rv", [128, 1], F32)
    w2v = sb3("w2v", [128, 1], F32)
    g1v = sb3("g1v", [128, 8], F32)
    pTf = [ps3("pTf%d" % i, [128, 512], F32) for i in range(3)]
    pLr = [ps3("pLr0", [E, 128], F32), ps3("pLr1", [E, 128], F32)]
    pG8 = ps3("pG8", [128, 16], F32)
    pTg = ps3("pTg", [E, 256], F32)

    P = Prog()
    for _eng in ENGINES:
        P.wait(_eng, sC1, ld_consts)
        P.wait(_eng, sC2, wl_consts)
    rZ = Rot(3)
    rSQ = Rot(3)
    rMU = Rot(3)
    rRS = Rot(3)
    rTF = Rot(3)
    rT4 = Rot(3)
    rPL = Rot(2)

    for tc in range(NCH):
        par = tc % 3
        z = zch[par]
        rZ.acquire(P, "sync", tc)
        dZC.load(P, "sync", tc, lambda Eo, o=z[:], s=d_xown[tc * 128:(tc + 1) * 128, :]:
                 Eo.dma_start(out=o, in_=s))
        dZC.wait(P, "vector", tc)
        orep = o_tm[:, tc * HS:(tc + 1) * HS].unsqueeze(1).broadcast_to((128, 16, HS))
        P.eop("vector", lambda Eo, z_=z[:], r=orep:
              Eo.tensor_add(z_.rearrange("p (r h) -> p r h", r=16), z_.rearrange("p (r h) -> p r h", r=16), r), sDV)
        rMU.acquire(P, "vector", tc)
        mu = muv[par]
        P.eop("vector", lambda Eo, z_=z[:], m=mu[:]: Eo.reduce_sum(m, z_, axis=AX.X), sDV)
        P.eop("vector", lambda Eo, m=mu[:]:
              Eo.tensor_scalar(m, m, c_inv1024, None, op0=ALU.mult), sDV)
        P.wait("scalar", sDV)
        rSQ.acquire(P, "scalar", tc)
        P.op("scalar", lambda Eo, z_=z[:], m=mu[:], q=sqb[par][:]:
             Eo.activation(q, z_, AF.Square, bias=m, scale=f32(-1.0)), inc=sAC)
        P.wait("vector", sAC)
        varv = varv_[par]
        lnvv = lnvv_[par]
        P.eop("vector", lambda Eo, vv=varv[:], q=sqb[par][:]: Eo.reduce_sum(vv, q, axis=AX.X), sDV)
        P.op("vector", lambda Eo, vv=varv[:]:
             Eo.tensor_scalar(vv, vv, c_inv1024, None, op0=ALU.mult), inc=sDV)
        rSQ.rel[par] = [(sDV, sDV.n)]
        P.wait("scalar", sDV)
        rRS.acquire(P, "scalar", tc)
        rstd = rstdv[par]
        P.eop("scalar", lambda Eo, lv=lnvv[:], vv=varv[:]: Eo.activation(lv, vv, AF.Ln, bias=f32(EPS)), sAC)
        P.op("scalar", lambda Eo, r=rstd[:], lv=lnvv[:]:
             Eo.activation(r, lv, AF.Exp, scale=f32(-0.5)), inc=sAC)
        P.wait("vector", sAC)
        P.op("vector", lambda Eo, z_=z[:], m=mu[:], r=rstd[:]:
             Eo.tensor_scalar(z_, z_, m, r, op0=ALU.subtract, op1=ALU.mult), inc=sDV)
        rMU.release(tc, sDV)
        rRS.release(tc, sDV)
        y1_done = sDV.n
        P.wait("sync", sDV, y1_done)
        dSPp.load(P, "sync", tc, lambda Eo, o=d_y1sp[tc * 128:(tc + 1) * 128, :], z_=z[:]:
                  Eo.dma_start(out=o, in_=z_))
        P.wait("tensor", sDV, y1_done)
        for g in range(2):
            tfp = pTf[(tc * 2 + g) % 3]
            rTF.acquire(P, "tensor", tc * 2 + g)
            for t in range(4):
                dc = g * 4 + t
                P.op("tensor", lambda Eo, o=tfp[:, t * 128:(t + 1) * 128],
                     a=z[:, dc * 128:(dc + 1) * 128]: Eo.transpose(o, a, idf[:]),
                     inc=(sPE if t == 3 else None))
            P.wait("vector", sPE)
            rT4.acquire(P, "vector", tc * 2 + g)
            tm4 = tmp4[(tc * 2 + g) % 3]
            P.op("vector", lambda Eo, o=tm4[:], a=tfp[:]: Eo.tensor_copy(o, a), inc=sDV)
            P.wait("scalar", sDV)
            y1T_view = y1T[:].rearrange("p (d t) -> p d t", d=8)[:, g * 4:(g + 1) * 4,
                                                               tc * 128:(tc + 1) * 128]
            P.op("scalar", lambda Eo, o=y1T_view, a=tfp[:].rearrange("p (d t) -> p d t", d=4):
                 Eo.activation(o, a, AF.Copy), inc=sAC)
            rTF.release(tc * 2 + g, sAC)
            rTF.rel[(tc * 2 + g) % 3].append((sDV, sDV.n))
            P.wait("tensor", sDV)
            if g == 0:
                rPL.acquire(P, "tensor", tc)
            for t in range(4):
                dc = g * 4 + t
                P.op("tensor", lambda Eo, p=pLr[tc % 2][:],
                     w=rw[:, dc * 8:(dc + 1) * 8],
                     x=tm4[:, t * 128:(t + 1) * 128], st=(dc == 0), sp=(dc == 7):
                     Eo.matmul(p, w, x, start=st, stop=sp),
                     inc=(sPE if t == 3 else None))
            rT4.release(tc * 2 + g, sPE)
        P.wait("scalar", sPE)
        P.op("scalar", lambda Eo, o=logits[:, tc * 128:(tc + 1) * 128],
             p=pLr[tc % 2][:]: Eo.activation(o, p, AF.Copy), inc=sAC)
        rPL.release(tc, sAC)
        rZ.release(tc, sPE)
        rZ.rel[par].append(dSPp.sem_count(tc))
    logits_done = sAC.n
    gate_dve_done = {}
    for tc in range(NCH):
        sl = tc % 2
        P.wait("tensor", sAC)
        if tc >= 2:
            P.wait("tensor", sDV, gate_dve_done[sl])
        P.op("tensor", lambda Eo, o=pG8[:, sl * 8:(sl + 1) * 8],
             a=logits[:, tc * 128:(tc + 1) * 128]:
             Eo.transpose(o, a, idf[0:8, 0:8]), inc=sPE)
        P.wait("vector", sPE)
        P.eop("vector", lambda Eo, p=pG8[:, sl * 8:(sl + 1) * 8]: Eo.tensor_copy(gtm[:], p), sDV)
        P.eop("vector", lambda Eo: Eo.reduce_max(m1v[:], gtm[:], axis=AX.X), sDV)
        P.eop("vector", lambda Eo: Eo.tensor_scalar(eq1[:], gtm[:], m1v[:, 0:1], None, op0=ALU.is_equal), sDV)
        P.eop("vector", lambda Eo: Eo.scalar_tensor_tensor(mskl[:], eq1[:], c_neg1e30, gtm[:],
                                                           op0=ALU.mult, op1=ALU.add), sDV)
        P.eop("vector", lambda Eo: Eo.reduce_max(m2v[:], mskl[:], axis=AX.X), sDV)
        P.eop("vector", lambda Eo: Eo.tensor_scalar(eq2[:], mskl[:], m2v[:, 0:1], None, op0=ALU.is_equal), sDV)
        P.op("vector", lambda Eo: Eo.tensor_tensor(edv2[:], m2v[:], m1v[:], op=ALU.subtract), inc=sDV)
        P.wait("scalar", sDV)
        P.op("scalar", lambda Eo: Eo.activation(edv[:], edv2[:], AF.Exp), inc=sAC)
        P.wait("vector", sAC)
        P.eop("vector", lambda Eo: Eo.tensor_scalar(s1v[:], edv[:], c_one, None, op0=ALU.add), sDV)
        P.eop("vector", lambda Eo: Eo.reciprocal(rv[:], s1v[:]), sDV)
        P.eop("vector", lambda Eo: Eo.tensor_tensor(w2v[:], edv[:], rv[:], op=ALU.mult), sDV)
        P.eop("vector", lambda Eo: Eo.tensor_scalar(g1v[:], eq1[:], rv[:, 0:1], None, op0=ALU.mult), sDV)
        P.op("vector", lambda Eo, o=gates_tm[:, tc * 8:(tc + 1) * 8]:
             Eo.scalar_tensor_tensor(o, eq2[:], w2v[:, 0:1], g1v[:], op0=ALU.mult, op1=ALU.add),
             inc=sDV)
        gate_dve_done[sl] = sDV.n
    P.wait("tensor", sDV)
    for tc in range(NCH):
        sl = tc % 2
        P.op("tensor", lambda Eo, o=pTg[:, sl * 128:(sl + 1) * 128],
             a=gates_tm[:, tc * 8:(tc + 1) * 8]:
             Eo.transpose(o, a, idf[:]), inc=sPE)
        P.wait("scalar", sPE)
        P.op("scalar", lambda Eo, o=gatesT[:, tc * 128:(tc + 1) * 128],
             p=pTg[:, sl * 128:(sl + 1) * 128]: Eo.activation(o, p, AF.Copy), inc=sAC)
        P.wait("tensor", sAC)

    if "logits" in taps:
        P.wait("sync", sAC, logits_done)
        P.op("sync", lambda Eo: Eo.dma_start(out=taps["logits"], in_=logits[:]),
             inc=sTAP, inc_n=16)
    if "gates" in taps:
        P.wait("sync", sDV)
        P.op("sync", lambda Eo: Eo.dma_start(out=taps["gates"], in_=gates_tm[:]),
             inc=sTAP, inc_n=16)
    if "y1T" in taps:
        P.wait("gpsimd", sAC)
        P.op("gpsimd", lambda Eo: Eo.dma_start(out=taps["y1T"], in_=y1T[:]),
             inc=sTAP, inc_n=16)
    if "gatesT" in taps:
        P.wait("gpsimd", sAC)
        P.op("gpsimd", lambda Eo: Eo.dma_start(out=taps["gatesT"], in_=gatesT[:]),
             inc=sTAP, inc_n=16)
    if "y1" in taps:
        for s_ in dSPp.sems:
            P.wait("sync", s_)
        P.op("sync", lambda Eo: Eo.dma_start(out=taps["y1"], in_=d_y1sp), inc=sTAP, inc_n=16)
    for s_ in dSPp.sems:
        P.wait("sync", s_)
    P.wait("sync", sTAP)
    barrier(P)
    with nc.Block("b3") as blk:
        P.run(blk)
    es3.close()
    esA.close()

    esO = ExitStack()
    O_acc = esO.enter_context(nc.sbuf_tensor("O_acc", [128, 8 * TOK], F32))
    es4 = ExitStack()
    sb4 = lambda name, shape, dt: es4.enter_context(nc.sbuf_tensor(name, shape, dt))
    ps4 = lambda name, shape, dt: es4.enter_context(nc.psum_tensor(name, shape, dt))
    H = sb4("H", [128, 32 * NT], BF16)
    w1b = [sb4("w1b0", [128, 8 * 256], BF16), sb4("w1b1", [128, 8 * 256], BF16)]
    w2b = [sb4("w2b%d" % i, [128, 16 * 128], BF16) for i in range(4)]
    gbb = [sb4("gbb0", [128, NT], BF16), sb4("gbb1", [128, NT], BF16)]
    pH = [ps4("pH0", [128, 1024], F32), ps4("pH1", [128, 1024], F32)]
    pO2 = [ps4("pO20", [128, 512], F32), ps4("pO21", [128, 512], F32)]
    pGB = ps4("pGB", [128, NT], F32)

    P = Prog()
    for _eng in ENGINES:
        P.wait(_eng, sC1, ld_consts)
        P.wait(_eng, sC2, wl_consts)
    rW1 = Rot(2)
    rW2 = Rot(4)
    rGB = Rot(2)
    rPH = Rot(2)
    rPO2 = Rot(2)
    rGBpsum = Rot(1)

    w1_view = d_w1
    w2_view = d_w2
    ei = 0
    for nt in range(2):
        for e in range(E):
            gpar = ei % 2
            rGBpsum.acquire(P, "tensor", 0)
            for s in range(2):
                P.op("tensor", lambda Eo, p=pGB[:, s * 512:(s + 1) * 512],
                     sel=selb[:, e * 128:(e + 1) * 128],
                     g=gatesT[:, nt * NT + s * 512:nt * NT + (s + 1) * 512]:
                     Eo.matmul(p, sel, g, start=True, stop=True),
                     inc=(sPE if s == 1 else None))
            P.wait("vector", sPE)
            rGB.acquire(P, "vector", ei)
            P.op("vector", lambda Eo, o=gbb[gpar][:]: Eo.tensor_copy(o, pGB[:]), inc=sDV)
            rGBpsum.rel[0] = [(sDV, sDV.n)]
            gb_ready = sDV.n
            def issue_w2(k, e=e, ei=ei):
                gk = ei * 16 + k
                dc, fh = k // 2, k % 2
                w2i = gk % 4
                rW2.acquire(P, "gpsimd", gk)
                srcw = w2_view[e, fh * 2048:(fh + 1) * 2048,
                               dc * 128:(dc + 1) * 128].rearrange(
                    "(fc p) d -> p fc d", p=128)
                dW2.load(P, "gpsimd", gk,
                         lambda Eo, o=w2b[w2i][:].rearrange("p (fc d) -> p fc d", fc=16),
                         s=srcw: Eo.dma_start(out=o, in_=s))

            gate_done = {}
            for fp in range(16):
                wpar = (ei * 16 + fp) % 2
                rW1.acquire(P, "gpsimd", ei * 16 + fp)
                src = w1_view[e, :, fp * 256:(fp + 1) * 256].rearrange(
                    "(kc p) f -> p kc f", p=128)
                dW1.load(P, "gpsimd", ei * 16 + fp,
                         lambda Eo, o=w1b[wpar][:].rearrange("p (kc f) -> p kc f", kc=8),
                         s=src: Eo.dma_start(out=o, in_=s))
                dW1.wait(P, "tensor", ei * 16 + fp)
                for fc in range(2):
                    fcg = fp * 2 + fc
                    hpar = fcg % 2
                    rPH.acquire(P, "tensor", fcg)
                    for nb in range(2):
                        for kc in range(8):
                            P.op("tensor", lambda Eo,
                                 p=pH[hpar][:, nb * 512:(nb + 1) * 512],
                                 w=w1b[wpar][:, kc * 256 + fc * 128:kc * 256 + (fc + 1) * 128],
                                 x=y1T[:, kc * TOK + nt * NT + nb * 512:
                                       kc * TOK + nt * NT + (nb + 1) * 512],
                                 st=(kc == 0), sp=(kc == 7):
                                 Eo.matmul(p, w, x, start=st, stop=sp),
                                 inc=(sPE if (nb == 1 and kc == 7) else None))
                    P.wait("scalar", sPE)
                    hsl = H[:, fcg * NT:(fcg + 1) * NT]
                    P.op("scalar", lambda Eo, o=hsl, p=pH[hpar][:],
                         b=b1t[:, e * 32 + fcg:e * 32 + fcg + 1]:
                         Eo.activation(o, p, AF.Gelu_apprx_tanh, bias=b, scale=f32(1.0)),
                         inc=sAC)
                    rPH.release(fcg, sAC)
                    P.wait("vector", sAC)
                    P.wait("vector", sDV, gb_ready)
                    P.op("vector", lambda Eo, h=hsl, g=gbb[gpar][:]:
                         Eo.tensor_tensor(h, h, g, op=ALU.mult), inc=sDV)
                    gate_done[fcg] = sDV.n
                rW1.release(ei * 16 + fp, sPE)
                if fp >= 12:
                    issue_w2(fp - 12)
            h_gated = sDV.n
            rGB.release(ei, sDV)
            for dc in range(8):
                if dc == 1:
                    P.wait("tensor", sDV, h_gated)
                for fh in range(2):
                    k = dc * 2 + fh
                    w2i = (ei * 16 + k) % 4
                    dW2.wait(P, "tensor", ei * 16 + k)
                    for nb in range(2):
                        opar = nb
                        if fh == 0:
                            rPO2.acquire(P, "tensor", ei * 16 + dc * 2 + nb)
                        for fcL in range(16):
                            fcg = fh * 16 + fcL
                            if e == 0 and fh == 0 and fcL == 0:
                                P.op("tensor", lambda Eo, p=pO2[opar][:],
                                     b=b2b[:, dc * 128:(dc + 1) * 128],
                                     g=gatesT[:, nt * NT + nb * 512:nt * NT + (nb + 1) * 512]:
                                     Eo.matmul(p, b, g, start=True, stop=False))
                            if dc == 0:
                                P.wait("tensor", sDV, gate_done[fcg])
                            P.op("tensor", lambda Eo, p=pO2[opar][:],
                                 w=w2b[w2i][:, fcL * 128:(fcL + 1) * 128],
                                 x=H[:, fcg * NT + nb * 512:fcg * NT + (nb + 1) * 512],
                                 st=(fh == 0 and fcL == 0 and e != 0),
                                 sp=(fh == 1 and fcL == 15):
                                 Eo.matmul(p, w, x, start=st, stop=sp),
                                 inc=(sPE if fcL == 15 else None))
                        if fh == 1:
                            P.wait("vector", sPE)
                            osl = O_acc[:, dc * TOK + nt * NT + nb * 512:
                                        dc * TOK + nt * NT + (nb + 1) * 512]
                            if e == 0:
                                P.op("vector", lambda Eo, o=osl, p=pO2[opar][:]:
                                     Eo.tensor_copy(o, p), inc=sDV)
                            else:
                                P.op("vector", lambda Eo, o=osl, p=pO2[opar][:]:
                                     Eo.tensor_add(o, o, p), inc=sDV)
                            rPO2.release(ei * 16 + dc * 2 + nb, sDV)
                    rW2.release(ei * 16 + k, sPE)
                    if k + 4 < 16:
                        issue_w2(k + 4)
            ei += 1

    if "O_acc" in taps:
        P.wait("sync", sDV)
        P.op("sync", lambda Eo: Eo.dma_start(out=taps["O_acc"], in_=O_acc[:]),
             inc=sTAP, inc_n=16)
    P.wait("sync", sTAP)
    barrier(P)
    with nc.Block("b4") as blk:
        P.run(blk)
    es4.close()

    es5 = ExitStack()
    sb5 = lambda name, shape, dt: es5.enter_context(nc.sbuf_tensor(name, shape, dt))
    ps5 = lambda name, shape, dt: es5.enter_context(nc.psum_tensor(name, shape, dt))
    y1ch = [sb5("y1ch%d" % i, [128, C], F32) for i in range(3)]
    z2ch = [sb5("z2ch%d" % i, [128, C], F32) for i in range(3)]
    sq2 = [sb5("sq2%d" % i, [128, C], F32) for i in range(3)]
    mu2 = [sb5("mu2%d" % i, [128, 1], F32) for i in range(3)]
    var2_ = [sb5("var2%d" % i, [128, 1], F32) for i in range(3)]
    lnv2_ = [sb5("lnv2%d" % i, [128, 1], F32) for i in range(3)]
    rstd2 = [sb5("rstd2%d" % i, [128, 1], F32) for i in range(3)]
    pTf2 = [ps5("pTf2%d" % i, [128, 512], F32) for i in range(3)]

    P = Prog()
    for _eng in ENGINES:
        P.wait(_eng, sC1, ld_consts)
        P.wait(_eng, sC2, wl_consts)
    rY1 = Rot(3)
    rZ2 = Rot(3)
    rSQ2 = Rot(3)
    rMU2 = Rot(3)
    rRS2 = Rot(3)
    rTF2 = Rot(3)
    for s_ in dSPp.sems:
        P.wait("sync", s_)
    for tc in range(NCH):
        par = tc % 3
        rY1.acquire(P, "sync", tc)
        dYC.load(P, "sync", tc, lambda Eo, o=y1ch[par][:], s=d_y1sp[tc * 128:(tc + 1) * 128, :]:
                 Eo.dma_start(out=o, in_=s))
        for g in range(2):
            tfp = pTf2[(tc * 2 + g) % 3]
            rTF2.acquire(P, "tensor", tc * 2 + g)
            for t in range(4):
                dc = g * 4 + t
                P.op("tensor", lambda Eo, o=tfp[:, t * 128:(t + 1) * 128],
                     a=O_acc[:, dc * TOK + tc * 128:dc * TOK + (tc + 1) * 128]:
                     Eo.transpose(o, a, idf[:]), inc=(sPE if t == 3 else None))
            P.wait("vector", sPE)
            dYC.wait(P, "vector", tc)
            rZ2.acquire(P, "vector", tc) if g == 0 else None
            P.op("vector", lambda Eo, o=z2ch[par][:, g * 512:(g + 1) * 512],
                 a=tfp[:], b=y1ch[par][:, g * 512:(g + 1) * 512]:
                 Eo.tensor_add(o, a, b), inc=sDV)
            rTF2.release(tc * 2 + g, sDV)
        rY1.release(tc, sDV)
        z2 = z2ch[par]
        rMU2.acquire(P, "vector", tc)
        mu = mu2[par]
        P.wait("vector", sDV)
        P.eop("vector", lambda Eo, m=mu[:], z_=z2[:]: Eo.reduce_sum(m, z_, axis=AX.X), sDV)
        P.eop("vector", lambda Eo, m=mu[:]:
              Eo.tensor_scalar(m, m, c_inv1024, None, op0=ALU.mult), sDV)
        P.wait("scalar", sDV)
        rSQ2.acquire(P, "scalar", tc)
        P.op("scalar", lambda Eo, q=sq2[par][:], z_=z2[:], m=mu[:]:
             Eo.activation(q, z_, AF.Square, bias=m, scale=f32(-1.0)), inc=sAC)
        P.wait("vector", sAC)
        var2 = var2_[par]
        lnv2 = lnv2_[par]
        P.eop("vector", lambda Eo, vv=var2[:], q=sq2[par][:]: Eo.reduce_sum(vv, q, axis=AX.X), sDV)
        P.op("vector", lambda Eo, vv=var2[:]:
             Eo.tensor_scalar(vv, vv, c_inv1024, None, op0=ALU.mult), inc=sDV)
        rSQ2.rel[par] = [(sDV, sDV.n)]
        P.wait("scalar", sDV)
        rRS2.acquire(P, "scalar", tc)
        P.eop("scalar", lambda Eo, lv=lnv2[:], vv=var2[:]: Eo.activation(lv, vv, AF.Ln, bias=f32(EPS)), sAC)
        P.op("scalar", lambda Eo, r=rstd2[par][:], lv=lnv2[:]:
             Eo.activation(r, lv, AF.Exp, scale=f32(-0.5)), inc=sAC)
        P.wait("vector", sAC)
        P.op("vector", lambda Eo, z_=z2[:], m=mu[:], r=rstd2[par][:]:
             Eo.tensor_scalar(z_, z_, m, r, op0=ALU.subtract, op1=ALU.mult), inc=sDV)
        rMU2.release(tc, sDV)
        rRS2.release(tc, sDV)
        P.wait("sync", sDV)
        dSTp.load(P, "sync", tc, lambda Eo, o=d_y2[tc * 128:(tc + 1) * 128, :], z_=z2[:]:
                  Eo.dma_start(out=o, in_=z_))
        rZ2.rel[par] = [dSTp.sem_count(tc)]
    for s_ in dSTp.sems:
        P.wait("sync", s_)
    P.wait("sync", sTAP)
    with nc.Block("b5") as blk:
        P.run(blk)
    es5.close()
    esO.close()
    esP.close()
    esS.close()
    return nc


_CACHE = {}


def kernel(L0_attn_x, L0_attn_Wq, L0_attn_bq, L0_attn_Wk, L0_attn_bk,
           L0_attn_Wv, L0_attn_bv, L0_ffn_router_w, L0_ffn_w1, L0_ffn_b1,
           L0_ffn_w2, L0_ffn_b2):
    x = np.ascontiguousarray(np.asarray(L0_attn_x, dtype=np.float32))
    Wq = np.asarray(L0_attn_Wq, np.float32)
    Wk = np.asarray(L0_attn_Wk, np.float32)
    Wv = np.asarray(L0_attn_Wv, np.float32)
    bq = np.asarray(L0_attn_bq, np.float32)
    bk = np.asarray(L0_attn_bk, np.float32)
    bv = np.asarray(L0_attn_bv, np.float32)
    rw_in = np.asarray(L0_ffn_router_w, np.float32)
    w1 = np.ascontiguousarray(np.asarray(L0_ffn_w1, np.float32))
    b1 = np.asarray(L0_ffn_b1, np.float32)
    w2 = np.ascontiguousarray(np.asarray(L0_ffn_w2, np.float32))
    b2 = np.asarray(L0_ffn_b2, np.float32)

    wqkvT = np.concatenate([Wq.T, Wk.T, Wv.T], axis=1)
    wqkvT = np.ascontiguousarray(wqkvT.reshape(8, 128, 192).transpose(1, 0, 2))
    qkb = np.zeros((128, 2), np.float32)
    qkb[0:64, 0] = bq
    qkb[0:64, 1] = bk
    bvrow = bv.reshape(1, HS)
    b1t = np.ascontiguousarray(
        b1.reshape(E, 32, 128).transpose(2, 0, 1))
    rw_pre = np.ascontiguousarray(rw_in.reshape(8, 128, E).transpose(1, 0, 2))
    identb = np.eye(128, dtype=bf16)
    identf = np.eye(128, dtype=np.float32)
    qpiota = (np.arange(128, dtype=np.float32)[:, None]
              - np.arange(512, dtype=np.float32)[None, :])
    onesb = np.ones((1, 128), dtype=bf16)
    consts = np.zeros((128, 8), np.float32)
    consts[:, C_INV1024] = 1.0 / 1024.0
    consts[:, C_NEG132] = -1.0 / 32.0
    consts[:, C_ONE] = 1.0
    consts[:, C_NEG1E30] = -1e30
    selb = np.zeros((E, E * 128), dtype=bf16)
    for e in range(E):
        selb[e, e * 128:(e + 1) * 128] = 1

    nc = build_nc()
    in_maps = []
    for c in range(NCORES):
        b = c // 4
        ph = c % 4
        chunks = [4 * j + ph for j in range(NCH)]
        x_b = x[b]
        x_own = np.ascontiguousarray(
            x_b.reshape(64, 128, C)[chunks].reshape(TOK, C))
        qoffb = np.full((128, 1), 128.0 * ph, np.float32)
        in_maps.append(dict(
            x_own=x_own, x_batch=x_b, wqkvT=wqkvT, qkb=qkb, bvrow=bvrow,
            b1t=b1t, b2w=b2, rw=rw_pre, w1=w1, w2=w2, identb=identb,
            identf=identf, qpiota=qpiota, onesb=onesb, qoffb=qoffb,
            consts=consts, selb=selb))
    res = run_bass_kernel_spmd(nc, in_maps, core_ids=list(range(NCORES)))
    out = np.zeros((B, T, C), np.float32)
    for c in range(NCORES):
        b = c // 4
        ph = c % 4
        y2 = np.asarray(res.results[c]["y2"]).reshape(NCH, 128, C)
        for j in range(NCH):
            out[b, (4 * j + ph) * 128:(4 * j + ph + 1) * 128, :] = y2[j]
    return out
